# version 10
# baseline (speedup 1.0000x reference)
from collections import defaultdict
from contextlib import ExitStack

import numpy as np
import ml_dtypes

import concourse.bass as bass
import concourse.mybir as mybir
from concourse.bass_utils import run_bass_kernel_spmd

F32 = mybir.dt.float32
BF16 = mybir.dt.bfloat16
I32 = mybir.dt.int32
AF = mybir.ActivationFunctionType
ALU = mybir.AluOpType
AX = mybir.AxisListType

MAGIC = 12582912.0
N = 4096
D = 1024
NH = 16
HD = 64
F0 = 3072
DK = 128
F1 = 4096
INV127 = 1.0 / 127.0
INV127SQ = (1.0 / 127.0) * (1.0 / 127.0)
ISQ_DK = 1.0 / float(np.sqrt(np.float64(DK)))

WS = 510
NWIN = 9

ENGS = ("sync", "tensor", "vector", "scalar", "gpsimd")


class Prog:

    def __init__(self, nc):
        self.nc = nc
        self.items = []
        self.cnt = defaultdict(int)
        self.h = {}
        names = ["d", "g", "t", "v", "s", "p"]
        for ab in "abcdefg":
            names += [f"d{ab}0", f"d{ab}1"]
        for nm in names:
            self.h[nm] = nc.alloc_semaphore(name=f"sem_{nm}")
        self.waited = {e_: defaultdict(int) for e_ in ENGS}
        self._pending = None
        self.eps_ln = None

    def after(self, *sems):
        return [(nm, self.cnt[nm]) for nm in sems]

    def mark(self, sem):
        return (sem, self.cnt[sem])

    OWN = {"sync": None, "tensor": "t", "vector": "v", "scalar": "s",
           "gpsimd": "p"}
    DROP_SELF = False

    def emit(self, eng, fn, waits=(), inc=None, k=1, ks=False):
        if inc is not None:
            self.cnt[inc] += k
        ws = [w for w in waits if w is not None]
        if self.DROP_SELF and not ks:
            own = self.OWN[eng]
            ws = [w for w in ws if w[0] != own]
        self.items.append((eng, fn, ws, inc, k))
        return (inc, self.cnt[inc]) if inc is not None else None

    def flush(self, barrier_after=True):
        nc = self.nc
        items = self.items
        pend = self._pending or {}

        def runner(eng_name):
            def run(eng):
                extra = pend.get(eng_name)
                used_extra = [False]
                for it_eng, fn, waits, inc, k in items:
                    if it_eng != eng_name:
                        continue
                    ws = list(waits)
                    if extra is not None and not used_extra[0]:
                        ws = extra + ws
                        used_extra[0] = True
                    for nm, v in ws:
                        if v > 0 and self.waited[eng_name][nm] < v:
                            eng.wait_ge(self.h[nm], v)
                            self.waited[eng_name][nm] = v
                    bi = fn(eng)
                    if inc is not None:
                        assert bi is not None
                        bi.then_inc(self.h[inc], k)
            return run

        with nc.Block() as block:
            block.sync(runner("sync"))
            block.tensor(runner("tensor"))
            block.vector(runner("vector"))
            block.scalar(runner("scalar"))
            block.gpsimd(runner("gpsimd"))
        self.items = []
        if barrier_after:
            snap = [(nm, self.cnt[nm]) for nm in self.h if self.cnt[nm] > 0]
            self._pending = {e_: list(snap) for e_ in ENGS}
        else:
            self._pending = None


def round_emit(p, out_ap, in_ap, tt, tr, gatef, scale_ap=None, pre=(),
               eng="vector"):
    sem = {"vector": "v", "gpsimd": "p"}[eng]
    if scale_ap is not None:
        p.emit(eng, lambda e, o=tt, i=in_ap, s=scale_ap:
               e.tensor_scalar(o, i, s, 0.5, ALU.mult, ALU.subtract),
               waits=list(pre), inc=sem)
    else:
        p.emit(eng, lambda e, o=tt, i=in_ap:
               e.tensor_single_scalar(o, i, 0.5, ALU.subtract),
               waits=list(pre), inc=sem)
    p.emit(eng, lambda e, o=gatef, i=tt:
           e.tensor_scalar(o, i, MAGIC, MAGIC, ALU.add, ALU.subtract),
           waits=p.after(sem), inc=sem)
    p.emit(eng, lambda e, o=tr, i=tt, gg=gatef:
           e.tensor_tensor(o, i, gg, ALU.subtract),
           waits=p.after(sem), inc=sem)
    if eng == "vector":
        p.emit(eng, lambda e, o=out_ap, i=tr, gg=gatef:
               e.scalar_tensor_tensor(o, i, 0.0, gg, ALU.is_ge, ALU.add),
               waits=p.after(sem), inc=sem)
    else:
        p.emit(eng, lambda e, i=tr:
               e.tensor_single_scalar(i, i, 0.0, ALU.is_ge),
               waits=p.after(sem), inc=sem)
        p.emit(eng, lambda e, o=out_ap, i=tr, gg=gatef:
               e.tensor_tensor(o, i, gg, ALU.add),
               waits=p.after(sem), inc=sem)
    return p.mark(sem)


def round_wide(p, out_ap, y_ap, u_t, d_t, gatef, K=2, pre=(), eng="vector",
               use_s=False):
    sem = {"vector": "v", "gpsimd": "p"}[eng]
    if use_s is True:
        p.emit("scalar", lambda e, o=d_t, i=y_ap:
               e.activation(o, i, AF.Identity, bias=p.Mt[0:y_ap.shape[0], :]),
               waits=list(pre) + p.after("s"), inc="s", ks=True)
        p.emit("scalar", lambda e, o=gatef, i=d_t:
               e.activation(o, i, AF.Identity, bias=p.negMt[0:y_ap.shape[0], :]),
               waits=p.after("s"), inc="s", ks=True)
        m_g = p.mark("s")
        p.emit(eng, lambda e, o=d_t, i=y_ap, gg=gatef:
               e.tensor_tensor(o, i, gg, ALU.subtract),
               waits=[m_g] + p.after(sem), inc=sem)
    else:
        p.emit(eng, lambda e, o=gatef, i=y_ap:
               e.tensor_scalar(o, i, MAGIC, MAGIC, ALU.add, ALU.subtract),
               waits=list(pre), inc=sem)
        p.emit(eng, lambda e, o=d_t, i=y_ap, gg=gatef:
               e.tensor_tensor(o, i, gg, ALU.subtract),
               waits=p.after(sem), inc=sem)
    if eng == "vector":
        p.emit(eng, lambda e, o=u_t, i=d_t, gg=gatef:
               e.scalar_tensor_tensor(o, i, 0.5, gg, ALU.is_ge, ALU.add),
               waits=p.after(sem), inc=sem)
        for j in range(2, K + 1):
            p.emit(eng, lambda e, o=u_t, i=d_t, th=j - 0.5:
                   e.scalar_tensor_tensor(o, i, th, o, ALU.is_ge, ALU.add),
                   waits=p.after(sem), inc=sem)
        p.emit(eng, lambda e, o=u_t, i=d_t:
               e.scalar_tensor_tensor(o, i, -0.5, o, ALU.is_lt, ALU.subtract),
               waits=p.after(sem), inc=sem)
        for j in range(2, K + 1):
            p.emit(eng, lambda e, o=u_t, i=d_t, th=-(j - 0.5):
                   e.scalar_tensor_tensor(o, i, th, o, ALU.is_lt, ALU.add),
                   waits=p.after(sem), inc=sem)
        if use_s:
            p.emit("scalar", lambda e, o=out_ap, i=u_t:
                   e.activation(o, i, AF.Identity, scale=-1.0),
                   waits=p.after(sem) + p.after("s"), inc="s", ks=True)
            return p.mark("s")
        p.emit(eng, lambda e, o=out_ap, i=u_t:
               e.tensor_single_scalar(o, i, -1.0, ALU.mult),
               waits=p.after(sem), inc=sem)
    else:
        p.emit(eng, lambda e, o=u_t, gg=gatef: e.tensor_copy(o, gg),
               waits=p.after(sem), inc=sem)
        for j in range(1, K + 1):
            p.emit(eng, lambda e, o=y_ap, i=d_t, th=j - 0.5:
                   e.tensor_single_scalar(o, i, th, ALU.is_ge),
                   waits=p.after(sem), inc=sem)
            p.emit(eng, lambda e, o=u_t, c=y_ap:
                   e.tensor_tensor(o, o, c, ALU.add),
                   waits=p.after(sem), inc=sem)
        for j in range(1, K + 1):
            p.emit(eng, lambda e, o=y_ap, i=d_t, th=-(j - 0.5):
                   e.tensor_single_scalar(o, i, th, ALU.is_lt),
                   waits=p.after(sem), inc=sem)
            p.emit(eng, lambda e, o=u_t, c=y_ap:
                   e.tensor_tensor(o, o, c, ALU.subtract),
                   waits=p.after(sem), inc=sem)
        p.emit(eng, lambda e, o=out_ap, i=u_t: e.tensor_copy(o, i),
               waits=p.after(sem), inc=sem)
    return p.mark(sem)


def ln_stats(p, x_ap, bn12, mv, rstd, xc_ap, rows=128, pre=(), xc_pre=()):
    p.emit("vector", lambda e, o=bn12[0:rows, 0:6], i=x_ap[:, 0:512]:
           e.bn_stats(o, i), waits=list(pre), inc="v")
    p.emit("vector", lambda e, o=bn12[0:rows, 6:12], i=x_ap[:, 512:1024]:
           e.bn_stats(o, i), waits=list(pre), inc="v")
    p.emit("vector", lambda e, o=mv[0:rows, :], i=bn12[0:rows, :]:
           e.bn_aggr(o, i), waits=p.after("v"), inc="v", ks=True)
    m_aggr = p.mark("v")
    p.emit("vector", lambda e, o=xc_ap, i=x_ap, m=mv[0:rows, 0:1]:
           e.tensor_single_scalar(o, i, m, ALU.subtract),
           waits=[m_aggr] + list(xc_pre), inc="v", ks=True)
    m_xc = p.mark("v")
    p.emit("scalar", lambda e, o=rstd[0:rows, :], i=mv[0:rows, 1:2]:
           e.activation(o, i, AF.Ln, bias=p.eps_ln[0:rows, :]),
           waits=[m_aggr], inc="s")
    p.emit("scalar", lambda e, o=rstd[0:rows, :], i=rstd[0:rows, :]:
           e.activation(o, i, AF.Exp, scale=-0.5),
           waits=p.after("s"), inc="s", ks=True)
    return p.mark("s"), m_xc


def build_program(debug_outputs=(), phases=7):
    nc = bass.Bass("TRN2", target_bir_lowering=False)
    p = Prog(nc)

    x0_d = nc.dram_tensor("x", [N, D], F32, kind="ExternalInput").ap()
    wqT_d = nc.dram_tensor("wqT", [D, D], F32, kind="ExternalInput").ap()
    wkT_d = nc.dram_tensor("wkT", [D, D], F32, kind="ExternalInput").ap()
    wvT_d = nc.dram_tensor("wvT", [D, D], F32, kind="ExternalInput").ap()
    wpT_d = nc.dram_tensor("wpT", [D, D], F32, kind="ExternalInput").ap()
    bproj_d = nc.dram_tensor("bproj", [1, D], F32, kind="ExternalInput").ap()
    wgT_d = nc.dram_tensor("wgT", [D, F0], I32, kind="ExternalInput").ap()
    wuT_d = nc.dram_tensor("wuT", [D, F0], I32, kind="ExternalInput").ap()
    wdT_d = nc.dram_tensor("wdT", [F0, D], I32, kind="ExternalInput").ap()
    wQT_d = nc.dram_tensor("wQT", [D, DK], F32, kind="ExternalInput").ap()
    wKT_d = nc.dram_tensor("wKT", [D, DK], F32, kind="ExternalInput").ap()
    gam_d = nc.dram_tensor("gam", [1, D], F32, kind="ExternalInput").ap()
    bet_d = nc.dram_tensor("bet", [1, D], F32, kind="ExternalInput").ap()
    w1T_d = nc.dram_tensor("w1T", [D, F1], F32, kind="ExternalInput").ap()
    cw_d = nc.dram_tensor("cw", [128, 96], F32, kind="ExternalInput").ap()
    cb_d = nc.dram_tensor("cb", [128, 32], F32, kind="ExternalInput").ap()
    w2T_d = nc.dram_tensor("w2T", [F1, D], F32, kind="ExternalInput").ap()
    out_d = nc.dram_tensor("out", [N, D], F32, kind="ExternalOutput").ap()

    def scratch(name, shape, dtype):
        kind = "ExternalOutput" if name in debug_outputs else "Internal"
        return nc.dram_tensor(name, shape, dtype, kind=kind).ap()

    xn0_d = scratch("xn0", [N, D], BF16)
    g0_d = scratch("g0", [N, D], F32)
    x1_d = scratch("x1", [N, D], F32)
    w1q_d = scratch("w1q", [8, 128, F1], BF16)
    w2q_d = scratch("w2q", [32, 128, D], BF16)
    xq2_d = scratch("xq2", [NWIN, 128, 32 * 512], BF16)
    kvdbg_d = (nc.dram_tensor("kvdbg", [64, NH * 65], BF16,
                              kind="ExternalOutput").ap()
               if "kvdbg" in debug_outputs else None)

    ctx = ExitStack()
    en = ctx.enter_context

    ident = en(nc.sbuf_tensor("ident", [128, 128], BF16))
    identf = en(nc.sbuf_tensor("identf", [128, 128], F32))
    ones1f = en(nc.sbuf_tensor("ones1f", [1, 128], F32))
    ones1b = en(nc.sbuf_tensor("ones1b", [1, 128], BF16))
    onesc = en(nc.sbuf_tensor("onesc", [128, 1], F32))
    eps_ln = en(nc.sbuf_tensor("eps_ln", [128, 1], F32))
    p.eps_ln = eps_ln
    m_bc = en(nc.sbuf_tensor("m_bc", [128, 4], F32))
    mval = en(nc.sbuf_tensor("mval", [1, 8], F32))
    kv_sb = en(nc.sbuf_tensor("kv_sb", [64, NH * 65], BF16))
    Msb = en(nc.sbuf_tensor("Msb", [128, D], BF16))
    Q1_sb = en(nc.sbuf_tensor("Q1_sb", [128, N], BF16))
    recs2_sb = en(nc.sbuf_tensor("recs2_sb", [128, NWIN * 4], F32))
    bn12 = en(nc.sbuf_tensor("bn12", [128, 12], F32))
    mv_t = en(nc.sbuf_tensor("mv_t", [128, 2], F32))
    rstd_t = en(nc.sbuf_tensor("rstd_t", [128, 1], F32))
    gam_bc = en(nc.sbuf_tensor("gam_bc", [128, D], F32))
    bet_bc = en(nc.sbuf_tensor("bet_bc", [128, D], F32))
    Mt = en(nc.sbuf_tensor("Mt", [128, 1], F32))
    negMt = en(nc.sbuf_tensor("negMt", [128, 1], F32))
    p.Mt = Mt
    p.negMt = negMt

    marks = {}

    with ExitStack() as px:
        pe = px.enter_context
        iota_i = pe(nc.sbuf_tensor("iota_i", [128, 128], I32))
        grow = pe(nc.sbuf_tensor("grow", [1, D], F32))
        brow = pe(nc.sbuf_tensor("brow", [1, D], F32))
        mtile = [pe(nc.sbuf_tensor(f"mtile{i}", [128, 2048], F32)) for i in range(2)]
        mred = pe(nc.sbuf_tensor("mred", [128, 2], F32))
        macc = pe(nc.sbuf_tensor("macc", [128, 2], F32))
        ps_bc = pe(nc.psum_tensor("ps_bc", [128, 512], F32))
        ps_m = pe(nc.psum_tensor("ps_m", [1, 1], F32))
        ps_mb = pe(nc.psum_tensor("ps_mb", [128, 1], F32))

        p.emit("gpsimd", lambda e: e.iota(iota_i[:, :], [[1, 128]],
                                          channel_multiplier=-1), inc="p")
        m_iota = p.mark("p")
        p.emit("vector", lambda e: e.tensor_single_scalar(
            ident[:], iota_i[:], 0, ALU.is_equal), waits=[m_iota], inc="v")
        p.emit("vector", lambda e: e.tensor_single_scalar(
            identf[:], iota_i[:], 0, ALU.is_equal), waits=[m_iota], inc="v")
        p.emit("vector", lambda e: e.memset(ones1f[:], 1.0), inc="v")
        p.emit("vector", lambda e: e.memset(ones1b[:], 1.0), inc="v")
        p.emit("vector", lambda e: e.memset(onesc[:], 1.0), inc="v")
        p.emit("vector", lambda e: e.memset(eps_ln[:], 1e-5), inc="v")
        p.emit("vector", lambda e: e.memset(Mt[:], MAGIC), inc="v")
        p.emit("vector", lambda e: e.memset(negMt[:], -MAGIC), inc="v")
        p.emit("vector", lambda e: e.memset(macc[:], 0.0), inc="v")
        m_consts = p.mark("v")

        for wi, (w_d, n_k, n_n, fw, fdim) in enumerate(
                ((w1T_d, 8, 2, 2048, F1), (w2T_d, 32, 1, 1024, F1))):
            for c in range(n_k):
                for nn2 in range(n_n):
                    i2 = c * n_n + nn2
                    buf = mtile[i2 % 2]
                    mk_free = marks.get(("mt", i2 % 2))
                    mk_ld = p.emit(
                        "sync", lambda e, b=buf, w=w_d, cc=c, n3=nn2, f2=fw:
                        e.dma_start(b[:, 0:f2], w[128 * cc:128 * (cc + 1),
                                                  f2 * n3:f2 * (n3 + 1)]),
                        waits=[mk_free] if mk_free else [],
                        inc=f"da{i2 % 2}", k=16)
                    p.emit("vector",
                           lambda e, b=buf, wi2=wi, f2=fw:
                           e.reduce_sum(mred[:, wi2:wi2 + 1], b[:, 0:f2], AX.X,
                                        apply_absolute_value=True),
                           waits=[mk_ld] + p.after("v"), inc="v", ks=True)
                    marks[("mt", i2 % 2)] = p.mark("v")
                    p.emit("vector",
                           lambda e, wi2=wi:
                           e.tensor_tensor(macc[:, wi2:wi2 + 1],
                                           macc[:, wi2:wi2 + 1],
                                           mred[:, wi2:wi2 + 1], ALU.add),
                           waits=p.after("v"), inc="v", ks=True)
            m_acc = p.mark("v")
            p.emit("tensor", lambda e, wi2=wi:
                   e.matmul(ps_m[:, :], macc[:, wi2:wi2 + 1], onesc[:, :],
                            start=True, stop=True),
                   waits=[m_acc, m_consts], inc="t")
            p.emit("vector", lambda e, wi2=wi, sc=1.0 / (fdim * D):
                   e.tensor_single_scalar(mval[0:1, 4 * wi2:4 * wi2 + 1],
                                          ps_m[:, :], sc, ALU.mult),
                   waits=p.after("t"), inc="v")
            p.emit("vector", lambda e, wi2=wi:
                   e.tensor_single_scalar(mval[0:1, 4 * wi2:4 * wi2 + 1],
                                          mval[0:1, 4 * wi2:4 * wi2 + 1],
                                          1e-5, ALU.max),
                   waits=p.after("v"), inc="v", ks=True)
            p.emit("vector", lambda e, wi2=wi:
                   e.reciprocal(mval[0:1, 4 * wi2 + 1:4 * wi2 + 2],
                                mval[0:1, 4 * wi2:4 * wi2 + 1]),
                   waits=p.after("v"), inc="v", ks=True)
            m_mv = p.mark("v")
            for part in range(2):
                p.emit("tensor", lambda e, wi2=wi, pp=part:
                       e.matmul(ps_mb[:, :], ones1f[:, :],
                                mval[0:1, 4 * wi2 + pp:4 * wi2 + pp + 1],
                                start=True, stop=True),
                       waits=[m_mv] + p.after("v"), inc="t")
                p.emit("vector", lambda e, wi2=wi, pp=part:
                       e.tensor_copy(m_bc[:, 2 * wi2 + pp:2 * wi2 + pp + 1],
                                     ps_mb[:, :]),
                       waits=p.after("t"), inc="v")
        marks["m_bc"] = p.mark("v")

        p.emit("sync", lambda e: e.dma_start(grow[:], gam_d), inc="d", k=16)
        p.emit("sync", lambda e: e.dma_start(brow[:], bet_d), inc="d", k=16)
        m_rows = p.mark("d")
        for half in range(2):
            for (row, dst) in ((grow, gam_bc), (brow, bet_bc)):
                p.emit("tensor", lambda e, r=row, h2=half:
                       e.matmul(ps_bc[:, :], ones1f[:, :],
                                r[0:1, 512 * h2:512 * (h2 + 1)],
                                start=True, stop=True),
                       waits=[m_rows, m_consts] + p.after("v"), inc="t")
                p.emit("vector", lambda e, d2=dst, h2=half:
                       e.tensor_copy(d2[:, 512 * h2:512 * (h2 + 1)],
                                     ps_bc[:, :]),
                       waits=p.after("t"), inc="v")
        marks["gb_bc"] = p.mark("v")
        p.flush()

    if phases >= 1:
        with ExitStack() as px:
            pe = px.enter_context
            x0t = [pe(nc.sbuf_tensor(f"x0t{i}", [128, 4 * D], F32)) for i in range(2)]
            xnt = [pe(nc.sbuf_tensor(f"xnt{i}", [128, 4 * D], BF16)) for i in range(2)]
            xc1 = pe(nc.sbuf_tensor("xc1p1", [128, D], F32))
            marks["xn0"] = [None] * 8
            last_xn = {}
            for j in range(8):
                par = j % 2
                mk_ld = p.emit(
                    "sync", lambda e, b=x0t[par], jj=j:
                    e.dma_start(b[:].rearrange("p (a d) -> p a d", a=4),
                                x0_d[512 * jj:512 * (jj + 1), :]
                                .rearrange("(a p) d -> p a d", p=128)),
                    waits=[last_xn.get(par)], inc=f"da{par}", k=16)
                for tc in range(4):
                    xap = x0t[par][:, D * tc:D * (tc + 1)]
                    m_r, m_xc = ln_stats(p, xap, bn12, mv_t, rstd_t, xc1[:, :],
                                         pre=[mk_ld] + p.after("s"),
                                         xc_pre=p.after("s"))
                    p.emit("scalar",
                           lambda e, o=xnt[par][:, D * tc:D * (tc + 1)],
                           i=xc1[:, :], r=rstd_t[:, 0:1]:
                           e.activation(o, i, AF.Identity, scale=r),
                           waits=[m_r, m_xc,
                                  marks.get(("xnst", par))], inc="s")
                last_xn[par] = p.mark("s")
                mk_st = p.emit(
                    "sync", lambda e, b=xnt[par], jj=j:
                    e.dma_start(xn0_d[512 * jj:512 * (jj + 1), :]
                                .rearrange("(a p) d -> p a d", p=128),
                                b[:].rearrange("p (a d) -> p a d", a=4)),
                    waits=[last_xn[par]], inc=f"db{par}", k=16)
                marks[("xnst", par)] = mk_st
                marks["xn0"][j] = mk_st
            p.flush()

    if phases >= 2:
        _p2(nc, p, en, marks, locals())
    if phases >= 3:
        _p3(nc, p, en, marks, locals())
    if phases >= 4:
        _p4(nc, p, en, marks, locals())
    if phases >= 5:
        _p5a(nc, p, en, marks, locals())
    if phases >= 6:
        _p5b(nc, p, en, marks, locals())

    p.emit("sync", lambda e: e.nop(), waits=p.after(*list(p.h.keys())))
    p.flush()
    ctx.close()
    return nc


def _p2(nc, p, en, marks, L):
    xn0_d, wkT_d, wvT_d, kv_sb, ident = (
        L["xn0_d"], L["wkT_d"], L["wvT_d"], L["kv_sb"], L["ident"])
    with ExitStack() as px:
        pe = px.enter_context
        wkv = pe(nc.sbuf_tensor("wkv", [128, 8 * 2048], BF16))
        xnf = [pe(nc.sbuf_tensor(f"xnf{i}", [128, 8 * 512], BF16)) for i in range(2)]
        ktmp = pe(nc.sbuf_tensor("ktmp", [128, 512], F32))
        ksb = pe(nc.sbuf_tensor("ksb", [128, 1024], BF16))
        v_aug = pe(nc.sbuf_tensor("v_aug", [128, NH * 65], BF16))
        kvw = [pe(nc.psum_tensor(f"kvw{i}", [128, 512], F32)) for i in range(4)]
        kvacc = [pe(nc.psum_tensor(f"kvacc{i}", [64, 260], F32)) for i in range(4)]

        for c in range(8):
            p.emit("gpsimd", lambda e, cc=c:
                   e.dma_start(wkv[:, 2048 * cc:2048 * cc + 1024],
                               wkT_d[128 * cc:128 * (cc + 1), :]), inc="g", k=16)
            p.emit("gpsimd", lambda e, cc=c:
                   e.dma_start(wkv[:, 2048 * cc + 1024:2048 * (cc + 1)],
                               wvT_d[128 * cc:128 * (cc + 1), :]), inc="g", k=16)
        m_wkv = p.mark("g")
        for h in range(NH):
            p.emit("vector", lambda e, hh=h:
                   e.memset(v_aug[:, 65 * hh + 64:65 * hh + 65], 1.0), inc="v")
        m_vones = p.mark("v")

        m_evac_s = None
        m_evac_v = None
        for j in range(8):
            par = j % 2
            mk_tp = None
            for c in range(8):
                mk_tp = p.emit(
                    "sync", lambda e, cc=c, jj=j, b=xnf[par]:
                    e.dma_start_transpose(
                        b[:, 512 * cc:512 * (cc + 1)],
                        xn0_d[512 * jj:512 * (jj + 1), 128 * cc:128 * (cc + 1)]),
                    waits=[marks.get(("xnf_used", par))],
                    inc=f"da{par}", k=16)
            for tc in range(4):
                first = True
                mm = None
                for c in range(8):
                    for nch in range(4):
                        w0 = []
                        if first:
                            w0 = [mk_tp, m_wkv, m_evac_s, m_evac_v]
                            first = False
                        mm = p.emit(
                            "tensor", lambda e, cc=c, nn=nch, t2=tc, b=xnf[par]:
                            e.matmul(kvw[nn][:, :],
                                     b[:, 512 * cc + 128 * t2:512 * cc + 128 * (t2 + 1)],
                                     wkv[:, 2048 * cc + 512 * nn:2048 * cc + 512 * (nn + 1)],
                                     start=(cc == 0), stop=(cc == 7)),
                            waits=w0, inc="t")
                m_qkv = p.mark("t")
                if tc == 3:
                    marks[("xnf_used", par)] = m_qkv
                for half in range(2):
                    p.emit("scalar", lambda e, h2=half:
                           e.activation(ktmp[:, :], kvw[h2][:, :], AF.Square),
                           waits=[m_qkv] + p.after("s"), inc="s")
                    p.emit("scalar", lambda e, h2=half:
                           e.activation(ksb[:, 512 * h2:512 * (h2 + 1)],
                                        ktmp[:, :], AF.Exp, scale=-0.5),
                           waits=p.after("s"), inc="s")
                m_evac_s = p.mark("s")
                for h in range(NH):
                    src = kvw[2 + h // 8]
                    p.emit("vector", lambda e, hh=h, sp=src:
                           e.tensor_copy(v_aug[:, 65 * hh:65 * hh + 64],
                                         sp[:, 64 * (hh % 8):64 * (hh % 8) + 64]),
                           waits=[m_qkv, m_vones], inc="v")
                m_evac_v = p.mark("v")
                for h in range(NH):
                    g4, slot = h // 4, h % 4
                    stf = (j == 0 and tc == 0 and slot == 0)
                    spf = (j == 7 and tc == 3 and slot == 3)
                    p.emit("tensor", lambda e, hh=h, g2=g4, s2=slot, st=stf, sp=spf:
                           e.matmul(kvacc[g2][0:64, 65 * s2:65 * (s2 + 1)],
                                    ksb[:, 64 * hh:64 * (hh + 1)],
                                    v_aug[:, 65 * hh:65 * (hh + 1)],
                                    start=st, stop=sp),
                           waits=[m_evac_s, m_evac_v], inc="t")
        m_kvmm = p.mark("t")
        for g4 in range(4):
            p.emit("vector", lambda e, g2=g4:
                   e.tensor_copy(kv_sb[0:64, 260 * g2:260 * (g2 + 1)],
                                 kvacc[g2][0:64, :]),
                   waits=[m_kvmm], inc="v")
        marks["kv_sb"] = p.mark("v")
        if "kvdbg" in L["debug_outputs"]:
            kvdbg_d = L["kvdbg_d"]
            p.emit("sync", lambda e: e.dma_start(kvdbg_d, kv_sb[0:64, :]),
                   waits=[marks["kv_sb"]], inc="d", k=16)
        p.flush()


def _p3(nc, p, en, marks, L):
    xn0_d, g0_d = L["xn0_d"], L["g0_d"]
    wgT_d, wuT_d, wdT_d = L["wgT_d"], L["wuT_d"], L["wdT_d"]
    w1T_d, w2T_d, w1q_d, w2q_d = L["w1T_d"], L["w2T_d"], L["w1q_d"], L["w2q_d"]
    m_bc = L["m_bc"]
    with ExitStack() as px:
        pe = px.enter_context
        wg = pe(nc.sbuf_tensor("wg", [128, 8 * F0], BF16))
        wu = pe(nc.sbuf_tensor("wu", [128, 8 * F0], BF16))
        wd = pe(nc.sbuf_tensor("wd", [128, 24 * 1024], BF16))
        xnf3 = [pe(nc.sbuf_tensor(f"xnf3{i}", [128, 8 * 256], BF16)) for i in range(2)]
        hsb = pe(nc.sbuf_tensor("hsb", [128, 24 * 256], BF16))
        sgt = [pe(nc.sbuf_tensor(f"sgt{i}", [128, 256], F32)) for i in range(2)]
        g0t = [pe(nc.sbuf_tensor(f"g0t{i}", [128, D], F32)) for i in range(2)]
        qtile = [pe(nc.sbuf_tensor("qtile0", [128, 512], F32))] * 2
        qb = pe(nc.sbuf_tensor("qb", [128, 512], F32))
        qout = [pe(nc.sbuf_tensor(f"qout{i}", [128, 512], BF16)) for i in range(2)]
        qtr = pe(nc.sbuf_tensor("qtr", [128, 512], F32))
        qgate = pe(nc.sbuf_tensor("qgate", [128, 512], I32))
        pg = [pe(nc.psum_tensor(f"pg{i}", [128, 256], F32)) for i in range(2)]
        pu = [pe(nc.psum_tensor(f"pu{i}", [128, 256], F32)) for i in range(2)]
        pd = [pe(nc.psum_tensor(f"pd{i}", [128, 512], F32)) for i in range(4)]

        for c in range(8):
            p.emit("gpsimd", lambda e, cc=c:
                   e.dma_start(wg[:, F0 * cc:F0 * (cc + 1)],
                               wgT_d[128 * cc:128 * (cc + 1), :]), inc="g", k=16)
            p.emit("gpsimd", lambda e, cc=c:
                   e.dma_start(wu[:, F0 * cc:F0 * (cc + 1)],
                               wuT_d[128 * cc:128 * (cc + 1), :]), inc="g", k=16)
        m_wgu = p.mark("g")
        for m in range(24):
            p.emit("gpsimd", lambda e, mm=m:
                   e.dma_start(wd[:, 1024 * mm:1024 * (mm + 1)],
                               wdT_d[128 * mm:128 * (mm + 1), :]), inc="g", k=16)
        m_w = p.mark("g")

        qjobs = []
        for c in range(8):
            for nn2 in range(8):
                qjobs.append((0, w1T_d, w1q_d, c, nn2))
        for c in range(32):
            for nn2 in range(2):
                qjobs.append((1, w2T_d, w2q_d, c, nn2))
        qidx = [0]

        def quant_step():
            if qidx[0] >= len(qjobs):
                return
            wi, src_d, dst_d, c, nn2 = qjobs[qidx[0]]
            par = qidx[0] % 2
            qidx[0] += 1
            mk_ld = p.emit("gpsimd", lambda e, b=qtile[par], w=src_d, cc=c, n3=nn2:
                           e.dma_start(b[:], w[128 * cc:128 * (cc + 1),
                                              512 * n3:512 * (n3 + 1)]),
                           waits=[marks.get(("qt_used", par))],
                           inc=f"df{par}", k=16)
            p.emit("vector", lambda e, b=qtile[par], sc=m_bc[:, 2 * wi + 1:2 * wi + 2]:
                   e.tensor_single_scalar(qb[:], b[:], sc, ALU.mult),
                   waits=[mk_ld, marks["m_bc"]] + p.after("v"), inc="v")
            marks[("qt_used", par)] = p.mark("v")
            p.emit("vector", lambda e:
                   e.tensor_scalar(qb[:], qb[:], 1.0, -1.0, ALU.min, ALU.max),
                   waits=p.after("v"), inc="v")
            mk_r = round_emit(p, qout[par][:, :], qb[:], qb[:], qtr[:],
                              qgate[:].bitcast(F32),
                              pre=p.after("v") + [marks.get(("qo_used", par))])
            marks[("qo_used", par)] = p.emit(
                "gpsimd", lambda e, b=qout[par], w=dst_d, cc=c, n3=nn2:
                e.dma_start(w[cc, :, 512 * n3:512 * (n3 + 1)], b[:]),
                waits=[mk_r], inc=f"dg{par}", k=16)

        m_sg = {}
        m_h = {}
        m_g0s = {}
        marks["g0"] = [None] * 32
        for j in range(16):
            par = j % 2
            mk_tp = None
            for c in range(8):
                mk_tp = p.emit(
                    "sync", lambda e, cc=c, jj=j, b=xnf3[par]:
                    e.dma_start_transpose(
                        b[:, 256 * cc:256 * (cc + 1)],
                        xn0_d[256 * jj:256 * (jj + 1), 128 * cc:128 * (cc + 1)]),
                    waits=[marks.get(("xnf3_used", par))],
                    inc=f"da{par}", k=16)
            for _ in range(8):
                quant_step()
            m_dn_prev = marks.get(("pd_used",))
            for m in range(24):
                mp = m % 2
                first_w = [mk_tp, m_wgu, m_sg.get(mp), m_h.get(mp)]
                mm = None
                for c in range(8):
                    mm = p.emit("tensor", lambda e, cc=c, mm2=m, b=xnf3[par]:
                                e.matmul(pg[mm2 % 2][:, :],
                                         wg[:, F0 * cc + 128 * mm2:F0 * cc + 128 * (mm2 + 1)],
                                         b[:, 256 * cc:256 * (cc + 1)],
                                         start=(cc == 0), stop=(cc == 7)),
                                waits=first_w if c == 0 else [], inc="t")
                m_pg = p.mark("t")
                for c in range(8):
                    mm = p.emit("tensor", lambda e, cc=c, mm2=m, b=xnf3[par]:
                                e.matmul(pu[mm2 % 2][:, :],
                                         wu[:, F0 * cc + 128 * mm2:F0 * cc + 128 * (mm2 + 1)],
                                         b[:, 256 * cc:256 * (cc + 1)],
                                         start=(cc == 0), stop=(cc == 7)),
                                waits=[], inc="t")
                m_pu = p.mark("t")
                p.emit("scalar", lambda e, mp2=mp:
                       e.activation(sgt[mp2][:], pg[mp2][:, :], AF.Silu,
                                    scale=INV127),
                       waits=[m_pg, m_h.get(mp)] + p.after("s"), inc="s")
                m_sg[mp] = p.mark("s")
                w_h = [m_sg[mp], m_pu]
                if m == 0:
                    w_h.append(marks.get(("hsb_used",)))
                p.emit("vector", lambda e, mp2=mp, mm2=m:
                       e.tensor_tensor(hsb[:, 256 * mm2:256 * (mm2 + 1)],
                                       sgt[mp2][:], pu[mp2][:, :], ALU.mult),
                       waits=w_h, inc="v")
                m_h[mp] = p.mark("v")
            marks[("xnf3_used", par)] = p.mark("t")
            m_hall = p.mark("v")
            for tc2 in range(2):
                for half in range(2):
                    bank = 2 * tc2 + half
                    for m in range(24):
                        w0 = []
                        if m == 0:
                            w0 = [m_hall, m_dn_prev, m_w]
                        p.emit("tensor",
                               lambda e, mm2=m, t3=tc2, h3=half, b2=bank:
                               e.matmul(pd[b2][:, :],
                                        hsb[:, 256 * mm2 + 128 * t3:256 * mm2 + 128 * (t3 + 1)],
                                        wd[:, 1024 * mm2 + 512 * h3:1024 * mm2 + 512 * (h3 + 1)],
                                        start=(mm2 == 0), stop=(mm2 == 23)),
                               waits=w0, inc="t")
                    m_pd = p.mark("t")
                    p.emit("scalar", lambda e, t3=tc2, h3=half, b2=bank:
                           e.activation(g0t[t3][:, 512 * h3:512 * (h3 + 1)],
                                        pd[b2][:, :], AF.Silu, scale=INV127SQ),
                           waits=[m_pd, m_g0s.get(tc2)] + p.after("s"), inc="s")
            marks[("pd_used",)] = p.mark("s")
            marks[("hsb_used",)] = p.mark("t")
            for tc2 in range(2):
                rc = 2 * j + tc2
                m_g0s[tc2] = p.emit(
                    "sync", lambda e, t3=tc2, r2=rc:
                    e.dma_start(g0_d[128 * r2:128 * (r2 + 1), :], g0t[t3][:]),
                    waits=[marks[("pd_used",)]], inc=f"de{tc2}", k=16)
                marks["g0"][rc] = m_g0s[tc2]
        while qidx[0] < len(qjobs):
            quant_step()
        marks["wq_done"] = None
        p.flush()


def _p4(nc, p, en, marks, L):
    xn0_d, x0_d, g0_d, x1_d = L["xn0_d"], L["x0_d"], L["g0_d"], L["x1_d"]
    wqT_d, wpT_d, wQT_d, wKT_d, bproj_d = (
        L["wqT_d"], L["wpT_d"], L["wQT_d"], L["wKT_d"], L["bproj_d"])
    kv_sb, Msb, Q1_sb, ident, ones1b = (
        L["kv_sb"], L["Msb"], L["Q1_sb"], L["ident"], L["ones1b"])
    bn12, mv_t, rstd_t = L["bn12"], L["mv_t"], L["rstd_t"]
    with ExitStack() as px:
        pe = px.enter_context
        wq2 = pe(nc.sbuf_tensor("wq2", [128, 8 * 1024], BF16))
        wp2 = pe(nc.sbuf_tensor("wp2", [128, 8 * 1024], BF16))
        wQs = pe(nc.sbuf_tensor("wQs", [128, 8 * 128], BF16))
        wKs = pe(nc.sbuf_tensor("wKs", [128, 8 * 128], BF16))
        bproj_b = pe(nc.sbuf_tensor("bproj_b", [1, D], BF16))
        xnf4 = [pe(nc.sbuf_tensor(f"xnf4{i}", [128, 8 * 512], BF16)) for i in range(2)]
        qtmp = pe(nc.sbuf_tensor("qtmp", [128, 512], F32))
        q_sb2 = pe(nc.sbuf_tensor("q_sb2", [64, 16 * 512], BF16))
        num_sb = pe(nc.sbuf_tensor("num_sb", [128, 16 * 64], BF16))
        den16 = pe(nc.sbuf_tensor("den16", [128, 16], F32))
        rec16 = pe(nc.sbuf_tensor("rec16", [128, 16], F32))
        numz_tm = pe(nc.sbuf_tensor("numz_tm", [128, 1024], BF16))
        numz_fm = pe(nc.sbuf_tensor("numz_fm", [128, 8 * 128], BF16))
        x0c = [pe(nc.sbuf_tensor(f"x0c{i}", [128, D], F32)) for i in range(2)]
        g0c = [pe(nc.sbuf_tensor(f"g0c{i}", [128, D], F32)) for i in range(2)]
        x1t = [pe(nc.sbuf_tensor(f"x1t{i}", [128, D], F32)) for i in range(2)]
        cmb = pe(nc.sbuf_tensor("cmb", [128, D], F32))
        xc4 = pe(nc.sbuf_tensor("xc4", [128, D], F32))
        xn1t = pe(nc.sbuf_tensor("xn1t", [128, D], BF16))
        xn1f = pe(nc.sbuf_tensor("xn1f", [128, 8 * 128], BF16))
        k1sb = pe(nc.sbuf_tensor("k1sb", [128, 128], BF16))
        M_acc = [pe(nc.psum_tensor(f"Macc{i}", [128, 512], F32)) for i in range(2)]
        pA = pe(nc.psum_tensor("pA", [128, 512], F32))
        pB = pe(nc.psum_tensor("pB", [128, 512], F32))
        ptp = pe(nc.psum_tensor("ptp", [128, 128], BF16))
        psA = pe(nc.psum_tensor("psA", [128, 512], F32))
        psB = pe(nc.psum_tensor("psB", [128, 512], F32))
        pk = pe(nc.psum_tensor("pk", [128, 128], F32))

        for c in range(8):
            p.emit("gpsimd", lambda e, cc=c:
                   e.dma_start(wq2[:, 1024 * cc:1024 * (cc + 1)],
                               wqT_d[128 * cc:128 * (cc + 1), :]), inc="g", k=16)
            p.emit("gpsimd", lambda e, cc=c:
                   e.dma_start(wp2[:, 1024 * cc:1024 * (cc + 1)],
                               wpT_d[128 * cc:128 * (cc + 1), :]), inc="g", k=16)
            p.emit("gpsimd", lambda e, cc=c:
                   e.dma_start(wQs[:, 128 * cc:128 * (cc + 1)],
                               wQT_d[128 * cc:128 * (cc + 1), :]), inc="g", k=16)
            p.emit("gpsimd", lambda e, cc=c:
                   e.dma_start(wKs[:, 128 * cc:128 * (cc + 1)],
                               wKT_d[128 * cc:128 * (cc + 1), :]), inc="g", k=16)
        p.emit("gpsimd", lambda e: e.dma_start(bproj_b[:], bproj_d), inc="g", k=16)
        m_w4 = p.mark("g")

        marks["x1"] = [None] * 32
        qb_banks = [pA, pB]
        m_qevac = {}
        m_nd_used = {}
        m_proj_used = {}
        m_cmb = {}
        ld_marks = {}

        def emit_xg_load(j, tc):
            rc = 4 * j + tc
            if rc >= 32 or rc in ld_marks:
                return
            parc = tc % 2
            mk1 = p.emit("sync", lambda e, b=x0c[parc], r2=rc:
                         e.dma_start(b[:], x0_d[128 * r2:128 * (r2 + 1), :]),
                         waits=[m_cmb.get(parc)], inc=f"dc{parc}", k=16)
            mk2 = p.emit("sync", lambda e, b=g0c[parc], r2=rc:
                         e.dma_start(b[:], g0_d[128 * r2:128 * (r2 + 1), :]),
                         waits=[m_cmb.get(parc)], inc=f"dd{parc}", k=16)
            ld_marks[rc] = (mk1, mk2)

        for j in range(8):
            par = j % 2
            mk_tp = None
            for c in range(8):
                mk_tp = p.emit(
                    "sync", lambda e, cc=c, jj=j, b=xnf4[par]:
                    e.dma_start_transpose(
                        b[:, 512 * cc:512 * (cc + 1)],
                        xn0_d[512 * jj:512 * (jj + 1), 128 * cc:128 * (cc + 1)]),
                    waits=[marks.get(("xnf4_used", par))],
                    inc=f"da{par}", k=16)
            emit_xg_load(j, 0)
            emit_xg_load(j, 1)
            for m in range(8):
                bank = qb_banks[m % 2]
                for c in range(8):
                    w0 = []
                    if c == 0:
                        w0 = [mk_tp, m_w4, m_qevac.get(m % 2),
                              marks.get(("q_used",))]
                    p.emit("tensor", lambda e, cc=c, mm2=m, b=xnf4[par], bk=bank:
                           e.matmul(bk[:, :],
                                    wq2[:, 1024 * cc + 128 * mm2:1024 * cc + 128 * (mm2 + 1)],
                                    b[:, 512 * cc:512 * (cc + 1)],
                                    start=(cc == 0), stop=(cc == 7)),
                           waits=w0, inc="t")
                m_q = p.mark("t")
                p.emit("scalar", lambda e, bk=bank:
                       e.activation(qtmp[:], bk[:, :], AF.Square),
                       waits=[m_q] + p.after("s"), inc="s")
                for hh2 in range(2):
                    p.emit("scalar", lambda e, mm2=m, h3=hh2:
                           e.activation(q_sb2[0:64, 512 * (2 * mm2 + h3):512 * (2 * mm2 + h3 + 1)],
                                        qtmp[64 * h3:64 * (h3 + 1), :],
                                        AF.Exp, scale=-0.5),
                           waits=p.after("s"), inc="s")
                m_qevac[m % 2] = p.mark("s")
            m_qstage = p.mark("s")
            for tc in range(4):
                rc = 4 * j + tc
                for h in range(NH):
                    bank = qb_banks[h % 2]
                    w0 = [m_nd_used.get((h % 2, "s")), m_nd_used.get((h % 2, "v"))]
                    if h == 0:
                        w0 += [m_qstage, marks["kv_sb"]]
                    p.emit("tensor", lambda e, hh=h, t2=tc, bk=bank:
                           e.matmul(bk[:, 0:65],
                                    q_sb2[0:64, 512 * hh + 128 * t2:512 * hh + 128 * (t2 + 1)],
                                    kv_sb[0:64, 65 * hh:65 * (hh + 1)],
                                    start=True, stop=True),
                           waits=w0, inc="t")
                    m_nd = p.mark("t")
                    p.emit("scalar", lambda e, hh=h, bk=bank:
                           e.copy(num_sb[:, 64 * hh:64 * (hh + 1)], bk[:, 0:64]),
                           waits=[m_nd] + p.after("s"), inc="s")
                    m_nd_used[(h % 2, "s")] = p.mark("s")
                    p.emit("vector", lambda e, hh=h, bk=bank:
                           e.tensor_copy(den16[:, hh:hh + 1], bk[:, 64:65]),
                           waits=[m_nd], inc="v")
                    m_nd_used[(h % 2, "v")] = p.mark("v")
                p.emit("vector", lambda e:
                       e.tensor_single_scalar(rec16[:], den16[:], 1e-6, ALU.add),
                       waits=p.after("v", "s"), inc="v", ks=True)
                p.emit("vector", lambda e: e.reciprocal(rec16[:], rec16[:]),
                       waits=p.after("v"), inc="v", ks=True)
                m_rec = p.mark("v")
                for h in range(NH):
                    p.emit("vector", lambda e, hh=h:
                           e.tensor_single_scalar(numz_tm[:, 64 * hh:64 * (hh + 1)],
                                                  num_sb[:, 64 * hh:64 * (hh + 1)],
                                                  rec16[:, hh:hh + 1], ALU.mult),
                           waits=[m_rec], inc="v", ks=True)
                m_numz = p.mark("v")
                m_tpc = marks.get(("ptp_free",))
                for c in range(8):
                    p.emit("tensor", lambda e, cc=c:
                           e.transpose(ptp[:, :], numz_tm[:, 128 * cc:128 * (cc + 1)],
                                       ident[:]),
                           waits=[m_numz, m_tpc] if c == 0 else [m_tpc], inc="t")
                    m_t1 = p.mark("t")
                    m_tpc = p.emit("vector", lambda e, cc=c:
                                   e.tensor_copy(numz_fm[:, 128 * cc:128 * (cc + 1)],
                                                 ptp[:, :]),
                                   waits=[m_t1], inc="v")
                marks[("ptp_free",)] = m_tpc
                for half in range(2):
                    bank = psA if half == 0 else psB
                    for c in range(8):
                        w0 = [m_tpc, m_cmb.get(("proj", half))] if c == 0 else []
                        p.emit("tensor", lambda e, cc=c, h3=half, bk=bank:
                               e.matmul(bk[:, :],
                                        numz_fm[:, 128 * cc:128 * (cc + 1)],
                                        wp2[:, 1024 * cc + 512 * h3:1024 * cc + 512 * (h3 + 1)],
                                        start=(cc == 0), stop=False),
                               waits=w0, inc="t")
                    p.emit("tensor", lambda e, h3=half, bk=bank:
                           e.matmul(bk[:, :], ones1b[:, :],
                                    bproj_b[0:1, 512 * h3:512 * (h3 + 1)],
                                    start=False, stop=True),
                           waits=[], inc="t")
                m_proj = p.mark("t")
                parc = tc % 2
                parx = tc % 2
                mk1, mk2 = ld_marks[rc]
                for half in range(2):
                    bank = psA if half == 0 else psB
                    p.emit("vector", lambda e, h3=half, bk=bank, gc=g0c[parc]:
                           e.tensor_tensor(cmb[:, 512 * h3:512 * (h3 + 1)],
                                           gc[:, 512 * h3:512 * (h3 + 1)],
                                           bk[:, :], ALU.mult),
                           waits=[m_proj, mk2] + p.after("v"), inc="v")
                    p.emit("vector", lambda e, h3=half, xc=x0c[parc], xt=x1t[parx]:
                           e.tensor_tensor(xt[:, 512 * h3:512 * (h3 + 1)],
                                           cmb[:, 512 * h3:512 * (h3 + 1)],
                                           xc[:, 512 * h3:512 * (h3 + 1)], ALU.add),
                           waits=[mk1, marks.get(("x1st", parx))] + p.after("v"),
                           inc="v")
                m_x1 = p.mark("v")
                m_cmb[parc] = m_x1
                m_cmb[("proj", 0)] = m_x1
                m_cmb[("proj", 1)] = m_x1
                emit_xg_load(j, tc + 2)
                marks["x1"][rc] = p.emit(
                    "sync", lambda e, xt=x1t[parx], r2=rc:
                    e.dma_start(x1_d[128 * r2:128 * (r2 + 1), :], xt[:]),
                    waits=[m_x1], inc=f"db{parx}", k=16)
                marks[("x1st", parx)] = marks["x1"][rc]
                m_r, m_xc = ln_stats(p, x1t[parx][:, :], bn12, mv_t, rstd_t,
                                     xc4[:, :], pre=[m_x1] + p.after("s"),
                                     xc_pre=p.after("s"))
                p.emit("scalar", lambda e:
                       e.activation(xn1t[:], xc4[:], AF.Identity,
                                    scale=rstd_t[:, 0:1]),
                       waits=[m_r, m_xc, marks.get(("xn1t_used",))], inc="s")
                m_xn1 = p.mark("s")
                m_tpc2 = marks.get(("ptp_free",))
                for c in range(8):
                    p.emit("tensor", lambda e, cc=c:
                           e.transpose(ptp[:, :], xn1t[:, 128 * cc:128 * (cc + 1)],
                                       ident[:]),
                           waits=[m_xn1, m_tpc2] if c == 0 else [m_tpc2], inc="t")
                    m_t1 = p.mark("t")
                    m_tpc2 = p.emit("vector", lambda e, cc=c:
                                    e.tensor_copy(xn1f[:, 128 * cc:128 * (cc + 1)],
                                                  ptp[:, :]),
                                    waits=[m_t1], inc="v")
                marks[("ptp_free",)] = m_tpc2
                for c in range(8):
                    w0 = [m_tpc2, marks.get(("pk_used",))] if c == 0 else []
                    p.emit("tensor", lambda e, cc=c:
                           e.matmul(pk[:, :], xn1f[:, 128 * cc:128 * (cc + 1)],
                                    wKs[:, 128 * cc:128 * (cc + 1)],
                                    start=(cc == 0), stop=(cc == 7)),
                           waits=w0, inc="t")
                m_k1 = p.mark("t")
                p.emit("vector", lambda e: e.tensor_copy(k1sb[:], pk[:, :]),
                       waits=[m_k1], inc="v")
                m_k1v = p.mark("v")
                for half in range(2):
                    stf = (j == 0 and tc == 0)
                    spf = (j == 7 and tc == 3)
                    p.emit("tensor", lambda e, h3=half, st=stf, sp=spf:
                           e.matmul(M_acc[h3][:, :], k1sb[:],
                                    xn1t[:, 512 * h3:512 * (h3 + 1)],
                                    start=st, stop=sp),
                           waits=[m_k1v] if half == 0 else [], inc="t")
                for c in range(8):
                    w0 = [m_k1v] if c == 0 else []
                    p.emit("tensor", lambda e, cc=c:
                           e.matmul(pk[:, :], wQs[:, 128 * cc:128 * (cc + 1)],
                                    xn1f[:, 128 * cc:128 * (cc + 1)],
                                    start=(cc == 0), stop=(cc == 7)),
                           waits=w0, inc="t")
                m_q1 = p.mark("t")
                marks[("xn1t_used",)] = m_q1
                mkq = p.emit("vector", lambda e, r2=rc:
                             e.tensor_copy(Q1_sb[:, 128 * r2:128 * (r2 + 1)],
                                           pk[:, :]),
                             waits=[m_q1], inc="v")
                marks[("pk_used",)] = mkq
                marks[("q_used",)] = m_q1 if tc == 3 else marks.get(("q_used",))
            marks[("xnf4_used",)] = p.mark("t")
            marks[("xnf4_used", par)] = marks[("xnf4_used",)]
        m_tfin = p.mark("t")
        for half in range(2):
            p.emit("vector", lambda e, h3=half:
                   e.tensor_copy(Msb[:, 512 * h3:512 * (h3 + 1)],
                                 M_acc[h3][:, :]),
                   waits=[m_tfin], inc="v")
        marks["Msb"] = p.mark("v")
        p.flush()


def _p5a(nc, p, en, marks, L):
    x1_d, w1q_d, xq2_d, cw_d, cb_d = (
        L["x1_d"], L["w1q_d"], L["xq2_d"], L["cw_d"], L["cb_d"])
    ident, identf, ones1f, m_bc, recs2_sb = (
        L["ident"], L["identf"], L["ones1f"], L["m_bc"], L["recs2_sb"])
    bn12, mv_t, rstd_t = L["bn12"], L["mv_t"], L["rstd_t"]
    with ExitStack() as px:
        pe = px.enter_context
        w1q = pe(nc.sbuf_tensor("w1q_sb", [128, 8 * F1], BF16))
        cwsb = pe(nc.sbuf_tensor("cwsb", [128, 96], F32))
        cbsb = pe(nc.sbuf_tensor("cbsb", [128, 32], F32))
        x1c5 = [pe(nc.sbuf_tensor(f"x1c5{i}", [128, D], F32)) for i in range(2)]
        xc5 = pe(nc.sbuf_tensor("xc5", [128, D], F32))
        xn5 = pe(nc.sbuf_tensor("xn5", [128, D], F32))
        xqt = pe(nc.sbuf_tensor("xqt", [128, D], BF16))
        rtt = pe(nc.sbuf_tensor("rtt", [128, 1024], F32))
        rtr = pe(nc.sbuf_tensor("rtr", [128, 512], F32))
        rgate = pe(nc.sbuf_tensor("rgate", [128, 1024], I32))
        xqf = pe(nc.sbuf_tensor("xqf", [128, 8 * 512], BF16))
        h5 = pe(nc.sbuf_tensor("h5", [128, 32 * 513], F32))
        hs5 = pe(nc.sbuf_tensor("hs5", [128, 32], F32))
        recs5 = pe(nc.sbuf_tensor("recs5", [128, 4], F32))
        amx_c = pe(nc.sbuf_tensor("amx_c", [128, 1], F32))
        s127 = pe(nc.sbuf_tensor("s127", [128, 1], F32))
        a2t = pe(nc.sbuf_tensor("a2t", [128, 1], F32))
        s2c = pe(nc.sbuf_tensor("s2c", [128, 1], F32))
        recs_row = pe(nc.sbuf_tensor("recs_row", [1, 512], F32))
        s2row = pe(nc.sbuf_tensor("s2row", [1, 512], F32))
        rbc = pe(nc.sbuf_tensor("rbc", [128, 512], F32))
        cv1 = pe(nc.sbuf_tensor("cv1", [128, 1024], F32))
        cv2 = pe(nc.sbuf_tensor("cv2", [128, 512], F32))
        mx5 = pe(nc.sbuf_tensor("mx5", [128, 512], F32))
        mn5 = pe(nc.sbuf_tensor("mn5", [128, 512], F32))
        xq2t = [pe(nc.sbuf_tensor(f"xq2t{i}", [128, 1024], BF16)) for i in range(2)]
        ph5 = [pe(nc.psum_tensor(f"ph5{i}", [128, 512], F32)) for i in range(2)]
        ptp5 = pe(nc.psum_tensor("ptp5", [128, 128], BF16))
        ptp5f = pe(nc.psum_tensor("ptp5f", [128, 128], F32))
        pbc5 = pe(nc.psum_tensor("pbc5", [128, 512], F32))

        for c in range(8):
            p.emit("sync", lambda e, cc=c:
                   e.dma_start(w1q[:, F1 * cc:F1 * (cc + 1)], w1q_d[cc]),
                   inc="dc0", k=16)
        m_w1q = p.mark("dc0")
        p.emit("sync", lambda e: e.dma_start(cwsb[:], cw_d), inc="d", k=16)
        p.emit("sync", lambda e: e.dma_start(cbsb[:], cb_d), inc="d", k=16)
        m_cw = p.mark("d")
        p.emit("vector", lambda e: e.memset(recs_row[:], 0.0), inc="v")
        p.emit("vector", lambda e: e.memset(s2row[:], 0.0), inc="v")
        p.emit("vector", lambda e: e.memset(xq2t[0][:], 0.0), inc="v")
        p.emit("vector", lambda e: e.memset(cv1[:], 0.0), inc="v")
        p.emit("vector", lambda e: e.memset(xq2t[1][:], 0.0), inc="v")
        marks["xq2st"] = [None] * NWIN

        for w in range(NWIN):
            o0 = WS * w
            ow = min(WS * (w + 1), N) - o0
            hn = min(512, N - o0)
            nck = (hn + 127) // 128
            p.emit("vector", lambda e: e.memset(mx5[:], -3.0e38),
                   waits=[marks.get(("mx_used",))], inc="v")
            p.emit("vector", lambda e: e.memset(mn5[:], 3.0e38),
                   waits=[marks.get(("mn_used",))], inc="v")
            m_mninit = p.mark("v")
            m_xqf = None
            for k2 in range(nck):
                rows = min(128, hn - 128 * k2)
                par = k2 % 2
                mk_ld = p.emit(
                    "sync", lambda e, b=x1c5[par], r0=o0 + 128 * k2, rr=rows:
                    e.dma_start(b[0:rr, :], x1_d[r0:r0 + rr, :]),
                    waits=[marks.get(("x1c5_used", par))], inc=f"da{par}", k=16)
                m_r, m_xc = ln_stats(p, x1c5[par][0:rows, :], bn12, mv_t,
                                     rstd_t, xc5[0:rows, :], rows=rows,
                                     pre=[mk_ld] + p.after("s", "v"),
                                     xc_pre=p.after("s"))
                p.emit("vector", lambda e, rr=rows:
                       e.reduce_max(amx_c[0:rr, :], xc5[0:rr, :], AX.X,
                                    apply_absolute_value=True),
                       waits=[m_xc] + p.after("v"), inc="v", ks=True)
                p.emit("vector", lambda e, rr=rows:
                       e.tensor_tensor(amx_c[0:rr, :], amx_c[0:rr, :],
                                       rstd_t[0:rr, :], ALU.mult),
                       waits=[m_r] + p.after("v"), inc="v", ks=True)
                p.emit("vector", lambda e, rr=rows:
                       e.tensor_single_scalar(amx_c[0:rr, :], amx_c[0:rr, :],
                                              1e-5, ALU.max),
                       waits=p.after("v"), inc="v", ks=True)
                p.emit("vector", lambda e, rr=rows:
                       e.reciprocal(s127[0:rr, :], amx_c[0:rr, :]),
                       waits=p.after("v"), inc="v", ks=True)
                p.emit("vector", lambda e, rr=rows:
                       e.tensor_single_scalar(s127[0:rr, :], s127[0:rr, :],
                                              127.0, ALU.mult),
                       waits=p.after("v"), inc="v", ks=True)
                p.emit("vector", lambda e, rr=rows, kk=k2:
                       e.tensor_single_scalar(recs5[0:rr, kk:kk + 1],
                                              amx_c[0:rr, :], INV127, ALU.mult),
                       waits=p.after("v"), inc="v", ks=True)
                m_recs = p.mark("v")
                marks[("x1c5_used", par)] = m_recs
                p.emit("scalar", lambda e, rr=rows:
                       e.activation(xn5[0:rr, :], xc5[0:rr, :], AF.Identity,
                                    scale=rstd_t[0:rr, 0:1]),
                       waits=[m_r, m_xc, marks.get(("xn5_used",))] + p.after("s"),
                       inc="s")
                m_xn5 = p.mark("s")
                p.emit("vector", lambda e, rr=rows:
                       e.tensor_scalar(cv1[0:rr, 0:D],
                                       xn5[0:rr, 0:D],
                                       s127[0:rr, 0:1], None, ALU.mult),
                       waits=[m_xn5, marks.get(("xqt_used",)),
                              marks.get(("cv1_used",))] + p.after("v"),
                       inc="v", ks=True)
                mk_rnd = round_wide(
                    p, xqt[0:rows, 0:D], cv1[0:rows, 0:D],
                    rtt[0:rows, 0:D], xn5[0:rows, 0:D],
                    rgate[0:rows, 0:D].bitcast(F32), K=3,
                    pre=p.after("v"), use_s="neg")
                marks[("cv1_used",)] = mk_rnd
                marks[("xn5_used",)] = mk_rnd
                for c in range(8):
                    p.emit("tensor", lambda e, cc=c, rr=rows:
                           e.transpose(ptp5[:, 0:rr],
                                       xqt[0:rr, 128 * cc:128 * (cc + 1)],
                                       ident[0:rr, 0:rr]),
                           waits=[mk_rnd, marks.get(("ptp5_free",)),
                                  marks.get(("xqf_used",))], inc="t")
                    m_t1 = p.mark("t")
                    mk_cp = p.emit(
                        "vector", lambda e, cc=c, rr=rows, kk=k2:
                        e.tensor_copy(xqf[:, 512 * cc + 128 * kk:512 * cc + 128 * kk + rr],
                                      ptp5[:, 0:rr]),
                        waits=[m_t1], inc="v")
                    marks[("ptp5_free",)] = mk_cp
                marks[("xqt_used",)] = p.mark("t")
                m_xqf = marks[("ptp5_free",)]
                p.emit("tensor", lambda e, rr=rows, kk=k2:
                       e.transpose(ptp5f[0:1, 0:rr], recs5[0:rr, kk:kk + 1],
                                   identf[0:rr, 0:rr]),
                       waits=[m_recs, marks.get(("ptp5f_free",))], inc="t")
                m_t2 = p.mark("t")
                marks[("ptp5f_free",)] = p.emit(
                    "vector", lambda e, rr=rows, kk=k2:
                    e.tensor_copy(recs_row[0:1, 128 * kk:128 * kk + rr],
                                  ptp5f[0:1, 0:rr]),
                    waits=[m_t2], inc="v")
            p.emit("tensor", lambda e:
                   e.matmul(pbc5[:, :], ones1f[:, :], recs_row[0:1, :],
                            start=True, stop=True),
                   waits=[marks[("ptp5f_free",)], marks.get(("pbc5_free",))] +
                         p.after("v"), inc="t")
            m_t3 = p.mark("t")
            m_rbc = p.emit("vector", lambda e: e.tensor_copy(rbc[:], pbc5[:, :]),
                           waits=[m_t3] + p.after("p"), inc="v")
            marks[("pbc5_free",)] = m_rbc
            mhv5 = {}
            mcv1_5 = {}
            msilu5 = {}

            def conv_A(m):
                mp = m % 2
                for c in range(8):
                    w0 = []
                    if c == 0:
                        w0 = [m_xqf, m_w1q, marks.get(("ph_used", mp))]
                    p.emit("tensor", lambda e, cc=c, mm2=m, hn2=hn, bk=ph5[mp]:
                           e.matmul(bk[:, 0:hn2],
                                    w1q[:, F1 * cc + 128 * mm2:F1 * cc + 128 * (mm2 + 1)],
                                    xqf[:, 512 * cc:512 * cc + hn2],
                                    start=(cc == 0), stop=(cc == 7)),
                           waits=w0, inc="t")
                m_mm = p.mark("t")
                p.emit("vector", lambda e, mm2=m, hn2=hn, bk=ph5[mp]:
                       e.scalar_tensor_tensor(
                           h5[:, 513 * mm2 + 1:513 * mm2 + 1 + hn2],
                           bk[:, 0:hn2], m_bc[:, 0:1], rbc[:, 0:hn2],
                           ALU.mult, ALU.mult),
                       waits=[m_mm, m_rbc, marks.get(("h5q", m))] + p.after("v"),
                       inc="v")
                marks[("ph_used", mp)] = p.mark("v")
                if w == 0:
                    p.emit("vector", lambda e, mm2=m:
                           e.memset(h5[:, 513 * mm2:513 * mm2 + 1], 0.0),
                           waits=p.after("v"), inc="v")
                else:
                    p.emit("vector", lambda e, mm2=m:
                           e.tensor_copy(h5[:, 513 * mm2:513 * mm2 + 1],
                                         hs5[:, mm2:mm2 + 1]),
                           waits=p.after("v"), inc="v", ks=True)
                if w == NWIN - 1:
                    p.emit("vector", lambda e, mm2=m:
                           e.memset(h5[:, 513 * mm2 + 17:513 * mm2 + 18], 0.0),
                           waits=p.after("v"), inc="v")
                else:
                    p.emit("vector", lambda e, mm2=m:
                           e.tensor_copy(hs5[:, mm2:mm2 + 1],
                                         h5[:, 513 * mm2 + 510:513 * mm2 + 511]),
                           waits=p.after("v"), inc="v")
                mhv5[m] = p.mark("v")
                p.emit("scalar", lambda e, mm2=m, ow2=ow:
                       e.activation(cv1[:, 0:ow2],
                                    h5[:, 513 * mm2 + 1:513 * mm2 + 1 + ow2],
                                    AF.Identity,
                                    scale=cwsb[:, 3 * mm2 + 1:3 * mm2 + 2],
                                    bias=cbsb[:, mm2:mm2 + 1]),
                       waits=[mhv5[m], m_cw, marks.get(("cv1_used",))] +
                             p.after("s"), inc="s")
                mcv1_5[m] = p.mark("s")

            def conv_B(m):
                p.emit("vector", lambda e, mm2=m, ow2=ow:
                       e.scalar_tensor_tensor(cv2[:, 0:ow2],
                                              h5[:, 513 * mm2:513 * mm2 + ow2],
                                              cwsb[:, 3 * mm2:3 * mm2 + 1],
                                              cv1[:, 0:ow2], ALU.mult, ALU.add),
                       waits=[mcv1_5[m], mhv5[m], marks.get(("cv2_used",))] +
                             p.after("v"), inc="v")
                m_cv2 = p.mark("v")
                marks[("cv1_used",)] = m_cv2
                p.emit("vector", lambda e, mm2=m, ow2=ow:
                       e.scalar_tensor_tensor(h5[:, 513 * mm2:513 * mm2 + ow2],
                                              h5[:, 513 * mm2 + 2:513 * mm2 + 2 + ow2],
                                              cwsb[:, 3 * mm2 + 2:3 * mm2 + 3],
                                              cv2[:, 0:ow2], ALU.mult, ALU.add),
                       waits=[m_cv2] + p.after("v"), inc="v")
                m_cnv = p.mark("v")
                marks[("cv2_used",)] = m_cnv
                p.emit("scalar", lambda e, mm2=m, ow2=ow:
                       e.activation(h5[:, 513 * mm2:513 * mm2 + ow2],
                                    h5[:, 513 * mm2:513 * mm2 + ow2], AF.Silu),
                       waits=[m_cnv] + p.after("s"), inc="s")
                msilu5[m] = p.mark("s")

            def conv_C(m):
                p.emit("vector", lambda e, mm2=m, ow2=ow:
                       e.tensor_tensor(mx5[:, 0:ow2], mx5[:, 0:ow2],
                                       h5[:, 513 * mm2:513 * mm2 + ow2],
                                       ALU.max),
                       waits=[msilu5[m]] + p.after("v"), inc="v")
                p.emit("vector", lambda e, mm2=m, ow2=ow:
                       e.tensor_tensor(mn5[:, 0:ow2], mn5[:, 0:ow2],
                                       h5[:, 513 * mm2:513 * mm2 + ow2],
                                       ALU.min),
                       waits=[msilu5[m]] + p.after("v"), inc="v")

            for m in range(34):
                if 1 <= m < 33:
                    conv_B(m - 1)
                if m < 32:
                    conv_A(m)
                if m >= 2:
                    conv_C(m - 2)
            marks[("xqf_used",)] = p.mark("t")
            m_mins = p.mark("p")
            p.emit("vector", lambda e:
                   e.tensor_single_scalar(mn5[:], mn5[:], -1.0, ALU.mult),
                   waits=[m_mins] + p.after("v"), inc="v")
            p.emit("vector", lambda e:
                   e.tensor_tensor(mx5[:], mx5[:], mn5[:], ALU.max),
                   waits=p.after("v"), inc="v")
            m_amax = p.mark("v")
            marks[("mn_used",)] = m_amax
            for k2 in range((ow + 127) // 128):
                koff = 128 * k2
                rk = min(128, ow - koff)
                p.emit("tensor", lambda e, ko=koff, rr=rk:
                       e.transpose(ptp5f[0:rr, :], mx5[:, ko:ko + rr], identf[:]),
                       waits=[m_amax, marks.get(("ptp5f_free",))], inc="t")
                m_t4 = p.mark("t")
                p.emit("vector", lambda e, rr=rk:
                       e.reduce_max(a2t[0:rr, :], ptp5f[0:rr, :], AX.X),
                       waits=[m_t4] + p.after("v"), inc="v", ks=True)
                marks[("ptp5f_free",)] = p.mark("v")
                p.emit("vector", lambda e, rr=rk:
                       e.tensor_single_scalar(a2t[0:rr, :], a2t[0:rr, :],
                                              1e-5, ALU.max),
                       waits=p.after("v"), inc="v", ks=True)
                p.emit("vector", lambda e, rr=rk:
                       e.reciprocal(s2c[0:rr, :], a2t[0:rr, :]),
                       waits=p.after("v"), inc="v", ks=True)
                p.emit("vector", lambda e, rr=rk:
                       e.tensor_single_scalar(s2c[0:rr, :], s2c[0:rr, :],
                                              127.0, ALU.mult),
                       waits=p.after("v"), inc="v", ks=True)
                p.emit("vector", lambda e, rr=rk, w2=w, kk=k2:
                       e.tensor_single_scalar(recs2_sb[0:rr, 4 * w2 + kk:4 * w2 + kk + 1],
                                              a2t[0:rr, :], INV127, ALU.mult),
                       waits=p.after("v"), inc="v", ks=True)
                m_s2 = p.mark("v")
                p.emit("tensor", lambda e, rr=rk:
                       e.transpose(ptp5f[0:1, 0:rr], s2c[0:rr, 0:1],
                                   identf[0:rr, 0:rr]),
                       waits=[m_s2, marks.get(("ptp5f_free",))], inc="t")
                m_t5 = p.mark("t")
                marks[("ptp5f_free",)] = p.emit(
                    "vector", lambda e, ko=koff, rr=rk:
                    e.tensor_copy(s2row[0:1, ko:ko + rr], ptp5f[0:1, 0:rr]),
                    waits=[m_t5], inc="v")
            p.emit("tensor", lambda e:
                   e.matmul(pbc5[:, :], ones1f[:, :], s2row[0:1, :],
                            start=True, stop=True),
                   waits=[marks[("ptp5f_free",)], marks.get(("pbc5_free",))] +
                         p.after("v"), inc="t")
            m_t6 = p.mark("t")
            m_s2bc = p.emit("vector", lambda e: e.tensor_copy(rbc[:], pbc5[:, :]),
                            waits=[m_t6] + p.after("p"), inc="v")
            marks[("pbc5_free",)] = m_s2bc
            marks[("mx_used",)] = m_t6
            for mp in range(16):
                par = mp % 2
                m0, m1 = 2 * mp, 2 * mp + 1
                for sub, mm in ((0, m0), (1, m1)):
                    p.emit("vector", lambda e, mm2=mm, ow2=ow, s4=sub:
                           e.tensor_tensor(cv1[:, 512 * s4:512 * s4 + ow2],
                                           h5[:, 513 * mm2:513 * mm2 + ow2],
                                           rbc[:, 0:ow2], ALU.mult),
                           waits=[m_s2bc, marks.get(("xq2st", par))] +
                                 p.after("v"), inc="v")
                    marks[("h5q", mm)] = p.mark("v")
                wid = 512 + ow
                mk_r4 = round_wide(p, xq2t[par][:, 0:wid], cv1[:, 0:wid],
                                   rtt[:, 0:wid], xn5[:, 0:wid],
                                   rgate[:, 0:wid].bitcast(F32), K=4,
                                   pre=p.after("v"), eng="vector", use_s=True)
                marks[("xq2st", par)] = p.emit(
                    "sync", lambda e, mm2=m0, w2=w, b=xq2t[par]:
                    e.dma_start(xq2_d[w2, :, 512 * mm2:512 * (mm2 + 2)], b[:]),
                    waits=[mk_r4], inc=f"db{par}", k=16)
                marks[("xn5_used",)] = mk_r4
            marks["xq2st"][w] = (marks[("xq2st", 0)], marks[("xq2st", 1)])
        p.flush()


def _p5b(nc, p, en, marks, L):
    xq2_d, w2q_d, x1_d, out_d = L["xq2_d"], L["w2q_d"], L["x1_d"], L["out_d"]
    Q1_sb, Msb, gam_bc, bet_bc, m_bc, recs2_sb = (
        L["Q1_sb"], L["Msb"], L["gam_bc"], L["bet_bc"], L["m_bc"],
        L["recs2_sb"])
    bn12, mv_t, rstd_t = L["bn12"], L["mv_t"], L["rstd_t"]
    with ExitStack() as px:
        pe = px.enter_context
        w2q = pe(nc.sbuf_tensor("w2q_sb", [128, 32 * 1024], BF16))
        xq2s = [pe(nc.sbuf_tensor(f"xq2s{i}", [128, 32 * 512], BF16)) for i in range(2)]
        x1c6 = [pe(nc.sbuf_tensor(f"x1c6{i}", [128, D], F32)) for i in range(2)]
        g1t = [pe(nc.sbuf_tensor(f"g1t{i}", [128, D], F32)) for i in range(2)]
        xc6 = pe(nc.sbuf_tensor("xc6", [128, D], F32))
        xn6 = pe(nc.sbuf_tensor("xn6", [128, D], F32))
        Jn6 = pe(nc.sbuf_tensor("Jn6", [128, D], F32))
        tn6 = pe(nc.sbuf_tensor("tn6", [128, D], F32))
        s1t = pe(nc.sbuf_tensor("s1t", [128, D], F32))
        ot6 = [pe(nc.sbuf_tensor(f"ot6{i}", [128, D], F32)) for i in range(2)]
        sc2 = [pe(nc.sbuf_tensor(f"sc2{i}", [128, 1], F32)) for i in range(2)]
        po6 = [pe(nc.psum_tensor(f"po6{i}", [128, 512], F32)) for i in range(2)]
        pj6 = [pe(nc.psum_tensor(f"pj6{i}", [128, 512], F32)) for i in range(2)]

        for c in range(32):
            p.emit("sync", lambda e, cc=c:
                   e.dma_start(w2q[:, 1024 * cc:1024 * (cc + 1)], w2q_d[cc]),
                   inc="dc0", k=16)
        m_w2q = p.mark("dc0")

        for w in range(NWIN):
            o0 = WS * w
            ow = min(WS * (w + 1), N) - o0
            wpar = w % 2
            mk_x = p.emit("sync", lambda e, w2=w, b=xq2s[wpar]:
                          e.dma_start(b[:], xq2_d[w2]),
                          waits=[marks.get(("xq2s_used", wpar))],
                          inc=f"da{wpar}", k=16)
            for k2 in range((ow + 127) // 128):
                t0 = o0 + 128 * k2
                rk = min(128, ow - 128 * k2)
                par = k2 % 2
                mk_l = p.emit("sync", lambda e, b=x1c6[par], t1=t0, rr=rk:
                              e.dma_start(b[0:rr, :], x1_d[t1:t1 + rr, :]),
                              waits=[marks.get(("x1c6_used", par))],
                              inc=f"dd{par}", k=16)
                p.emit("vector", lambda e, rr=rk, w2=w, kk=k2, s2=sc2[par]:
                       e.tensor_tensor(s2[0:rr, :],
                                       recs2_sb[0:rr, 4 * w2 + kk:4 * w2 + kk + 1],
                                       m_bc[0:rr, 2:3], ALU.mult),
                       waits=[marks.get(("sc2_used", par))] + p.after("v"),
                       inc="v", ks=True)
                m_sc2 = p.mark("v")
                for half in range(2):
                    for c in range(32):
                        w0 = []
                        if c == 0:
                            w0 = [mk_x, m_w2q, marks.get(("po6_used", half))]
                        p.emit("tensor",
                               lambda e, cc=c, kk=k2, h3=half, rr=rk,
                               bk=po6[half], b=xq2s[wpar]:
                               e.matmul(bk[0:rr, :],
                                        b[:, 512 * cc + 128 * kk:512 * cc + 128 * kk + rr],
                                        w2q[:, 1024 * cc + 512 * h3:1024 * cc + 512 * (h3 + 1)],
                                        start=(cc == 0), stop=(cc == 31)),
                               waits=w0, inc="t")
                    m_o = p.mark("t")
                    p.emit("scalar", lambda e, h3=half, rr=rk, bk=po6[half],
                           g1=g1t[par], s2=sc2[par]:
                           e.activation(g1[0:rr, 512 * h3:512 * (h3 + 1)],
                                        bk[0:rr, :], AF.Silu,
                                        scale=s2[0:rr, 0:1]),
                           waits=[m_o, m_sc2, marks.get(("g1_used", par))] +
                                 p.after("s"), inc="s")
                    marks[("po6_used", half)] = p.mark("s")
                m_g1 = p.mark("s")
                marks[("sc2_used", par)] = m_g1
                marks[("xq2s_used", wpar)] = p.mark("t")
                m_r, m_xc = ln_stats(p, x1c6[par][0:rk, :], bn12, mv_t, rstd_t,
                                     xc6[0:rk, :], rows=rk,
                                     pre=[mk_l] + p.after("s", "v"),
                                     xc_pre=p.after("s"))
                p.emit("scalar", lambda e, rr=rk:
                       e.activation(xn6[0:rr, :], xc6[0:rr, :], AF.Identity,
                                    scale=rstd_t[0:rr, 0:1]),
                       waits=[m_r, m_xc, marks.get(("xn6_used",))] + p.after("s"),
                       inc="s")
                m_xn6 = p.mark("s")
                for half in range(2):
                    p.emit("tensor", lambda e, h3=half, t1=t0, rr=rk, bk=pj6[half]:
                           e.matmul(bk[0:rr, :], Q1_sb[:, t1:t1 + rr],
                                    Msb[:, 512 * h3:512 * (h3 + 1)],
                                    start=True, stop=True),
                           waits=[marks.get(("pj6_used", half))], inc="t")
                    m_j = p.mark("t")
                    p.emit("vector", lambda e, h3=half, rr=rk, bk=pj6[half]:
                           e.scalar_tensor_tensor(
                               Jn6[0:rr, 512 * h3:512 * (h3 + 1)],
                               bk[0:rr, :], ISQ_DK,
                               xn6[0:rr, 512 * h3:512 * (h3 + 1)],
                               ALU.mult, ALU.add),
                           waits=[m_j, m_xn6] + p.after("v"), inc="v")
                    marks[("pj6_used", half)] = p.mark("v")
                marks[("xn6_used",)] = p.mark("v")
                m_jn = p.mark("v")
                m_r2, m_xc2 = ln_stats(p, Jn6[0:rk, :], bn12, mv_t, rstd_t,
                                       xc6[0:rk, :], rows=rk,
                                       pre=[m_jn] + p.after("s"),
                                       xc_pre=p.after("s"))
                p.emit("scalar", lambda e, rr=rk:
                       e.activation(tn6[0:rr, :], xc6[0:rr, :], AF.Identity,
                                    scale=rstd_t[0:rr, 0:1]),
                       waits=[m_r2, m_xc2, marks.get(("tn6_used",))] +
                             p.after("s"), inc="s")
                m_tn = p.mark("s")
                p.emit("vector", lambda e, rr=rk:
                       e.tensor_tensor(s1t[0:rr, :], tn6[0:rr, :],
                                       gam_bc[0:rr, :], ALU.mult),
                       waits=[m_tn, marks["gb_bc"]] + p.after("v"), inc="v")
                marks[("tn6_used",)] = p.mark("v")
                p.emit("vector", lambda e, rr=rk:
                       e.tensor_tensor(s1t[0:rr, :], s1t[0:rr, :],
                                       bet_bc[0:rr, :], ALU.add),
                       waits=p.after("v"), inc="v")
                p.emit("vector", lambda e, rr=rk, g1=g1t[par]:
                       e.tensor_tensor(s1t[0:rr, :], s1t[0:rr, :],
                                       g1[0:rr, :], ALU.mult),
                       waits=[m_g1] + p.after("v"), inc="v")
                marks[("g1_used", par)] = p.mark("v")
                p.emit("vector", lambda e, rr=rk, b=x1c6[par], o=ot6[par]:
                       e.tensor_tensor(o[0:rr, :], s1t[0:rr, :], b[0:rr, :],
                                       ALU.add),
                       waits=[marks.get(("ot_used", par))] + p.after("v"),
                       inc="v")
                m_out = p.mark("v")
                marks[("x1c6_used", par)] = m_out
                mk_st = p.emit("sync", lambda e, t1=t0, rr=rk, o=ot6[par]:
                               e.dma_start(out_d[t1:t1 + rr, :], o[0:rr, :]),
                               waits=[m_out], inc=f"de{par}", k=16)
                marks[("ot_used", par)] = mk_st
        p.flush()


def _prep_inputs(inputs):
    f32 = np.float32
    W_qkv = np.asarray(inputs["L0_attn_W_qkv"], f32)
    base = {
        "wqT": np.ascontiguousarray(W_qkv[0:D, :].T),
        "wkT": np.ascontiguousarray(W_qkv[D:2 * D, :].T),
        "wvT": np.ascontiguousarray(W_qkv[2 * D:3 * D, :].T),
        "wpT": np.ascontiguousarray(np.asarray(inputs["L0_attn_W_proj"], f32).T),
        "bproj": np.ascontiguousarray(
            np.asarray(inputs["L0_attn_b_proj"], f32).reshape(1, D)),
        "wgT": np.ascontiguousarray(np.asarray(inputs["L0_ffn_w_gate"]).T),
        "wuT": np.ascontiguousarray(np.asarray(inputs["L0_ffn_w_up"]).T),
        "wdT": np.ascontiguousarray(np.asarray(inputs["L0_ffn_w_down"]).T),
        "wQT": np.ascontiguousarray(np.asarray(inputs["L1_attn_W_Q"], f32).T),
        "wKT": np.ascontiguousarray(np.asarray(inputs["L1_attn_W_K"], f32).T),
        "gam": np.ascontiguousarray(
            np.asarray(inputs["L1_attn_gamma"], f32).reshape(1, D)),
        "bet": np.ascontiguousarray(
            np.asarray(inputs["L1_attn_beta"], f32).reshape(1, D)),
        "w1T": np.ascontiguousarray(np.asarray(inputs["L1_ffn_w1"], f32).T),
        "cw": np.ascontiguousarray(
            np.asarray(inputs["L1_ffn_conv_w"], f32).reshape(32, 128, 3)
            .transpose(1, 0, 2).reshape(128, 96)),
        "cb": np.ascontiguousarray(
            np.asarray(inputs["L1_ffn_conv_b"], f32).reshape(32, 128).T),
        "w2T": np.ascontiguousarray(np.asarray(inputs["L1_ffn_w2"], f32).T),
    }
    x_all = np.asarray(inputs["L0_attn_x"], f32)
    return base, x_all


def kernel(**inputs):
    base, x_all = _prep_inputs(inputs)
    nc = build_program()
    in_maps = []
    for b in range(8):
        m = dict(base)
        m["x"] = np.ascontiguousarray(x_all[b])
        in_maps.append(m)
    res = run_bass_kernel_spmd(nc, in_maps, core_ids=list(range(8)))
    out = np.stack([np.asarray(r["out"], dtype=np.float32)
                    for r in res.results], axis=0)
    return out


# version 11
# speedup vs baseline: 1.2870x; 1.2870x over previous
from collections import defaultdict
from contextlib import ExitStack

import numpy as np
import ml_dtypes

import concourse.bass as bass
import concourse.mybir as mybir
from concourse.bass_utils import run_bass_kernel_spmd

F32 = mybir.dt.float32
BF16 = mybir.dt.bfloat16
I32 = mybir.dt.int32
AF = mybir.ActivationFunctionType
ALU = mybir.AluOpType
AX = mybir.AxisListType

MAGIC = 12582912.0
N = 4096
D = 1024
NH = 16
HD = 64
F0 = 3072
DK = 128
F1 = 4096
INV127 = 1.0 / 127.0
INV127SQ = (1.0 / 127.0) * (1.0 / 127.0)
ISQ_DK = 1.0 / float(np.sqrt(np.float64(DK)))

WS = 510
NWIN = 9

ENGS = ("sync", "tensor", "vector", "scalar", "gpsimd")


class Prog:

    def __init__(self, nc):
        self.nc = nc
        self.items = []
        self.cnt = defaultdict(int)
        self.h = {}
        names = ["d", "g", "t", "v", "s", "p"]
        for ab in "abcdefg":
            names += [f"d{ab}0", f"d{ab}1"]
        for nm in names:
            self.h[nm] = nc.alloc_semaphore(name=f"sem_{nm}")
        self.waited = {e_: defaultdict(int) for e_ in ENGS}
        self._pending = None
        self.eps_ln = None

    def after(self, *sems):
        return [(nm, self.cnt[nm]) for nm in sems]

    def mark(self, sem):
        return (sem, self.cnt[sem])

    OWN = {"sync": None, "tensor": "t", "vector": "v", "scalar": "s",
           "gpsimd": "p"}
    DROP_SELF = False

    def emit(self, eng, fn, waits=(), inc=None, k=1, ks=False):
        if inc is not None:
            self.cnt[inc] += k
        ws = [w for w in waits if w is not None]
        if self.DROP_SELF and not ks:
            own = self.OWN[eng]
            ws = [w for w in ws if w[0] != own]
        self.items.append((eng, fn, ws, inc, k))
        return (inc, self.cnt[inc]) if inc is not None else None

    def flush(self, barrier_after=True):
        nc = self.nc
        items = self.items
        pend = self._pending or {}

        def runner(eng_name):
            def run(eng):
                extra = pend.get(eng_name)
                used_extra = [False]
                for it_eng, fn, waits, inc, k in items:
                    if it_eng != eng_name:
                        continue
                    ws = list(waits)
                    if extra is not None and not used_extra[0]:
                        ws = extra + ws
                        used_extra[0] = True
                    for nm, v in ws:
                        if v > 0 and self.waited[eng_name][nm] < v:
                            eng.wait_ge(self.h[nm], v)
                            self.waited[eng_name][nm] = v
                    bi = fn(eng)
                    if inc is not None:
                        assert bi is not None
                        bi.then_inc(self.h[inc], k)
            return run

        with nc.Block() as block:
            block.sync(runner("sync"))
            block.tensor(runner("tensor"))
            block.vector(runner("vector"))
            block.scalar(runner("scalar"))
            block.gpsimd(runner("gpsimd"))
        self.items = []
        if barrier_after:
            snap = [(nm, self.cnt[nm]) for nm in self.h if self.cnt[nm] > 0]
            self._pending = {e_: list(snap) for e_ in ENGS}
        else:
            self._pending = None


def round_emit(p, out_ap, in_ap, tt, tr, gatef, scale_ap=None, pre=(),
               eng="vector"):
    sem = {"vector": "v", "gpsimd": "p"}[eng]
    if scale_ap is not None:
        p.emit(eng, lambda e, o=tt, i=in_ap, s=scale_ap:
               e.tensor_scalar(o, i, s, 0.5, ALU.mult, ALU.subtract),
               waits=list(pre), inc=sem)
    else:
        p.emit(eng, lambda e, o=tt, i=in_ap:
               e.tensor_single_scalar(o, i, 0.5, ALU.subtract),
               waits=list(pre), inc=sem)
    p.emit(eng, lambda e, o=gatef, i=tt:
           e.tensor_scalar(o, i, MAGIC, MAGIC, ALU.add, ALU.subtract),
           waits=p.after(sem), inc=sem)
    p.emit(eng, lambda e, o=tr, i=tt, gg=gatef:
           e.tensor_tensor(o, i, gg, ALU.subtract),
           waits=p.after(sem), inc=sem)
    if eng == "vector":
        p.emit(eng, lambda e, o=out_ap, i=tr, gg=gatef:
               e.scalar_tensor_tensor(o, i, 0.0, gg, ALU.is_ge, ALU.add),
               waits=p.after(sem), inc=sem)
    else:
        p.emit(eng, lambda e, i=tr:
               e.tensor_single_scalar(i, i, 0.0, ALU.is_ge),
               waits=p.after(sem), inc=sem)
        p.emit(eng, lambda e, o=out_ap, i=tr, gg=gatef:
               e.tensor_tensor(o, i, gg, ALU.add),
               waits=p.after(sem), inc=sem)
    return p.mark(sem)


def round_wide(p, out_ap, y_ap, u_t, d_t, gatef, K=2, pre=(), eng="vector",
               use_s=False):
    sem = {"vector": "v", "gpsimd": "p"}[eng]
    if use_s is True:
        p.emit("scalar", lambda e, o=d_t, i=y_ap:
               e.activation(o, i, AF.Identity, bias=p.Mt[0:y_ap.shape[0], :]),
               waits=list(pre) + p.after("s"), inc="s", ks=True)
        p.emit("scalar", lambda e, o=gatef, i=d_t:
               e.activation(o, i, AF.Identity, bias=p.negMt[0:y_ap.shape[0], :]),
               waits=p.after("s"), inc="s", ks=True)
        m_g = p.mark("s")
        p.emit(eng, lambda e, o=d_t, i=y_ap, gg=gatef:
               e.tensor_tensor(o, i, gg, ALU.subtract),
               waits=[m_g] + p.after(sem), inc=sem)
    else:
        p.emit(eng, lambda e, o=gatef, i=y_ap:
               e.tensor_scalar(o, i, MAGIC, MAGIC, ALU.add, ALU.subtract),
               waits=list(pre), inc=sem)
        p.emit(eng, lambda e, o=d_t, i=y_ap, gg=gatef:
               e.tensor_tensor(o, i, gg, ALU.subtract),
               waits=p.after(sem), inc=sem)
    if eng == "vector":
        p.emit(eng, lambda e, o=u_t, i=d_t, gg=gatef:
               e.scalar_tensor_tensor(o, i, 0.5, gg, ALU.is_ge, ALU.add),
               waits=p.after(sem), inc=sem)
        for j in range(2, K + 1):
            p.emit(eng, lambda e, o=u_t, i=d_t, th=j - 0.5:
                   e.scalar_tensor_tensor(o, i, th, o, ALU.is_ge, ALU.add),
                   waits=p.after(sem), inc=sem)
        p.emit(eng, lambda e, o=u_t, i=d_t:
               e.scalar_tensor_tensor(o, i, -0.5, o, ALU.is_lt, ALU.subtract),
               waits=p.after(sem), inc=sem)
        for j in range(2, K + 1):
            p.emit(eng, lambda e, o=u_t, i=d_t, th=-(j - 0.5):
                   e.scalar_tensor_tensor(o, i, th, o, ALU.is_lt, ALU.add),
                   waits=p.after(sem), inc=sem)
        if use_s:
            p.emit("scalar", lambda e, o=out_ap, i=u_t:
                   e.activation(o, i, AF.Identity, scale=-1.0),
                   waits=p.after(sem) + p.after("s"), inc="s", ks=True)
            return p.mark("s")
        p.emit(eng, lambda e, o=out_ap, i=u_t:
               e.tensor_single_scalar(o, i, -1.0, ALU.mult),
               waits=p.after(sem), inc=sem)
    else:
        p.emit(eng, lambda e, o=u_t, gg=gatef: e.tensor_copy(o, gg),
               waits=p.after(sem), inc=sem)
        for j in range(1, K + 1):
            p.emit(eng, lambda e, o=y_ap, i=d_t, th=j - 0.5:
                   e.tensor_single_scalar(o, i, th, ALU.is_ge),
                   waits=p.after(sem), inc=sem)
            p.emit(eng, lambda e, o=u_t, c=y_ap:
                   e.tensor_tensor(o, o, c, ALU.add),
                   waits=p.after(sem), inc=sem)
        for j in range(1, K + 1):
            p.emit(eng, lambda e, o=y_ap, i=d_t, th=-(j - 0.5):
                   e.tensor_single_scalar(o, i, th, ALU.is_lt),
                   waits=p.after(sem), inc=sem)
            p.emit(eng, lambda e, o=u_t, c=y_ap:
                   e.tensor_tensor(o, o, c, ALU.subtract),
                   waits=p.after(sem), inc=sem)
        p.emit(eng, lambda e, o=out_ap, i=u_t: e.tensor_copy(o, i),
               waits=p.after(sem), inc=sem)
    return p.mark(sem)


def ln_stats(p, x_ap, bn12, mv, rstd, xc_ap, rows=128, pre=(), xc_pre=()):
    p.emit("vector", lambda e, o=bn12[0:rows, 0:6], i=x_ap[:, 0:512]:
           e.bn_stats(o, i), waits=list(pre), inc="v")
    p.emit("vector", lambda e, o=bn12[0:rows, 6:12], i=x_ap[:, 512:1024]:
           e.bn_stats(o, i), waits=list(pre), inc="v")
    p.emit("vector", lambda e, o=mv[0:rows, :], i=bn12[0:rows, :]:
           e.bn_aggr(o, i), waits=p.after("v"), inc="v", ks=True)
    m_aggr = p.mark("v")
    p.emit("vector", lambda e, o=xc_ap, i=x_ap, m=mv[0:rows, 0:1]:
           e.tensor_single_scalar(o, i, m, ALU.subtract),
           waits=[m_aggr] + list(xc_pre), inc="v", ks=True)
    m_xc = p.mark("v")
    p.emit("scalar", lambda e, o=rstd[0:rows, :], i=mv[0:rows, 1:2]:
           e.activation(o, i, AF.Ln, bias=p.eps_ln[0:rows, :]),
           waits=[m_aggr], inc="s")
    p.emit("scalar", lambda e, o=rstd[0:rows, :], i=rstd[0:rows, :]:
           e.activation(o, i, AF.Exp, scale=-0.5),
           waits=p.after("s"), inc="s", ks=True)
    return p.mark("s"), m_xc


def build_program(debug_outputs=(), phases=7):
    nc = bass.Bass("TRN2", target_bir_lowering=False)
    p = Prog(nc)

    x0_d = nc.dram_tensor("x", [N, D], F32, kind="ExternalInput").ap()
    wqT_d = nc.dram_tensor("wqT", [D, D], F32, kind="ExternalInput").ap()
    wkT_d = nc.dram_tensor("wkT", [D, D], F32, kind="ExternalInput").ap()
    wvT_d = nc.dram_tensor("wvT", [D, D], F32, kind="ExternalInput").ap()
    wpT_d = nc.dram_tensor("wpT", [D, D], F32, kind="ExternalInput").ap()
    bproj_d = nc.dram_tensor("bproj", [1, D], F32, kind="ExternalInput").ap()
    wgT_d = nc.dram_tensor("wgT", [D, F0], I32, kind="ExternalInput").ap()
    wuT_d = nc.dram_tensor("wuT", [D, F0], I32, kind="ExternalInput").ap()
    wdT_d = nc.dram_tensor("wdT", [F0, D], I32, kind="ExternalInput").ap()
    wQT_d = nc.dram_tensor("wQT", [D, DK], F32, kind="ExternalInput").ap()
    wKT_d = nc.dram_tensor("wKT", [D, DK], F32, kind="ExternalInput").ap()
    gam_d = nc.dram_tensor("gam", [1, D], F32, kind="ExternalInput").ap()
    bet_d = nc.dram_tensor("bet", [1, D], F32, kind="ExternalInput").ap()
    w1T_d = nc.dram_tensor("w1T", [D, F1], F32, kind="ExternalInput").ap()
    cw_d = nc.dram_tensor("cw", [128, 96], F32, kind="ExternalInput").ap()
    cb_d = nc.dram_tensor("cb", [128, 32], F32, kind="ExternalInput").ap()
    w2T_d = nc.dram_tensor("w2T", [F1, D], F32, kind="ExternalInput").ap()
    out_d = nc.dram_tensor("out", [N, D], F32, kind="ExternalOutput").ap()

    def scratch(name, shape, dtype):
        kind = "ExternalOutput" if name in debug_outputs else "Internal"
        return nc.dram_tensor(name, shape, dtype, kind=kind).ap()

    xn0_d = scratch("xn0", [N, D], BF16)
    g0_d = scratch("g0", [N, D], F32)
    x1_d = scratch("x1", [N, D], F32)
    w1q_d = scratch("w1q", [8, 128, F1], BF16)
    w2q_d = scratch("w2q", [32, 128, D], BF16)
    xq2_d = scratch("xq2", [NWIN, 128, 32 * 512], BF16)
    kvdbg_d = (nc.dram_tensor("kvdbg", [64, NH * 65], BF16,
                              kind="ExternalOutput").ap()
               if "kvdbg" in debug_outputs else None)

    ctx = ExitStack()
    en = ctx.enter_context

    ident = en(nc.sbuf_tensor("ident", [128, 128], BF16))
    identf = en(nc.sbuf_tensor("identf", [128, 128], F32))
    ones1f = en(nc.sbuf_tensor("ones1f", [1, 128], F32))
    ones1b = en(nc.sbuf_tensor("ones1b", [1, 128], BF16))
    onesc = en(nc.sbuf_tensor("onesc", [128, 1], F32))
    eps_ln = en(nc.sbuf_tensor("eps_ln", [128, 1], F32))
    p.eps_ln = eps_ln
    m_bc = en(nc.sbuf_tensor("m_bc", [128, 4], F32))
    mval = en(nc.sbuf_tensor("mval", [1, 8], F32))
    kv_sb = en(nc.sbuf_tensor("kv_sb", [64, NH * 65], BF16))
    Msb = en(nc.sbuf_tensor("Msb", [128, D], BF16))
    Q1_sb = en(nc.sbuf_tensor("Q1_sb", [128, N], BF16))
    recs2_sb = en(nc.sbuf_tensor("recs2_sb", [128, NWIN * 4], F32))
    bn12 = en(nc.sbuf_tensor("bn12", [128, 12], F32))
    mv_t = en(nc.sbuf_tensor("mv_t", [128, 2], F32))
    rstd_t = en(nc.sbuf_tensor("rstd_t", [128, 1], F32))
    gam_bc = en(nc.sbuf_tensor("gam_bc", [128, D], F32))
    bet_bc = en(nc.sbuf_tensor("bet_bc", [128, D], F32))
    Mt = en(nc.sbuf_tensor("Mt", [128, 1], F32))
    negMt = en(nc.sbuf_tensor("negMt", [128, 1], F32))
    p.Mt = Mt
    p.negMt = negMt

    marks = {}

    with ExitStack() as px:
        pe = px.enter_context
        iota_i = pe(nc.sbuf_tensor("iota_i", [128, 128], I32))
        grow = pe(nc.sbuf_tensor("grow", [1, D], F32))
        brow = pe(nc.sbuf_tensor("brow", [1, D], F32))
        mtile = [pe(nc.sbuf_tensor(f"mtile{i}", [128, 2048], F32)) for i in range(2)]
        mred = pe(nc.sbuf_tensor("mred", [128, 2], F32))
        macc = pe(nc.sbuf_tensor("macc", [128, 2], F32))
        ps_bc = pe(nc.psum_tensor("ps_bc", [128, 512], F32))
        ps_m = pe(nc.psum_tensor("ps_m", [1, 1], F32))
        ps_mb = pe(nc.psum_tensor("ps_mb", [128, 1], F32))

        p.emit("gpsimd", lambda e: e.iota(iota_i[:, :], [[1, 128]],
                                          channel_multiplier=-1), inc="p")
        m_iota = p.mark("p")
        p.emit("vector", lambda e: e.tensor_single_scalar(
            ident[:], iota_i[:], 0, ALU.is_equal), waits=[m_iota], inc="v")
        p.emit("vector", lambda e: e.tensor_single_scalar(
            identf[:], iota_i[:], 0, ALU.is_equal), waits=[m_iota], inc="v")
        p.emit("vector", lambda e: e.memset(ones1f[:], 1.0), inc="v")
        p.emit("vector", lambda e: e.memset(ones1b[:], 1.0), inc="v")
        p.emit("vector", lambda e: e.memset(onesc[:], 1.0), inc="v")
        p.emit("vector", lambda e: e.memset(eps_ln[:], 1e-5), inc="v")
        p.emit("vector", lambda e: e.memset(Mt[:], MAGIC), inc="v")
        p.emit("vector", lambda e: e.memset(negMt[:], -MAGIC), inc="v")
        p.emit("vector", lambda e: e.memset(macc[:], 0.0), inc="v")
        m_consts = p.mark("v")

        for wi, (w_d, n_k, n_n, fw, fdim) in enumerate(
                ((w1T_d, 8, 2, 2048, F1), (w2T_d, 32, 1, 1024, F1))):
            for c in range(n_k):
                for nn2 in range(n_n):
                    i2 = c * n_n + nn2
                    buf = mtile[i2 % 2]
                    mk_free = marks.get(("mt", i2 % 2))
                    mk_ld = p.emit(
                        "sync", lambda e, b=buf, w=w_d, cc=c, n3=nn2, f2=fw:
                        e.dma_start(b[:, 0:f2], w[128 * cc:128 * (cc + 1),
                                                  f2 * n3:f2 * (n3 + 1)]),
                        waits=[mk_free] if mk_free else [],
                        inc=f"da{i2 % 2}", k=16)
                    p.emit("vector",
                           lambda e, b=buf, wi2=wi, f2=fw:
                           e.reduce_sum(mred[:, wi2:wi2 + 1], b[:, 0:f2], AX.X,
                                        apply_absolute_value=True),
                           waits=[mk_ld] + p.after("v"), inc="v", ks=True)
                    marks[("mt", i2 % 2)] = p.mark("v")
                    p.emit("vector",
                           lambda e, wi2=wi:
                           e.tensor_tensor(macc[:, wi2:wi2 + 1],
                                           macc[:, wi2:wi2 + 1],
                                           mred[:, wi2:wi2 + 1], ALU.add),
                           waits=p.after("v"), inc="v", ks=True)
            m_acc = p.mark("v")
            p.emit("tensor", lambda e, wi2=wi:
                   e.matmul(ps_m[:, :], macc[:, wi2:wi2 + 1], onesc[:, :],
                            start=True, stop=True),
                   waits=[m_acc, m_consts], inc="t")
            p.emit("vector", lambda e, wi2=wi, sc=1.0 / (fdim * D):
                   e.tensor_single_scalar(mval[0:1, 4 * wi2:4 * wi2 + 1],
                                          ps_m[:, :], sc, ALU.mult),
                   waits=p.after("t"), inc="v")
            p.emit("vector", lambda e, wi2=wi:
                   e.tensor_single_scalar(mval[0:1, 4 * wi2:4 * wi2 + 1],
                                          mval[0:1, 4 * wi2:4 * wi2 + 1],
                                          1e-5, ALU.max),
                   waits=p.after("v"), inc="v", ks=True)
            p.emit("vector", lambda e, wi2=wi:
                   e.reciprocal(mval[0:1, 4 * wi2 + 1:4 * wi2 + 2],
                                mval[0:1, 4 * wi2:4 * wi2 + 1]),
                   waits=p.after("v"), inc="v", ks=True)
            m_mv = p.mark("v")
            for part in range(2):
                p.emit("tensor", lambda e, wi2=wi, pp=part:
                       e.matmul(ps_mb[:, :], ones1f[:, :],
                                mval[0:1, 4 * wi2 + pp:4 * wi2 + pp + 1],
                                start=True, stop=True),
                       waits=[m_mv] + p.after("v"), inc="t")
                p.emit("vector", lambda e, wi2=wi, pp=part:
                       e.tensor_copy(m_bc[:, 2 * wi2 + pp:2 * wi2 + pp + 1],
                                     ps_mb[:, :]),
                       waits=p.after("t"), inc="v")
        marks["m_bc"] = p.mark("v")

        p.emit("sync", lambda e: e.dma_start(grow[:], gam_d), inc="d", k=16)
        p.emit("sync", lambda e: e.dma_start(brow[:], bet_d), inc="d", k=16)
        m_rows = p.mark("d")
        for half in range(2):
            for (row, dst) in ((grow, gam_bc), (brow, bet_bc)):
                p.emit("tensor", lambda e, r=row, h2=half:
                       e.matmul(ps_bc[:, :], ones1f[:, :],
                                r[0:1, 512 * h2:512 * (h2 + 1)],
                                start=True, stop=True),
                       waits=[m_rows, m_consts] + p.after("v"), inc="t")
                p.emit("vector", lambda e, d2=dst, h2=half:
                       e.tensor_copy(d2[:, 512 * h2:512 * (h2 + 1)],
                                     ps_bc[:, :]),
                       waits=p.after("t"), inc="v")
        marks["gb_bc"] = p.mark("v")
        p.flush()

    if phases >= 1:
        with ExitStack() as px:
            pe = px.enter_context
            x0t = [pe(nc.sbuf_tensor(f"x0t{i}", [128, 4 * D], F32)) for i in range(2)]
            xnt = [pe(nc.sbuf_tensor(f"xnt{i}", [128, 4 * D], BF16)) for i in range(2)]
            xc1 = pe(nc.sbuf_tensor("xc1p1", [128, D], F32))
            marks["xn0"] = [None] * 8
            last_xn = {}
            for j in range(8):
                par = j % 2
                mk_ld = p.emit(
                    "sync", lambda e, b=x0t[par], jj=j:
                    e.dma_start(b[:].rearrange("p (a d) -> p a d", a=4),
                                x0_d[512 * jj:512 * (jj + 1), :]
                                .rearrange("(a p) d -> p a d", p=128)),
                    waits=[last_xn.get(par)], inc=f"da{par}", k=16)
                for tc in range(4):
                    xap = x0t[par][:, D * tc:D * (tc + 1)]
                    m_r, m_xc = ln_stats(p, xap, bn12, mv_t, rstd_t, xc1[:, :],
                                         pre=[mk_ld] + p.after("s"),
                                         xc_pre=p.after("s"))
                    p.emit("scalar",
                           lambda e, o=xnt[par][:, D * tc:D * (tc + 1)],
                           i=xc1[:, :], r=rstd_t[:, 0:1]:
                           e.activation(o, i, AF.Identity, scale=r),
                           waits=[m_r, m_xc,
                                  marks.get(("xnst", par))], inc="s")
                last_xn[par] = p.mark("s")
                mk_st = p.emit(
                    "sync", lambda e, b=xnt[par], jj=j:
                    e.dma_start(xn0_d[512 * jj:512 * (jj + 1), :]
                                .rearrange("(a p) d -> p a d", p=128),
                                b[:].rearrange("p (a d) -> p a d", a=4)),
                    waits=[last_xn[par]], inc=f"db{par}", k=16)
                marks[("xnst", par)] = mk_st
                marks["xn0"][j] = mk_st
            p.flush()

    if phases >= 2:
        _p2(nc, p, en, marks, locals())
    if phases >= 3:
        _p3(nc, p, en, marks, locals())
    if phases >= 4:
        _p4(nc, p, en, marks, locals())
    if phases >= 5:
        _p5a(nc, p, en, marks, locals())
    if phases >= 6:
        _p5b(nc, p, en, marks, locals())

    p.emit("sync", lambda e: e.nop(), waits=p.after(*list(p.h.keys())))
    p.flush()
    ctx.close()
    return nc


def _p2(nc, p, en, marks, L):
    xn0_d, wkT_d, wvT_d, kv_sb, ident = (
        L["xn0_d"], L["wkT_d"], L["wvT_d"], L["kv_sb"], L["ident"])
    with ExitStack() as px:
        pe = px.enter_context
        wkv = pe(nc.sbuf_tensor("wkv", [128, 8 * 2048], BF16))
        xnf = [pe(nc.sbuf_tensor(f"xnf{i}", [128, 8 * 512], BF16)) for i in range(2)]
        ktmp = pe(nc.sbuf_tensor("ktmp", [128, 512], F32))
        ksb = pe(nc.sbuf_tensor("ksb", [128, 1024], BF16))
        v_aug = pe(nc.sbuf_tensor("v_aug", [128, NH * 65], BF16))
        kvw = [pe(nc.psum_tensor(f"kvw{i}", [128, 512], F32)) for i in range(4)]
        kvacc = [pe(nc.psum_tensor(f"kvacc{i}", [64, 260], F32)) for i in range(4)]

        for c in range(8):
            p.emit("gpsimd", lambda e, cc=c:
                   e.dma_start(wkv[:, 2048 * cc:2048 * cc + 1024],
                               wkT_d[128 * cc:128 * (cc + 1), :]), inc="g", k=16)
            p.emit("gpsimd", lambda e, cc=c:
                   e.dma_start(wkv[:, 2048 * cc + 1024:2048 * (cc + 1)],
                               wvT_d[128 * cc:128 * (cc + 1), :]), inc="g", k=16)
        m_wkv = p.mark("g")
        for h in range(NH):
            p.emit("vector", lambda e, hh=h:
                   e.memset(v_aug[:, 65 * hh + 64:65 * hh + 65], 1.0), inc="v")
        m_vones = p.mark("v")

        m_evac_s = None
        m_evac_v = None
        for j in range(8):
            par = j % 2
            mk_tp = None
            for c in range(8):
                mk_tp = p.emit(
                    "sync", lambda e, cc=c, jj=j, b=xnf[par]:
                    e.dma_start_transpose(
                        b[:, 512 * cc:512 * (cc + 1)],
                        xn0_d[512 * jj:512 * (jj + 1), 128 * cc:128 * (cc + 1)]),
                    waits=[marks.get(("xnf_used", par))],
                    inc=f"da{par}", k=16)
            for tc in range(4):
                first = True
                mm = None
                for c in range(8):
                    for nch in range(4):
                        w0 = []
                        if first:
                            w0 = [mk_tp, m_wkv, m_evac_s, m_evac_v]
                            first = False
                        mm = p.emit(
                            "tensor", lambda e, cc=c, nn=nch, t2=tc, b=xnf[par]:
                            e.matmul(kvw[nn][:, :],
                                     b[:, 512 * cc + 128 * t2:512 * cc + 128 * (t2 + 1)],
                                     wkv[:, 2048 * cc + 512 * nn:2048 * cc + 512 * (nn + 1)],
                                     start=(cc == 0), stop=(cc == 7)),
                            waits=w0, inc="t")
                m_qkv = p.mark("t")
                if tc == 3:
                    marks[("xnf_used", par)] = m_qkv
                for half in range(2):
                    p.emit("scalar", lambda e, h2=half:
                           e.activation(ktmp[:, :], kvw[h2][:, :], AF.Square),
                           waits=[m_qkv] + p.after("s"), inc="s")
                    p.emit("scalar", lambda e, h2=half:
                           e.activation(ksb[:, 512 * h2:512 * (h2 + 1)],
                                        ktmp[:, :], AF.Exp, scale=-0.5),
                           waits=p.after("s"), inc="s")
                m_evac_s = p.mark("s")
                for h in range(NH):
                    src = kvw[2 + h // 8]
                    p.emit("vector", lambda e, hh=h, sp=src:
                           e.tensor_copy(v_aug[:, 65 * hh:65 * hh + 64],
                                         sp[:, 64 * (hh % 8):64 * (hh % 8) + 64]),
                           waits=[m_qkv, m_vones], inc="v")
                m_evac_v = p.mark("v")
                for h in range(NH):
                    g4, slot = h // 4, h % 4
                    stf = (j == 0 and tc == 0 and slot == 0)
                    spf = (j == 7 and tc == 3 and slot == 3)
                    p.emit("tensor", lambda e, hh=h, g2=g4, s2=slot, st=stf, sp=spf:
                           e.matmul(kvacc[g2][0:64, 65 * s2:65 * (s2 + 1)],
                                    ksb[:, 64 * hh:64 * (hh + 1)],
                                    v_aug[:, 65 * hh:65 * (hh + 1)],
                                    start=st, stop=sp),
                           waits=[m_evac_s, m_evac_v], inc="t")
        m_kvmm = p.mark("t")
        for g4 in range(4):
            p.emit("vector", lambda e, g2=g4:
                   e.tensor_copy(kv_sb[0:64, 260 * g2:260 * (g2 + 1)],
                                 kvacc[g2][0:64, :]),
                   waits=[m_kvmm], inc="v")
        marks["kv_sb"] = p.mark("v")
        if "kvdbg" in L["debug_outputs"]:
            kvdbg_d = L["kvdbg_d"]
            p.emit("sync", lambda e: e.dma_start(kvdbg_d, kv_sb[0:64, :]),
                   waits=[marks["kv_sb"]], inc="d", k=16)
        p.flush()


def _p3(nc, p, en, marks, L):
    xn0_d, g0_d = L["xn0_d"], L["g0_d"]
    wgT_d, wuT_d, wdT_d = L["wgT_d"], L["wuT_d"], L["wdT_d"]
    w1T_d, w2T_d, w1q_d, w2q_d = L["w1T_d"], L["w2T_d"], L["w1q_d"], L["w2q_d"]
    m_bc = L["m_bc"]
    with ExitStack() as px:
        pe = px.enter_context
        wg = pe(nc.sbuf_tensor("wg", [128, 8 * F0], BF16))
        wu = pe(nc.sbuf_tensor("wu", [128, 8 * F0], BF16))
        wd = pe(nc.sbuf_tensor("wd", [128, 24 * 1024], BF16))
        xnf3 = [pe(nc.sbuf_tensor(f"xnf3{i}", [128, 8 * 256], BF16)) for i in range(2)]
        hsb = pe(nc.sbuf_tensor("hsb", [128, 24 * 256], BF16))
        sgt = [pe(nc.sbuf_tensor(f"sgt{i}", [128, 256], F32)) for i in range(2)]
        g0t = [pe(nc.sbuf_tensor(f"g0t{i}", [128, D], F32)) for i in range(2)]
        qtile = [pe(nc.sbuf_tensor("qtile0", [128, 512], F32))] * 2
        qb = pe(nc.sbuf_tensor("qb", [128, 512], F32))
        qout = [pe(nc.sbuf_tensor(f"qout{i}", [128, 512], BF16)) for i in range(2)]
        qtr = pe(nc.sbuf_tensor("qtr", [128, 512], F32))
        qgate = pe(nc.sbuf_tensor("qgate", [128, 512], I32))
        pg = [pe(nc.psum_tensor(f"pg{i}", [128, 256], F32)) for i in range(2)]
        pu = [pe(nc.psum_tensor(f"pu{i}", [128, 256], F32)) for i in range(2)]
        pd = [pe(nc.psum_tensor(f"pd{i}", [128, 512], F32)) for i in range(4)]

        for c in range(8):
            p.emit("gpsimd", lambda e, cc=c:
                   e.dma_start(wg[:, F0 * cc:F0 * (cc + 1)],
                               wgT_d[128 * cc:128 * (cc + 1), :]), inc="g", k=16)
            p.emit("gpsimd", lambda e, cc=c:
                   e.dma_start(wu[:, F0 * cc:F0 * (cc + 1)],
                               wuT_d[128 * cc:128 * (cc + 1), :]), inc="g", k=16)
        m_wgu = p.mark("g")
        for m in range(24):
            p.emit("gpsimd", lambda e, mm=m:
                   e.dma_start(wd[:, 1024 * mm:1024 * (mm + 1)],
                               wdT_d[128 * mm:128 * (mm + 1), :]), inc="g", k=16)
        m_w = p.mark("g")

        qjobs = []
        for c in range(8):
            for nn2 in range(8):
                qjobs.append((0, w1T_d, w1q_d, c, nn2))
        for c in range(32):
            for nn2 in range(2):
                qjobs.append((1, w2T_d, w2q_d, c, nn2))
        qidx = [0]

        def quant_step():
            if qidx[0] >= len(qjobs):
                return
            wi, src_d, dst_d, c, nn2 = qjobs[qidx[0]]
            par = qidx[0] % 2
            qidx[0] += 1
            mk_ld = p.emit("gpsimd", lambda e, b=qtile[par], w=src_d, cc=c, n3=nn2:
                           e.dma_start(b[:], w[128 * cc:128 * (cc + 1),
                                              512 * n3:512 * (n3 + 1)]),
                           waits=[marks.get(("qt_used", par))],
                           inc=f"df{par}", k=16)
            p.emit("vector", lambda e, b=qtile[par], sc=m_bc[:, 2 * wi + 1:2 * wi + 2]:
                   e.tensor_single_scalar(qb[:], b[:], sc, ALU.mult),
                   waits=[mk_ld, marks["m_bc"]] + p.after("v"), inc="v")
            marks[("qt_used", par)] = p.mark("v")
            p.emit("vector", lambda e:
                   e.tensor_scalar(qb[:], qb[:], 1.0, -1.0, ALU.min, ALU.max),
                   waits=p.after("v"), inc="v")
            mk_r = round_emit(p, qout[par][:, :], qb[:], qb[:], qtr[:],
                              qgate[:].bitcast(F32),
                              pre=p.after("v") + [marks.get(("qo_used", par))])
            marks[("qo_used", par)] = p.emit(
                "gpsimd", lambda e, b=qout[par], w=dst_d, cc=c, n3=nn2:
                e.dma_start(w[cc, :, 512 * n3:512 * (n3 + 1)], b[:]),
                waits=[mk_r], inc=f"dg{par}", k=16)

        m_sg = {}
        m_h = {}
        m_g0s = {}
        marks["g0"] = [None] * 32
        for j in range(16):
            par = j % 2
            mk_tp = None
            for c in range(8):
                mk_tp = p.emit(
                    "sync", lambda e, cc=c, jj=j, b=xnf3[par]:
                    e.dma_start_transpose(
                        b[:, 256 * cc:256 * (cc + 1)],
                        xn0_d[256 * jj:256 * (jj + 1), 128 * cc:128 * (cc + 1)]),
                    waits=[marks.get(("xnf3_used", par))],
                    inc=f"da{par}", k=16)
            for _ in range(8):
                quant_step()
            m_dn_prev = marks.get(("pd_used",))
            for m in range(24):
                mp = m % 2
                first_w = [mk_tp, m_wgu, m_sg.get(mp), m_h.get(mp)]
                mm = None
                for c in range(8):
                    mm = p.emit("tensor", lambda e, cc=c, mm2=m, b=xnf3[par]:
                                e.matmul(pg[mm2 % 2][:, :],
                                         wg[:, F0 * cc + 128 * mm2:F0 * cc + 128 * (mm2 + 1)],
                                         b[:, 256 * cc:256 * (cc + 1)],
                                         start=(cc == 0), stop=(cc == 7)),
                                waits=first_w if c == 0 else [], inc="t")
                m_pg = p.mark("t")
                for c in range(8):
                    mm = p.emit("tensor", lambda e, cc=c, mm2=m, b=xnf3[par]:
                                e.matmul(pu[mm2 % 2][:, :],
                                         wu[:, F0 * cc + 128 * mm2:F0 * cc + 128 * (mm2 + 1)],
                                         b[:, 256 * cc:256 * (cc + 1)],
                                         start=(cc == 0), stop=(cc == 7)),
                                waits=[], inc="t")
                m_pu = p.mark("t")
                p.emit("scalar", lambda e, mp2=mp:
                       e.activation(sgt[mp2][:], pg[mp2][:, :], AF.Silu,
                                    scale=INV127),
                       waits=[m_pg, m_h.get(mp)] + p.after("s"), inc="s")
                m_sg[mp] = p.mark("s")
                w_h = [m_sg[mp], m_pu]
                if m == 0:
                    w_h.append(marks.get(("hsb_used",)))
                p.emit("vector", lambda e, mp2=mp, mm2=m:
                       e.tensor_tensor(hsb[:, 256 * mm2:256 * (mm2 + 1)],
                                       sgt[mp2][:], pu[mp2][:, :], ALU.mult),
                       waits=w_h, inc="v")
                m_h[mp] = p.mark("v")
            marks[("xnf3_used", par)] = p.mark("t")
            m_hall = p.mark("v")
            for tc2 in range(2):
                for half in range(2):
                    bank = 2 * tc2 + half
                    for m in range(24):
                        w0 = []
                        if m == 0:
                            w0 = [m_hall, m_dn_prev, m_w]
                        p.emit("tensor",
                               lambda e, mm2=m, t3=tc2, h3=half, b2=bank:
                               e.matmul(pd[b2][:, :],
                                        hsb[:, 256 * mm2 + 128 * t3:256 * mm2 + 128 * (t3 + 1)],
                                        wd[:, 1024 * mm2 + 512 * h3:1024 * mm2 + 512 * (h3 + 1)],
                                        start=(mm2 == 0), stop=(mm2 == 23)),
                               waits=w0, inc="t")
                    m_pd = p.mark("t")
                    p.emit("scalar", lambda e, t3=tc2, h3=half, b2=bank:
                           e.activation(g0t[t3][:, 512 * h3:512 * (h3 + 1)],
                                        pd[b2][:, :], AF.Silu, scale=INV127SQ),
                           waits=[m_pd, m_g0s.get(tc2)] + p.after("s"), inc="s")
            marks[("pd_used",)] = p.mark("s")
            marks[("hsb_used",)] = p.mark("t")
            for tc2 in range(2):
                rc = 2 * j + tc2
                m_g0s[tc2] = p.emit(
                    "sync", lambda e, t3=tc2, r2=rc:
                    e.dma_start(g0_d[128 * r2:128 * (r2 + 1), :], g0t[t3][:]),
                    waits=[marks[("pd_used",)]], inc=f"de{tc2}", k=16)
                marks["g0"][rc] = m_g0s[tc2]
        while qidx[0] < len(qjobs):
            quant_step()
        marks["wq_done"] = None
        p.flush()


def _p4(nc, p, en, marks, L):
    xn0_d, x0_d, g0_d, x1_d = L["xn0_d"], L["x0_d"], L["g0_d"], L["x1_d"]
    wqT_d, wpT_d, wQT_d, wKT_d, bproj_d = (
        L["wqT_d"], L["wpT_d"], L["wQT_d"], L["wKT_d"], L["bproj_d"])
    kv_sb, Msb, Q1_sb, ident, ones1b = (
        L["kv_sb"], L["Msb"], L["Q1_sb"], L["ident"], L["ones1b"])
    bn12, mv_t, rstd_t = L["bn12"], L["mv_t"], L["rstd_t"]
    with ExitStack() as px:
        pe = px.enter_context
        wq2 = pe(nc.sbuf_tensor("wq2", [128, 8 * 1024], BF16))
        wp2 = pe(nc.sbuf_tensor("wp2", [128, 8 * 1024], BF16))
        wQs = pe(nc.sbuf_tensor("wQs", [128, 8 * 128], BF16))
        wKs = pe(nc.sbuf_tensor("wKs", [128, 8 * 128], BF16))
        bproj_b = pe(nc.sbuf_tensor("bproj_b", [1, D], BF16))
        xnf4 = [pe(nc.sbuf_tensor(f"xnf4{i}", [128, 8 * 512], BF16)) for i in range(2)]
        qtmp = pe(nc.sbuf_tensor("qtmp", [128, 512], F32))
        q_sb2 = pe(nc.sbuf_tensor("q_sb2", [64, 16 * 512], BF16))
        num_sb = pe(nc.sbuf_tensor("num_sb", [128, 16 * 64], BF16))
        den16 = pe(nc.sbuf_tensor("den16", [128, 16], F32))
        rec16 = pe(nc.sbuf_tensor("rec16", [128, 16], F32))
        numz_tm = pe(nc.sbuf_tensor("numz_tm", [128, 1024], BF16))
        numz_fm = pe(nc.sbuf_tensor("numz_fm", [128, 8 * 128], BF16))
        x0c = [pe(nc.sbuf_tensor(f"x0c{i}", [128, D], F32)) for i in range(2)]
        g0c = [pe(nc.sbuf_tensor(f"g0c{i}", [128, D], F32)) for i in range(2)]
        x1t = [pe(nc.sbuf_tensor(f"x1t{i}", [128, D], F32)) for i in range(2)]
        cmb = pe(nc.sbuf_tensor("cmb", [128, D], F32))
        xc4 = pe(nc.sbuf_tensor("xc4", [128, D], F32))
        xn1t = pe(nc.sbuf_tensor("xn1t", [128, D], BF16))
        xn1f = pe(nc.sbuf_tensor("xn1f", [128, 8 * 128], BF16))
        k1sb = pe(nc.sbuf_tensor("k1sb", [128, 128], BF16))
        M_acc = [pe(nc.psum_tensor(f"Macc{i}", [128, 512], F32)) for i in range(2)]
        pA = pe(nc.psum_tensor("pA", [128, 512], F32))
        pB = pe(nc.psum_tensor("pB", [128, 512], F32))
        ptp = pe(nc.psum_tensor("ptp", [128, 128], BF16))
        psA = pe(nc.psum_tensor("psA", [128, 512], F32))
        psB = pe(nc.psum_tensor("psB", [128, 512], F32))
        pk = pe(nc.psum_tensor("pk", [128, 128], F32))

        for c in range(8):
            p.emit("gpsimd", lambda e, cc=c:
                   e.dma_start(wq2[:, 1024 * cc:1024 * (cc + 1)],
                               wqT_d[128 * cc:128 * (cc + 1), :]), inc="g", k=16)
            p.emit("gpsimd", lambda e, cc=c:
                   e.dma_start(wp2[:, 1024 * cc:1024 * (cc + 1)],
                               wpT_d[128 * cc:128 * (cc + 1), :]), inc="g", k=16)
            p.emit("gpsimd", lambda e, cc=c:
                   e.dma_start(wQs[:, 128 * cc:128 * (cc + 1)],
                               wQT_d[128 * cc:128 * (cc + 1), :]), inc="g", k=16)
            p.emit("gpsimd", lambda e, cc=c:
                   e.dma_start(wKs[:, 128 * cc:128 * (cc + 1)],
                               wKT_d[128 * cc:128 * (cc + 1), :]), inc="g", k=16)
        p.emit("gpsimd", lambda e: e.dma_start(bproj_b[:], bproj_d), inc="g", k=16)
        m_w4 = p.mark("g")

        marks["x1"] = [None] * 32
        qb_banks = [pA, pB]
        m_qevac = {}
        m_nd_used = {}
        m_proj_used = {}
        m_cmb = {}
        ld_marks = {}

        def emit_xg_load(j, tc):
            rc = 4 * j + tc
            if rc >= 32 or rc in ld_marks:
                return
            parc = tc % 2
            mk1 = p.emit("sync", lambda e, b=x0c[parc], r2=rc:
                         e.dma_start(b[:], x0_d[128 * r2:128 * (r2 + 1), :]),
                         waits=[m_cmb.get(parc)], inc=f"dc{parc}", k=16)
            mk2 = p.emit("sync", lambda e, b=g0c[parc], r2=rc:
                         e.dma_start(b[:], g0_d[128 * r2:128 * (r2 + 1), :]),
                         waits=[m_cmb.get(parc)], inc=f"dd{parc}", k=16)
            ld_marks[rc] = (mk1, mk2)

        for j in range(8):
            par = j % 2
            mk_tp = None
            for c in range(8):
                mk_tp = p.emit(
                    "sync", lambda e, cc=c, jj=j, b=xnf4[par]:
                    e.dma_start_transpose(
                        b[:, 512 * cc:512 * (cc + 1)],
                        xn0_d[512 * jj:512 * (jj + 1), 128 * cc:128 * (cc + 1)]),
                    waits=[marks.get(("xnf4_used", par))],
                    inc=f"da{par}", k=16)
            emit_xg_load(j, 0)
            emit_xg_load(j, 1)
            for m in range(8):
                bank = qb_banks[m % 2]
                for c in range(8):
                    w0 = []
                    if c == 0:
                        w0 = [mk_tp, m_w4, m_qevac.get(m % 2),
                              marks.get(("q_used",))]
                    p.emit("tensor", lambda e, cc=c, mm2=m, b=xnf4[par], bk=bank:
                           e.matmul(bk[:, :],
                                    wq2[:, 1024 * cc + 128 * mm2:1024 * cc + 128 * (mm2 + 1)],
                                    b[:, 512 * cc:512 * (cc + 1)],
                                    start=(cc == 0), stop=(cc == 7)),
                           waits=w0, inc="t")
                m_q = p.mark("t")
                p.emit("scalar", lambda e, bk=bank:
                       e.activation(qtmp[:], bk[:, :], AF.Square),
                       waits=[m_q] + p.after("s"), inc="s")
                for hh2 in range(2):
                    p.emit("scalar", lambda e, mm2=m, h3=hh2:
                           e.activation(q_sb2[0:64, 512 * (2 * mm2 + h3):512 * (2 * mm2 + h3 + 1)],
                                        qtmp[64 * h3:64 * (h3 + 1), :],
                                        AF.Exp, scale=-0.5),
                           waits=p.after("s"), inc="s")
                m_qevac[m % 2] = p.mark("s")
            m_qstage = p.mark("s")
            for tc in range(4):
                rc = 4 * j + tc
                for h in range(NH):
                    bank = qb_banks[h % 2]
                    w0 = [m_nd_used.get((h % 2, "s")), m_nd_used.get((h % 2, "v"))]
                    if h == 0:
                        w0 += [m_qstage, marks["kv_sb"]]
                    p.emit("tensor", lambda e, hh=h, t2=tc, bk=bank:
                           e.matmul(bk[:, 0:65],
                                    q_sb2[0:64, 512 * hh + 128 * t2:512 * hh + 128 * (t2 + 1)],
                                    kv_sb[0:64, 65 * hh:65 * (hh + 1)],
                                    start=True, stop=True),
                           waits=w0, inc="t")
                    m_nd = p.mark("t")
                    p.emit("scalar", lambda e, hh=h, bk=bank:
                           e.copy(num_sb[:, 64 * hh:64 * (hh + 1)], bk[:, 0:64]),
                           waits=[m_nd] + p.after("s"), inc="s")
                    m_nd_used[(h % 2, "s")] = p.mark("s")
                    p.emit("vector", lambda e, hh=h, bk=bank:
                           e.tensor_copy(den16[:, hh:hh + 1], bk[:, 64:65]),
                           waits=[m_nd], inc="v")
                    m_nd_used[(h % 2, "v")] = p.mark("v")
                p.emit("vector", lambda e:
                       e.tensor_single_scalar(rec16[:], den16[:], 1e-6, ALU.add),
                       waits=p.after("v", "s"), inc="v", ks=True)
                p.emit("vector", lambda e: e.reciprocal(rec16[:], rec16[:]),
                       waits=p.after("v"), inc="v", ks=True)
                m_rec = p.mark("v")
                for h in range(NH):
                    p.emit("vector", lambda e, hh=h:
                           e.tensor_single_scalar(numz_tm[:, 64 * hh:64 * (hh + 1)],
                                                  num_sb[:, 64 * hh:64 * (hh + 1)],
                                                  rec16[:, hh:hh + 1], ALU.mult),
                           waits=[m_rec], inc="v", ks=True)
                m_numz = p.mark("v")
                m_tpc = marks.get(("ptp_free",))
                for c in range(8):
                    p.emit("tensor", lambda e, cc=c:
                           e.transpose(ptp[:, :], numz_tm[:, 128 * cc:128 * (cc + 1)],
                                       ident[:]),
                           waits=[m_numz, m_tpc] if c == 0 else [m_tpc], inc="t")
                    m_t1 = p.mark("t")
                    m_tpc = p.emit("vector", lambda e, cc=c:
                                   e.tensor_copy(numz_fm[:, 128 * cc:128 * (cc + 1)],
                                                 ptp[:, :]),
                                   waits=[m_t1], inc="v")
                marks[("ptp_free",)] = m_tpc
                for half in range(2):
                    bank = psA if half == 0 else psB
                    for c in range(8):
                        w0 = [m_tpc, m_cmb.get(("proj", half))] if c == 0 else []
                        p.emit("tensor", lambda e, cc=c, h3=half, bk=bank:
                               e.matmul(bk[:, :],
                                        numz_fm[:, 128 * cc:128 * (cc + 1)],
                                        wp2[:, 1024 * cc + 512 * h3:1024 * cc + 512 * (h3 + 1)],
                                        start=(cc == 0), stop=False),
                               waits=w0, inc="t")
                    p.emit("tensor", lambda e, h3=half, bk=bank:
                           e.matmul(bk[:, :], ones1b[:, :],
                                    bproj_b[0:1, 512 * h3:512 * (h3 + 1)],
                                    start=False, stop=True),
                           waits=[], inc="t")
                m_proj = p.mark("t")
                parc = tc % 2
                parx = tc % 2
                mk1, mk2 = ld_marks[rc]
                for half in range(2):
                    bank = psA if half == 0 else psB
                    p.emit("vector", lambda e, h3=half, bk=bank, gc=g0c[parc]:
                           e.tensor_tensor(cmb[:, 512 * h3:512 * (h3 + 1)],
                                           gc[:, 512 * h3:512 * (h3 + 1)],
                                           bk[:, :], ALU.mult),
                           waits=[m_proj, mk2] + p.after("v"), inc="v")
                    p.emit("vector", lambda e, h3=half, xc=x0c[parc], xt=x1t[parx]:
                           e.tensor_tensor(xt[:, 512 * h3:512 * (h3 + 1)],
                                           cmb[:, 512 * h3:512 * (h3 + 1)],
                                           xc[:, 512 * h3:512 * (h3 + 1)], ALU.add),
                           waits=[mk1, marks.get(("x1st", parx))] + p.after("v"),
                           inc="v")
                m_x1 = p.mark("v")
                m_cmb[parc] = m_x1
                m_cmb[("proj", 0)] = m_x1
                m_cmb[("proj", 1)] = m_x1
                emit_xg_load(j, tc + 2)
                marks["x1"][rc] = p.emit(
                    "sync", lambda e, xt=x1t[parx], r2=rc:
                    e.dma_start(x1_d[128 * r2:128 * (r2 + 1), :], xt[:]),
                    waits=[m_x1], inc=f"db{parx}", k=16)
                marks[("x1st", parx)] = marks["x1"][rc]
                m_r, m_xc = ln_stats(p, x1t[parx][:, :], bn12, mv_t, rstd_t,
                                     xc4[:, :], pre=[m_x1] + p.after("s"),
                                     xc_pre=p.after("s"))
                p.emit("scalar", lambda e:
                       e.activation(xn1t[:], xc4[:], AF.Identity,
                                    scale=rstd_t[:, 0:1]),
                       waits=[m_r, m_xc, marks.get(("xn1t_used",))], inc="s")
                m_xn1 = p.mark("s")
                m_tpc2 = marks.get(("ptp_free",))
                for c in range(8):
                    p.emit("tensor", lambda e, cc=c:
                           e.transpose(ptp[:, :], xn1t[:, 128 * cc:128 * (cc + 1)],
                                       ident[:]),
                           waits=[m_xn1, m_tpc2] if c == 0 else [m_tpc2], inc="t")
                    m_t1 = p.mark("t")
                    m_tpc2 = p.emit("vector", lambda e, cc=c:
                                    e.tensor_copy(xn1f[:, 128 * cc:128 * (cc + 1)],
                                                  ptp[:, :]),
                                    waits=[m_t1], inc="v")
                marks[("ptp_free",)] = m_tpc2
                for c in range(8):
                    w0 = [m_tpc2, marks.get(("pk_used",))] if c == 0 else []
                    p.emit("tensor", lambda e, cc=c:
                           e.matmul(pk[:, :], xn1f[:, 128 * cc:128 * (cc + 1)],
                                    wKs[:, 128 * cc:128 * (cc + 1)],
                                    start=(cc == 0), stop=(cc == 7)),
                           waits=w0, inc="t")
                m_k1 = p.mark("t")
                p.emit("vector", lambda e: e.tensor_copy(k1sb[:], pk[:, :]),
                       waits=[m_k1], inc="v")
                m_k1v = p.mark("v")
                for half in range(2):
                    stf = (j == 0 and tc == 0)
                    spf = (j == 7 and tc == 3)
                    p.emit("tensor", lambda e, h3=half, st=stf, sp=spf:
                           e.matmul(M_acc[h3][:, :], k1sb[:],
                                    xn1t[:, 512 * h3:512 * (h3 + 1)],
                                    start=st, stop=sp),
                           waits=[m_k1v] if half == 0 else [], inc="t")
                for c in range(8):
                    w0 = [m_k1v] if c == 0 else []
                    p.emit("tensor", lambda e, cc=c:
                           e.matmul(pk[:, :], wQs[:, 128 * cc:128 * (cc + 1)],
                                    xn1f[:, 128 * cc:128 * (cc + 1)],
                                    start=(cc == 0), stop=(cc == 7)),
                           waits=w0, inc="t")
                m_q1 = p.mark("t")
                marks[("xn1t_used",)] = m_q1
                mkq = p.emit("vector", lambda e, r2=rc:
                             e.tensor_copy(Q1_sb[:, 128 * r2:128 * (r2 + 1)],
                                           pk[:, :]),
                             waits=[m_q1], inc="v")
                marks[("pk_used",)] = mkq
                marks[("q_used",)] = m_q1 if tc == 3 else marks.get(("q_used",))
            marks[("xnf4_used",)] = p.mark("t")
            marks[("xnf4_used", par)] = marks[("xnf4_used",)]
        m_tfin = p.mark("t")
        for half in range(2):
            p.emit("vector", lambda e, h3=half:
                   e.tensor_copy(Msb[:, 512 * h3:512 * (h3 + 1)],
                                 M_acc[h3][:, :]),
                   waits=[m_tfin], inc="v")
        marks["Msb"] = p.mark("v")
        p.flush()


def _p5a(nc, p, en, marks, L):
    x1_d, w1q_d, xq2_d, cw_d, cb_d = (
        L["x1_d"], L["w1q_d"], L["xq2_d"], L["cw_d"], L["cb_d"])
    ident, identf, ones1f, m_bc, recs2_sb = (
        L["ident"], L["identf"], L["ones1f"], L["m_bc"], L["recs2_sb"])
    bn12, mv_t, rstd_t = L["bn12"], L["mv_t"], L["rstd_t"]
    with ExitStack() as px:
        pe = px.enter_context
        w1q = pe(nc.sbuf_tensor("w1q_sb", [128, 8 * F1], BF16))
        cwsb = pe(nc.sbuf_tensor("cwsb", [128, 96], F32))
        cbsb = pe(nc.sbuf_tensor("cbsb", [128, 32], F32))
        x1c5 = [pe(nc.sbuf_tensor(f"x1c5{i}", [128, D], F32)) for i in range(2)]
        xc5 = pe(nc.sbuf_tensor("xc5", [128, D], F32))
        xn5 = pe(nc.sbuf_tensor("xn5", [128, D], F32))
        xqt = pe(nc.sbuf_tensor("xqt", [128, D], BF16))
        rtt = pe(nc.sbuf_tensor("rtt", [128, 1024], F32))
        rtr = pe(nc.sbuf_tensor("rtr", [128, 512], F32))
        rgate = pe(nc.sbuf_tensor("rgate", [128, 1024], I32))
        xqf = pe(nc.sbuf_tensor("xqf", [128, 8 * 512], BF16))
        h5 = pe(nc.sbuf_tensor("h5", [128, 32 * 513], F32))
        hs5 = pe(nc.sbuf_tensor("hs5", [128, 32], F32))
        recs5 = pe(nc.sbuf_tensor("recs5", [128, 4], F32))
        amx_c = pe(nc.sbuf_tensor("amx_c", [128, 1], F32))
        s127 = pe(nc.sbuf_tensor("s127", [128, 1], I32))
        a2t = pe(nc.sbuf_tensor("a2t", [128, 1], F32))
        s2c = pe(nc.sbuf_tensor("s2c", [128, 1], F32))
        recs_row = pe(nc.sbuf_tensor("recs_row", [1, 512], F32))
        s2row = pe(nc.sbuf_tensor("s2row", [1, 512], F32))
        rbc = pe(nc.sbuf_tensor("rbc", [128, 512], I32))
        cv1 = pe(nc.sbuf_tensor("cv1", [128, 1024], F32))
        cv2 = pe(nc.sbuf_tensor("cv2", [128, 512], F32))
        mx5 = pe(nc.sbuf_tensor("mx5", [128, 512], F32))
        mn5 = pe(nc.sbuf_tensor("mn5", [128, 512], F32))
        xq2t = [pe(nc.sbuf_tensor(f"xq2t{i}", [128, 1024], BF16)) for i in range(2)]
        ph5 = [pe(nc.psum_tensor(f"ph5{i}", [128, 512], F32)) for i in range(2)]
        ptp5 = pe(nc.psum_tensor("ptp5", [128, 128], BF16))
        ptp5f = pe(nc.psum_tensor("ptp5f", [128, 128], F32))
        pbc5 = pe(nc.psum_tensor("pbc5", [128, 512], F32))

        for c in range(8):
            p.emit("sync", lambda e, cc=c:
                   e.dma_start(w1q[:, F1 * cc:F1 * (cc + 1)], w1q_d[cc]),
                   inc="dc0", k=16)
        m_w1q = p.mark("dc0")
        p.emit("sync", lambda e: e.dma_start(cwsb[:], cw_d), inc="d", k=16)
        p.emit("sync", lambda e: e.dma_start(cbsb[:], cb_d), inc="d", k=16)
        m_cw = p.mark("d")
        p.emit("vector", lambda e: e.memset(recs_row[:], 0.0), inc="v")
        p.emit("vector", lambda e: e.memset(s2row[:], 0.0), inc="v")
        p.emit("vector", lambda e: e.memset(xq2t[0][:], 0.0), inc="v")
        p.emit("vector", lambda e: e.memset(cv1[:], 0.0), inc="v")
        p.emit("vector", lambda e: e.memset(xq2t[1][:], 0.0), inc="v")
        marks["xq2st"] = [None] * NWIN

        for w in range(NWIN):
            o0 = WS * w
            ow = min(WS * (w + 1), N) - o0
            hn = min(512, N - o0)
            nck = (hn + 127) // 128
            p.emit("vector", lambda e: e.memset(mx5[:], -3.0e38),
                   waits=[marks.get(("mx_used",))], inc="v")
            p.emit("vector", lambda e: e.memset(mn5[:], 3.0e38),
                   waits=[marks.get(("mn_used",))], inc="v")
            m_mninit = p.mark("v")
            m_xqf = None
            for k2 in range(nck):
                rows = min(128, hn - 128 * k2)
                par = k2 % 2
                mk_ld = p.emit(
                    "sync", lambda e, b=x1c5[par], r0=o0 + 128 * k2, rr=rows:
                    e.dma_start(b[0:rr, :], x1_d[r0:r0 + rr, :]),
                    waits=[marks.get(("x1c5_used", par))], inc=f"da{par}", k=16)
                m_r, m_xc = ln_stats(p, x1c5[par][0:rows, :], bn12, mv_t,
                                     rstd_t, xc5[0:rows, :], rows=rows,
                                     pre=[mk_ld] + p.after("s", "v"),
                                     xc_pre=p.after("s"))
                p.emit("vector", lambda e, rr=rows:
                       e.reduce_max(amx_c[0:rr, :], xc5[0:rr, :], AX.X,
                                    apply_absolute_value=True),
                       waits=[m_xc] + p.after("v"), inc="v", ks=True)
                p.emit("vector", lambda e, rr=rows:
                       e.tensor_tensor(amx_c[0:rr, :], amx_c[0:rr, :],
                                       rstd_t[0:rr, :], ALU.mult),
                       waits=[m_r] + p.after("v"), inc="v", ks=True)
                p.emit("vector", lambda e, rr=rows:
                       e.tensor_single_scalar(amx_c[0:rr, :], amx_c[0:rr, :],
                                              1e-5, ALU.max),
                       waits=p.after("v"), inc="v", ks=True)
                p.emit("vector", lambda e, rr=rows:
                       e.reciprocal(s127[0:rr, :].bitcast(F32),
                                    amx_c[0:rr, :]),
                       waits=p.after("v"), inc="v", ks=True)
                p.emit("vector", lambda e, rr=rows:
                       e.tensor_single_scalar(s127[0:rr, :].bitcast(F32),
                                              s127[0:rr, :].bitcast(F32),
                                              127.0, ALU.mult),
                       waits=p.after("v"), inc="v", ks=True)
                p.emit("vector", lambda e, rr=rows, kk=k2:
                       e.tensor_single_scalar(recs5[0:rr, kk:kk + 1],
                                              amx_c[0:rr, :], INV127, ALU.mult),
                       waits=p.after("v"), inc="v", ks=True)
                m_recs = p.mark("v")
                marks[("x1c5_used", par)] = m_recs
                p.emit("scalar", lambda e, rr=rows:
                       e.activation(xn5[0:rr, :], xc5[0:rr, :], AF.Identity,
                                    scale=rstd_t[0:rr, 0:1]),
                       waits=[m_r, m_xc, marks.get(("xn5_used",))] + p.after("s"),
                       inc="s")
                m_xn5 = p.mark("s")
                p.emit("vector", lambda e, rr=rows:
                       e.tensor_scalar(cv1[0:rr, 0:D],
                                       xn5[0:rr, 0:D],
                                       s127[0:rr, 0:1].bitcast(F32),
                                       None, ALU.mult),
                       waits=[m_xn5, marks.get(("xqt_used",)),
                              marks.get(("cv1_used",))] + p.after("v"),
                       inc="v", ks=True)
                mk_rnd = round_wide(
                    p, xqt[0:rows, 0:D], cv1[0:rows, 0:D],
                    rtt[0:rows, 0:D], xn5[0:rows, 0:D],
                    rgate[0:rows, 0:D].bitcast(F32), K=1,
                    pre=p.after("v"), use_s="neg")
                marks[("cv1_used",)] = mk_rnd
                marks[("xn5_used",)] = mk_rnd
                for c in range(8):
                    p.emit("tensor", lambda e, cc=c, rr=rows:
                           e.transpose(ptp5[:, 0:rr],
                                       xqt[0:rr, 128 * cc:128 * (cc + 1)],
                                       ident[0:rr, 0:rr]),
                           waits=[mk_rnd, marks.get(("ptp5_free",)),
                                  marks.get(("xqf_used",))], inc="t")
                    m_t1 = p.mark("t")
                    mk_cp = p.emit(
                        "vector", lambda e, cc=c, rr=rows, kk=k2:
                        e.tensor_copy(xqf[:, 512 * cc + 128 * kk:512 * cc + 128 * kk + rr],
                                      ptp5[:, 0:rr]),
                        waits=[m_t1], inc="v")
                    marks[("ptp5_free",)] = mk_cp
                marks[("xqt_used",)] = p.mark("t")
                m_xqf = marks[("ptp5_free",)]
                p.emit("tensor", lambda e, rr=rows, kk=k2:
                       e.transpose(ptp5f[0:1, 0:rr], recs5[0:rr, kk:kk + 1],
                                   identf[0:rr, 0:rr]),
                       waits=[m_recs, marks.get(("ptp5f_free",))], inc="t")
                m_t2 = p.mark("t")
                marks[("ptp5f_free",)] = p.emit(
                    "vector", lambda e, rr=rows, kk=k2:
                    e.tensor_copy(recs_row[0:1, 128 * kk:128 * kk + rr],
                                  ptp5f[0:1, 0:rr]),
                    waits=[m_t2], inc="v")
            p.emit("tensor", lambda e:
                   e.matmul(pbc5[:, :], ones1f[:, :], recs_row[0:1, :],
                            start=True, stop=True),
                   waits=[marks[("ptp5f_free",)], marks.get(("pbc5_free",))] +
                         p.after("v"), inc="t")
            m_t3 = p.mark("t")
            m_rbc = p.emit("vector", lambda e:
                           e.tensor_copy(rbc[:].bitcast(F32), pbc5[:, :]),
                           waits=[m_t3] + p.after("p"), inc="v")
            marks[("pbc5_free",)] = m_rbc
            mhv5 = {}
            mcv1_5 = {}
            msilu5 = {}

            def conv_A(m):
                mp = m % 2
                for c in range(8):
                    w0 = []
                    if c == 0:
                        w0 = [m_xqf, m_w1q, marks.get(("ph_used", mp))]
                    p.emit("tensor", lambda e, cc=c, mm2=m, hn2=hn, bk=ph5[mp]:
                           e.matmul(bk[:, 0:hn2],
                                    w1q[:, F1 * cc + 128 * mm2:F1 * cc + 128 * (mm2 + 1)],
                                    xqf[:, 512 * cc:512 * cc + hn2],
                                    start=(cc == 0), stop=(cc == 7)),
                           waits=w0, inc="t")
                m_mm = p.mark("t")
                p.emit("vector", lambda e, mm2=m, hn2=hn, bk=ph5[mp]:
                       e.scalar_tensor_tensor(
                           h5[:, 513 * mm2 + 1:513 * mm2 + 1 + hn2],
                           bk[:, 0:hn2], m_bc[:, 0:1],
                           rbc[:, 0:hn2].bitcast(F32),
                           ALU.mult, ALU.mult),
                       waits=[m_mm, m_rbc, marks.get(("h5q", m))] + p.after("v"),
                       inc="v")
                marks[("ph_used", mp)] = p.mark("v")
                if w == 0:
                    p.emit("vector", lambda e, mm2=m:
                           e.memset(h5[:, 513 * mm2:513 * mm2 + 1], 0.0),
                           waits=p.after("v"), inc="v")
                else:
                    p.emit("vector", lambda e, mm2=m:
                           e.tensor_copy(h5[:, 513 * mm2:513 * mm2 + 1],
                                         hs5[:, mm2:mm2 + 1]),
                           waits=p.after("v"), inc="v", ks=True)
                if w == NWIN - 1:
                    p.emit("vector", lambda e, mm2=m:
                           e.memset(h5[:, 513 * mm2 + 17:513 * mm2 + 18], 0.0),
                           waits=p.after("v"), inc="v")
                else:
                    p.emit("vector", lambda e, mm2=m:
                           e.tensor_copy(hs5[:, mm2:mm2 + 1],
                                         h5[:, 513 * mm2 + 510:513 * mm2 + 511]),
                           waits=p.after("v"), inc="v")
                mhv5[m] = p.mark("v")
                p.emit("scalar", lambda e, mm2=m, ow2=ow:
                       e.activation(cv1[:, 0:ow2],
                                    h5[:, 513 * mm2 + 1:513 * mm2 + 1 + ow2],
                                    AF.Identity,
                                    scale=cwsb[:, 3 * mm2 + 1:3 * mm2 + 2],
                                    bias=cbsb[:, mm2:mm2 + 1]),
                       waits=[mhv5[m], m_cw, marks.get(("cv1_used",))] +
                             p.after("s"), inc="s")
                mcv1_5[m] = p.mark("s")

            def conv_B(m):
                p.emit("vector", lambda e, mm2=m, ow2=ow:
                       e.scalar_tensor_tensor(cv2[:, 0:ow2],
                                              h5[:, 513 * mm2:513 * mm2 + ow2],
                                              cwsb[:, 3 * mm2:3 * mm2 + 1],
                                              cv1[:, 0:ow2], ALU.mult, ALU.add),
                       waits=[mcv1_5[m], mhv5[m], marks.get(("cv2_used",))] +
                             p.after("v"), inc="v")
                m_cv2 = p.mark("v")
                marks[("cv1_used",)] = m_cv2
                p.emit("vector", lambda e, mm2=m, ow2=ow:
                       e.scalar_tensor_tensor(h5[:, 513 * mm2:513 * mm2 + ow2],
                                              h5[:, 513 * mm2 + 2:513 * mm2 + 2 + ow2],
                                              cwsb[:, 3 * mm2 + 2:3 * mm2 + 3],
                                              cv2[:, 0:ow2], ALU.mult, ALU.add),
                       waits=[m_cv2] + p.after("v"), inc="v")
                m_cnv = p.mark("v")
                marks[("cv2_used",)] = m_cnv
                p.emit("scalar", lambda e, mm2=m, ow2=ow:
                       e.activation(h5[:, 513 * mm2:513 * mm2 + ow2],
                                    h5[:, 513 * mm2:513 * mm2 + ow2], AF.Silu),
                       waits=[m_cnv] + p.after("s"), inc="s")
                msilu5[m] = p.mark("s")

            def conv_C(m):
                p.emit("vector", lambda e, mm2=m, ow2=ow:
                       e.tensor_tensor(mx5[:, 0:ow2], mx5[:, 0:ow2],
                                       h5[:, 513 * mm2:513 * mm2 + ow2],
                                       ALU.max),
                       waits=[msilu5[m]] + p.after("v"), inc="v")
                p.emit("vector", lambda e, mm2=m, ow2=ow:
                       e.tensor_tensor(mn5[:, 0:ow2], mn5[:, 0:ow2],
                                       h5[:, 513 * mm2:513 * mm2 + ow2],
                                       ALU.min),
                       waits=[msilu5[m]] + p.after("v"), inc="v")

            for m in range(34):
                if 1 <= m < 33:
                    conv_B(m - 1)
                if m < 32:
                    conv_A(m)
                if m >= 2:
                    conv_C(m - 2)
            marks[("xqf_used",)] = p.mark("t")
            m_mins = p.mark("p")
            p.emit("vector", lambda e:
                   e.tensor_single_scalar(mn5[:], mn5[:], -1.0, ALU.mult),
                   waits=[m_mins] + p.after("v"), inc="v")
            p.emit("vector", lambda e:
                   e.tensor_tensor(mx5[:], mx5[:], mn5[:], ALU.max),
                   waits=p.after("v"), inc="v")
            m_amax = p.mark("v")
            marks[("mn_used",)] = m_amax
            for k2 in range((ow + 127) // 128):
                koff = 128 * k2
                rk = min(128, ow - koff)
                p.emit("tensor", lambda e, ko=koff, rr=rk:
                       e.transpose(ptp5f[0:rr, :], mx5[:, ko:ko + rr], identf[:]),
                       waits=[m_amax, marks.get(("ptp5f_free",))], inc="t")
                m_t4 = p.mark("t")
                p.emit("vector", lambda e, rr=rk:
                       e.reduce_max(a2t[0:rr, :], ptp5f[0:rr, :], AX.X),
                       waits=[m_t4] + p.after("v"), inc="v", ks=True)
                marks[("ptp5f_free",)] = p.mark("v")
                p.emit("vector", lambda e, rr=rk:
                       e.tensor_single_scalar(a2t[0:rr, :], a2t[0:rr, :],
                                              1e-5, ALU.max),
                       waits=p.after("v"), inc="v", ks=True)
                p.emit("vector", lambda e, rr=rk:
                       e.reciprocal(s2c[0:rr, :], a2t[0:rr, :]),
                       waits=p.after("v"), inc="v", ks=True)
                p.emit("vector", lambda e, rr=rk:
                       e.tensor_single_scalar(s2c[0:rr, :], s2c[0:rr, :],
                                              127.0, ALU.mult),
                       waits=p.after("v"), inc="v", ks=True)
                p.emit("vector", lambda e, rr=rk, w2=w, kk=k2:
                       e.tensor_single_scalar(recs2_sb[0:rr, 4 * w2 + kk:4 * w2 + kk + 1],
                                              a2t[0:rr, :], INV127, ALU.mult),
                       waits=p.after("v"), inc="v", ks=True)
                m_s2 = p.mark("v")
                p.emit("tensor", lambda e, rr=rk:
                       e.transpose(ptp5f[0:1, 0:rr], s2c[0:rr, 0:1],
                                   identf[0:rr, 0:rr]),
                       waits=[m_s2, marks.get(("ptp5f_free",))], inc="t")
                m_t5 = p.mark("t")
                marks[("ptp5f_free",)] = p.emit(
                    "vector", lambda e, ko=koff, rr=rk:
                    e.tensor_copy(s2row[0:1, ko:ko + rr], ptp5f[0:1, 0:rr]),
                    waits=[m_t5], inc="v")
            p.emit("tensor", lambda e:
                   e.matmul(pbc5[:, :], ones1f[:, :], s2row[0:1, :],
                            start=True, stop=True),
                   waits=[marks[("ptp5f_free",)], marks.get(("pbc5_free",))] +
                         p.after("v"), inc="t")
            m_t6 = p.mark("t")
            m_s2bc = p.emit("vector", lambda e:
                            e.tensor_copy(rbc[:].bitcast(F32), pbc5[:, :]),
                            waits=[m_t6] + p.after("p"), inc="v")
            marks[("pbc5_free",)] = m_s2bc
            marks[("mx_used",)] = m_t6
            for mp in range(16):
                par = mp % 2
                m0, m1 = 2 * mp, 2 * mp + 1
                for sub, mm in ((0, m0), (1, m1)):
                    p.emit("vector", lambda e, mm2=mm, ow2=ow, s4=sub:
                           e.tensor_tensor(cv1[:, 512 * s4:512 * s4 + ow2],
                                           h5[:, 513 * mm2:513 * mm2 + ow2],
                                           rbc[:, 0:ow2].bitcast(F32),
                                           ALU.mult),
                           waits=[m_s2bc, marks.get(("xq2st", par))] +
                                 p.after("v"), inc="v")
                    marks[("h5q", mm)] = p.mark("v")
                wid = 512 + ow
                mk_r4 = round_wide(p, xq2t[par][:, 0:wid], cv1[:, 0:wid],
                                   rtt[:, 0:wid], xn5[:, 0:wid],
                                   rgate[:, 0:wid].bitcast(F32), K=2,
                                   pre=p.after("v"), eng="vector", use_s=True)
                marks[("xq2st", par)] = p.emit(
                    "sync", lambda e, mm2=m0, w2=w, b=xq2t[par]:
                    e.dma_start(xq2_d[w2, :, 512 * mm2:512 * (mm2 + 2)], b[:]),
                    waits=[mk_r4], inc=f"db{par}", k=16)
                marks[("xn5_used",)] = mk_r4
            marks["xq2st"][w] = (marks[("xq2st", 0)], marks[("xq2st", 1)])
        p.flush()


def _p5b(nc, p, en, marks, L):
    xq2_d, w2q_d, x1_d, out_d = L["xq2_d"], L["w2q_d"], L["x1_d"], L["out_d"]
    Q1_sb, Msb, gam_bc, bet_bc, m_bc, recs2_sb = (
        L["Q1_sb"], L["Msb"], L["gam_bc"], L["bet_bc"], L["m_bc"],
        L["recs2_sb"])
    bn12, mv_t, rstd_t = L["bn12"], L["mv_t"], L["rstd_t"]
    with ExitStack() as px:
        pe = px.enter_context
        w2q = pe(nc.sbuf_tensor("w2q_sb", [128, 32 * 1024], BF16))
        xq2s = [pe(nc.sbuf_tensor(f"xq2s{i}", [128, 32 * 512], BF16)) for i in range(2)]
        x1c6 = [pe(nc.sbuf_tensor(f"x1c6{i}", [128, D], F32)) for i in range(2)]
        g1t = [pe(nc.sbuf_tensor(f"g1t{i}", [128, D], F32)) for i in range(2)]
        xc6 = pe(nc.sbuf_tensor("xc6", [128, D], F32))
        xn6 = pe(nc.sbuf_tensor("xn6", [128, D], F32))
        Jn6 = pe(nc.sbuf_tensor("Jn6", [128, D], F32))
        tn6 = pe(nc.sbuf_tensor("tn6", [128, D], F32))
        s1t = pe(nc.sbuf_tensor("s1t", [128, D], F32))
        ot6 = [pe(nc.sbuf_tensor(f"ot6{i}", [128, D], F32)) for i in range(2)]
        sc2 = [pe(nc.sbuf_tensor(f"sc2{i}", [128, 1], F32)) for i in range(2)]
        po6 = [pe(nc.psum_tensor(f"po6{i}", [128, 512], F32)) for i in range(2)]
        pj6 = [pe(nc.psum_tensor(f"pj6{i}", [128, 512], F32)) for i in range(2)]

        for c in range(32):
            p.emit("sync", lambda e, cc=c:
                   e.dma_start(w2q[:, 1024 * cc:1024 * (cc + 1)], w2q_d[cc]),
                   inc="dc0", k=16)
        m_w2q = p.mark("dc0")

        for w in range(NWIN):
            o0 = WS * w
            ow = min(WS * (w + 1), N) - o0
            wpar = w % 2
            mk_x = p.emit("sync", lambda e, w2=w, b=xq2s[wpar]:
                          e.dma_start(b[:], xq2_d[w2]),
                          waits=[marks.get(("xq2s_used", wpar))],
                          inc=f"da{wpar}", k=16)
            for k2 in range((ow + 127) // 128):
                t0 = o0 + 128 * k2
                rk = min(128, ow - 128 * k2)
                par = k2 % 2
                mk_l = p.emit("sync", lambda e, b=x1c6[par], t1=t0, rr=rk:
                              e.dma_start(b[0:rr, :], x1_d[t1:t1 + rr, :]),
                              waits=[marks.get(("x1c6_used", par))],
                              inc=f"dd{par}", k=16)
                p.emit("vector", lambda e, rr=rk, w2=w, kk=k2, s2=sc2[par]:
                       e.tensor_tensor(s2[0:rr, :],
                                       recs2_sb[0:rr, 4 * w2 + kk:4 * w2 + kk + 1],
                                       m_bc[0:rr, 2:3], ALU.mult),
                       waits=[marks.get(("sc2_used", par))] + p.after("v"),
                       inc="v", ks=True)
                m_sc2 = p.mark("v")
                for half in range(2):
                    for c in range(32):
                        w0 = []
                        if c == 0:
                            w0 = [mk_x, m_w2q, marks.get(("po6_used", half))]
                        p.emit("tensor",
                               lambda e, cc=c, kk=k2, h3=half, rr=rk,
                               bk=po6[half], b=xq2s[wpar]:
                               e.matmul(bk[0:rr, :],
                                        b[:, 512 * cc + 128 * kk:512 * cc + 128 * kk + rr],
                                        w2q[:, 1024 * cc + 512 * h3:1024 * cc + 512 * (h3 + 1)],
                                        start=(cc == 0), stop=(cc == 31)),
                               waits=w0, inc="t")
                    m_o = p.mark("t")
                    p.emit("scalar", lambda e, h3=half, rr=rk, bk=po6[half],
                           g1=g1t[par], s2=sc2[par]:
                           e.activation(g1[0:rr, 512 * h3:512 * (h3 + 1)],
                                        bk[0:rr, :], AF.Silu,
                                        scale=s2[0:rr, 0:1]),
                           waits=[m_o, m_sc2, marks.get(("g1_used", par))] +
                                 p.after("s"), inc="s")
                    marks[("po6_used", half)] = p.mark("s")
                m_g1 = p.mark("s")
                marks[("sc2_used", par)] = m_g1
                marks[("xq2s_used", wpar)] = p.mark("t")
                m_r, m_xc = ln_stats(p, x1c6[par][0:rk, :], bn12, mv_t, rstd_t,
                                     xc6[0:rk, :], rows=rk,
                                     pre=[mk_l] + p.after("s", "v"),
                                     xc_pre=p.after("s"))
                p.emit("scalar", lambda e, rr=rk:
                       e.activation(xn6[0:rr, :], xc6[0:rr, :], AF.Identity,
                                    scale=rstd_t[0:rr, 0:1]),
                       waits=[m_r, m_xc, marks.get(("xn6_used",))] + p.after("s"),
                       inc="s")
                m_xn6 = p.mark("s")
                for half in range(2):
                    p.emit("tensor", lambda e, h3=half, t1=t0, rr=rk, bk=pj6[half]:
                           e.matmul(bk[0:rr, :], Q1_sb[:, t1:t1 + rr],
                                    Msb[:, 512 * h3:512 * (h3 + 1)],
                                    start=True, stop=True),
                           waits=[marks.get(("pj6_used", half))], inc="t")
                    m_j = p.mark("t")
                    p.emit("vector", lambda e, h3=half, rr=rk, bk=pj6[half]:
                           e.scalar_tensor_tensor(
                               Jn6[0:rr, 512 * h3:512 * (h3 + 1)],
                               bk[0:rr, :], ISQ_DK,
                               xn6[0:rr, 512 * h3:512 * (h3 + 1)],
                               ALU.mult, ALU.add),
                           waits=[m_j, m_xn6] + p.after("v"), inc="v")
                    marks[("pj6_used", half)] = p.mark("v")
                marks[("xn6_used",)] = p.mark("v")
                m_jn = p.mark("v")
                m_r2, m_xc2 = ln_stats(p, Jn6[0:rk, :], bn12, mv_t, rstd_t,
                                       xc6[0:rk, :], rows=rk,
                                       pre=[m_jn] + p.after("s"),
                                       xc_pre=p.after("s"))
                p.emit("scalar", lambda e, rr=rk:
                       e.activation(tn6[0:rr, :], xc6[0:rr, :], AF.Identity,
                                    scale=rstd_t[0:rr, 0:1]),
                       waits=[m_r2, m_xc2, marks.get(("tn6_used",))] +
                             p.after("s"), inc="s")
                m_tn = p.mark("s")
                p.emit("vector", lambda e, rr=rk:
                       e.tensor_tensor(s1t[0:rr, :], tn6[0:rr, :],
                                       gam_bc[0:rr, :], ALU.mult),
                       waits=[m_tn, marks["gb_bc"]] + p.after("v"), inc="v")
                marks[("tn6_used",)] = p.mark("v")
                p.emit("vector", lambda e, rr=rk:
                       e.tensor_tensor(s1t[0:rr, :], s1t[0:rr, :],
                                       bet_bc[0:rr, :], ALU.add),
                       waits=p.after("v"), inc="v")
                p.emit("vector", lambda e, rr=rk, g1=g1t[par]:
                       e.tensor_tensor(s1t[0:rr, :], s1t[0:rr, :],
                                       g1[0:rr, :], ALU.mult),
                       waits=[m_g1] + p.after("v"), inc="v")
                marks[("g1_used", par)] = p.mark("v")
                p.emit("vector", lambda e, rr=rk, b=x1c6[par], o=ot6[par]:
                       e.tensor_tensor(o[0:rr, :], s1t[0:rr, :], b[0:rr, :],
                                       ALU.add),
                       waits=[marks.get(("ot_used", par))] + p.after("v"),
                       inc="v")
                m_out = p.mark("v")
                marks[("x1c6_used", par)] = m_out
                mk_st = p.emit("sync", lambda e, t1=t0, rr=rk, o=ot6[par]:
                               e.dma_start(out_d[t1:t1 + rr, :], o[0:rr, :]),
                               waits=[m_out], inc=f"de{par}", k=16)
                marks[("ot_used", par)] = mk_st
        p.flush()


def _prep_inputs(inputs):
    f32 = np.float32
    W_qkv = np.asarray(inputs["L0_attn_W_qkv"], f32)
    base = {
        "wqT": np.ascontiguousarray(W_qkv[0:D, :].T),
        "wkT": np.ascontiguousarray(W_qkv[D:2 * D, :].T),
        "wvT": np.ascontiguousarray(W_qkv[2 * D:3 * D, :].T),
        "wpT": np.ascontiguousarray(np.asarray(inputs["L0_attn_W_proj"], f32).T),
        "bproj": np.ascontiguousarray(
            np.asarray(inputs["L0_attn_b_proj"], f32).reshape(1, D)),
        "wgT": np.ascontiguousarray(np.asarray(inputs["L0_ffn_w_gate"]).T),
        "wuT": np.ascontiguousarray(np.asarray(inputs["L0_ffn_w_up"]).T),
        "wdT": np.ascontiguousarray(np.asarray(inputs["L0_ffn_w_down"]).T),
        "wQT": np.ascontiguousarray(np.asarray(inputs["L1_attn_W_Q"], f32).T),
        "wKT": np.ascontiguousarray(np.asarray(inputs["L1_attn_W_K"], f32).T),
        "gam": np.ascontiguousarray(
            np.asarray(inputs["L1_attn_gamma"], f32).reshape(1, D)),
        "bet": np.ascontiguousarray(
            np.asarray(inputs["L1_attn_beta"], f32).reshape(1, D)),
        "w1T": np.ascontiguousarray(np.asarray(inputs["L1_ffn_w1"], f32).T),
        "cw": np.ascontiguousarray(
            np.asarray(inputs["L1_ffn_conv_w"], f32).reshape(32, 128, 3)
            .transpose(1, 0, 2).reshape(128, 96)),
        "cb": np.ascontiguousarray(
            np.asarray(inputs["L1_ffn_conv_b"], f32).reshape(32, 128).T),
        "w2T": np.ascontiguousarray(np.asarray(inputs["L1_ffn_w2"], f32).T),
    }
    x_all = np.asarray(inputs["L0_attn_x"], f32)
    return base, x_all


def kernel(**inputs):
    base, x_all = _prep_inputs(inputs)
    nc = build_program()
    in_maps = []
    for b in range(8):
        m = dict(base)
        m["x"] = np.ascontiguousarray(x_all[b])
        in_maps.append(m)
    res = run_bass_kernel_spmd(nc, in_maps, core_ids=list(range(8)))
    out = np.stack([np.asarray(r["out"], dtype=np.float32)
                    for r in res.results], axis=0)
    return out


# version 12
# speedup vs baseline: 1.2996x; 1.0098x over previous
from collections import defaultdict
from contextlib import ExitStack

import numpy as np
import ml_dtypes

import concourse.bass as bass
import concourse.mybir as mybir
from concourse.bass_utils import run_bass_kernel_spmd

F32 = mybir.dt.float32
BF16 = mybir.dt.bfloat16
I32 = mybir.dt.int32
AF = mybir.ActivationFunctionType
ALU = mybir.AluOpType
AX = mybir.AxisListType

MAGIC = 12582912.0
N = 4096
D = 1024
NH = 16
HD = 64
F0 = 3072
DK = 128
F1 = 4096
INV127 = 1.0 / 127.0
INV127SQ = (1.0 / 127.0) * (1.0 / 127.0)
ISQ_DK = 1.0 / float(np.sqrt(np.float64(DK)))

WS = 510
NWIN = 9

ENGS = ("sync", "tensor", "vector", "scalar", "gpsimd")


class Prog:

    def __init__(self, nc):
        self.nc = nc
        self.items = []
        self.cnt = defaultdict(int)
        self.h = {}
        names = ["d", "g", "t", "v", "s", "p"]
        for ab in "abcdefgh":
            names += [f"d{ab}0", f"d{ab}1"]
        for nm in names:
            self.h[nm] = nc.alloc_semaphore(name=f"sem_{nm}")
        self.waited = {e_: defaultdict(int) for e_ in ENGS}
        self._pending = None
        self.eps_ln = None

    def after(self, *sems):
        return [(nm, self.cnt[nm]) for nm in sems]

    def mark(self, sem):
        return (sem, self.cnt[sem])

    OWN = {"sync": None, "tensor": "t", "vector": "v", "scalar": "s",
           "gpsimd": "p"}
    DROP_SELF = False

    def emit(self, eng, fn, waits=(), inc=None, k=1, ks=False):
        if inc is not None:
            self.cnt[inc] += k
        ws = [w for w in waits if w is not None]
        if self.DROP_SELF and not ks:
            own = self.OWN[eng]
            ws = [w for w in ws if w[0] != own]
        self.items.append((eng, fn, ws, inc, k))
        return (inc, self.cnt[inc]) if inc is not None else None

    def flush(self, barrier_after=True):
        nc = self.nc
        items = self.items
        pend = self._pending or {}

        def runner(eng_name):
            def run(eng):
                extra = pend.get(eng_name)
                used_extra = [False]
                for it_eng, fn, waits, inc, k in items:
                    if it_eng != eng_name:
                        continue
                    ws = list(waits)
                    if extra is not None and not used_extra[0]:
                        ws = extra + ws
                        used_extra[0] = True
                    for nm, v in ws:
                        if v > 0 and self.waited[eng_name][nm] < v:
                            eng.wait_ge(self.h[nm], v)
                            self.waited[eng_name][nm] = v
                    bi = fn(eng)
                    if inc is not None:
                        assert bi is not None
                        bi.then_inc(self.h[inc], k)
            return run

        with nc.Block() as block:
            block.sync(runner("sync"))
            block.tensor(runner("tensor"))
            block.vector(runner("vector"))
            block.scalar(runner("scalar"))
            block.gpsimd(runner("gpsimd"))
        self.items = []
        if barrier_after:
            snap = [(nm, self.cnt[nm]) for nm in self.h if self.cnt[nm] > 0]
            self._pending = {e_: list(snap) for e_ in ENGS}
        else:
            self._pending = None


def round_emit(p, out_ap, in_ap, tt, tr, gatef, scale_ap=None, pre=(),
               eng="vector"):
    sem = {"vector": "v", "gpsimd": "p"}[eng]
    if scale_ap is not None:
        p.emit(eng, lambda e, o=tt, i=in_ap, s=scale_ap:
               e.tensor_scalar(o, i, s, 0.5, ALU.mult, ALU.subtract),
               waits=list(pre), inc=sem)
    else:
        p.emit(eng, lambda e, o=tt, i=in_ap:
               e.tensor_single_scalar(o, i, 0.5, ALU.subtract),
               waits=list(pre), inc=sem)
    p.emit(eng, lambda e, o=gatef, i=tt:
           e.tensor_scalar(o, i, MAGIC, MAGIC, ALU.add, ALU.subtract),
           waits=p.after(sem), inc=sem)
    p.emit(eng, lambda e, o=tr, i=tt, gg=gatef:
           e.tensor_tensor(o, i, gg, ALU.subtract),
           waits=p.after(sem), inc=sem)
    if eng == "vector":
        p.emit(eng, lambda e, o=out_ap, i=tr, gg=gatef:
               e.scalar_tensor_tensor(o, i, 0.0, gg, ALU.is_ge, ALU.add),
               waits=p.after(sem), inc=sem)
    else:
        p.emit(eng, lambda e, i=tr:
               e.tensor_single_scalar(i, i, 0.0, ALU.is_ge),
               waits=p.after(sem), inc=sem)
        p.emit(eng, lambda e, o=out_ap, i=tr, gg=gatef:
               e.tensor_tensor(o, i, gg, ALU.add),
               waits=p.after(sem), inc=sem)
    return p.mark(sem)


def round_wide(p, out_ap, y_ap, u_t, d_t, gatef, K=2, pre=(), eng="vector",
               use_s=False):
    sem = {"vector": "v", "gpsimd": "p"}[eng]
    if use_s is True:
        p.emit("scalar", lambda e, o=d_t, i=y_ap:
               e.activation(o, i, AF.Identity, bias=p.Mt[0:y_ap.shape[0], :]),
               waits=list(pre) + p.after("s"), inc="s", ks=True)
        p.emit("scalar", lambda e, o=gatef, i=d_t:
               e.activation(o, i, AF.Identity, bias=p.negMt[0:y_ap.shape[0], :]),
               waits=p.after("s"), inc="s", ks=True)
        m_g = p.mark("s")
        p.emit(eng, lambda e, o=d_t, i=y_ap, gg=gatef:
               e.tensor_tensor(o, i, gg, ALU.subtract),
               waits=[m_g] + p.after(sem), inc=sem)
    else:
        p.emit(eng, lambda e, o=gatef, i=y_ap:
               e.tensor_scalar(o, i, MAGIC, MAGIC, ALU.add, ALU.subtract),
               waits=list(pre), inc=sem)
        p.emit(eng, lambda e, o=d_t, i=y_ap, gg=gatef:
               e.tensor_tensor(o, i, gg, ALU.subtract),
               waits=p.after(sem), inc=sem)
    if eng == "vector":
        p.emit(eng, lambda e, o=u_t, i=d_t, gg=gatef:
               e.scalar_tensor_tensor(o, i, 0.5, gg, ALU.is_ge, ALU.add),
               waits=p.after(sem), inc=sem)
        for j in range(2, K + 1):
            p.emit(eng, lambda e, o=u_t, i=d_t, th=j - 0.5:
                   e.scalar_tensor_tensor(o, i, th, o, ALU.is_ge, ALU.add),
                   waits=p.after(sem), inc=sem)
        p.emit(eng, lambda e, o=u_t, i=d_t:
               e.scalar_tensor_tensor(o, i, -0.5, o, ALU.is_lt, ALU.subtract),
               waits=p.after(sem), inc=sem)
        for j in range(2, K + 1):
            p.emit(eng, lambda e, o=u_t, i=d_t, th=-(j - 0.5):
                   e.scalar_tensor_tensor(o, i, th, o, ALU.is_lt, ALU.add),
                   waits=p.after(sem), inc=sem)
        if use_s:
            p.emit("scalar", lambda e, o=out_ap, i=u_t:
                   e.activation(o, i, AF.Identity, scale=-1.0),
                   waits=p.after(sem) + p.after("s"), inc="s", ks=True)
            return p.mark("s")
        p.emit(eng, lambda e, o=out_ap, i=u_t:
               e.tensor_single_scalar(o, i, -1.0, ALU.mult),
               waits=p.after(sem), inc=sem)
    else:
        p.emit(eng, lambda e, o=u_t, gg=gatef: e.tensor_copy(o, gg),
               waits=p.after(sem), inc=sem)
        for j in range(1, K + 1):
            p.emit(eng, lambda e, o=y_ap, i=d_t, th=j - 0.5:
                   e.tensor_single_scalar(o, i, th, ALU.is_ge),
                   waits=p.after(sem), inc=sem)
            p.emit(eng, lambda e, o=u_t, c=y_ap:
                   e.tensor_tensor(o, o, c, ALU.add),
                   waits=p.after(sem), inc=sem)
        for j in range(1, K + 1):
            p.emit(eng, lambda e, o=y_ap, i=d_t, th=-(j - 0.5):
                   e.tensor_single_scalar(o, i, th, ALU.is_lt),
                   waits=p.after(sem), inc=sem)
            p.emit(eng, lambda e, o=u_t, c=y_ap:
                   e.tensor_tensor(o, o, c, ALU.subtract),
                   waits=p.after(sem), inc=sem)
        p.emit(eng, lambda e, o=out_ap, i=u_t: e.tensor_copy(o, i),
               waits=p.after(sem), inc=sem)
    return p.mark(sem)


def ln_stats(p, x_ap, bn12, mv, rstd, xc_ap, rows=128, pre=(), xc_pre=()):
    p.emit("vector", lambda e, o=bn12[0:rows, 0:6], i=x_ap[:, 0:512]:
           e.bn_stats(o, i), waits=list(pre), inc="v")
    p.emit("vector", lambda e, o=bn12[0:rows, 6:12], i=x_ap[:, 512:1024]:
           e.bn_stats(o, i), waits=list(pre), inc="v")
    p.emit("vector", lambda e, o=mv[0:rows, :], i=bn12[0:rows, :]:
           e.bn_aggr(o, i), waits=p.after("v"), inc="v", ks=True)
    m_aggr = p.mark("v")
    p.emit("vector", lambda e, o=xc_ap, i=x_ap, m=mv[0:rows, 0:1]:
           e.tensor_single_scalar(o, i, m, ALU.subtract),
           waits=[m_aggr] + list(xc_pre), inc="v", ks=True)
    m_xc = p.mark("v")
    p.emit("scalar", lambda e, o=rstd[0:rows, :], i=mv[0:rows, 1:2]:
           e.activation(o, i, AF.Ln, bias=p.eps_ln[0:rows, :]),
           waits=[m_aggr], inc="s")
    p.emit("scalar", lambda e, o=rstd[0:rows, :], i=rstd[0:rows, :]:
           e.activation(o, i, AF.Exp, scale=-0.5),
           waits=p.after("s"), inc="s", ks=True)
    return p.mark("s"), m_xc


def build_program(debug_outputs=(), phases=7):
    nc = bass.Bass("TRN2", target_bir_lowering=False)
    p = Prog(nc)

    x0_d = nc.dram_tensor("x", [N, D], F32, kind="ExternalInput").ap()
    wqT_d = nc.dram_tensor("wqT", [D, D], F32, kind="ExternalInput").ap()
    wkT_d = nc.dram_tensor("wkT", [D, D], F32, kind="ExternalInput").ap()
    wvT_d = nc.dram_tensor("wvT", [D, D], F32, kind="ExternalInput").ap()
    wpT_d = nc.dram_tensor("wpT", [D, D], F32, kind="ExternalInput").ap()
    bproj_d = nc.dram_tensor("bproj", [1, D], F32, kind="ExternalInput").ap()
    wgT_d = nc.dram_tensor("wgT", [D, F0], I32, kind="ExternalInput").ap()
    wuT_d = nc.dram_tensor("wuT", [D, F0], I32, kind="ExternalInput").ap()
    wdT_d = nc.dram_tensor("wdT", [F0, D], I32, kind="ExternalInput").ap()
    wQT_d = nc.dram_tensor("wQT", [D, DK], F32, kind="ExternalInput").ap()
    wKT_d = nc.dram_tensor("wKT", [D, DK], F32, kind="ExternalInput").ap()
    gam_d = nc.dram_tensor("gam", [1, D], F32, kind="ExternalInput").ap()
    bet_d = nc.dram_tensor("bet", [1, D], F32, kind="ExternalInput").ap()
    w1T_d = nc.dram_tensor("w1T", [D, F1], F32, kind="ExternalInput").ap()
    cw_d = nc.dram_tensor("cw", [128, 96], F32, kind="ExternalInput").ap()
    cb_d = nc.dram_tensor("cb", [128, 32], F32, kind="ExternalInput").ap()
    w2T_d = nc.dram_tensor("w2T", [F1, D], F32, kind="ExternalInput").ap()
    out_d = nc.dram_tensor("out", [N, D], F32, kind="ExternalOutput").ap()

    def scratch(name, shape, dtype):
        kind = "ExternalOutput" if name in debug_outputs else "Internal"
        return nc.dram_tensor(name, shape, dtype, kind=kind).ap()

    xn0_d = scratch("xn0", [N, D], BF16)
    g0_d = scratch("g0", [N, D], F32)
    x1_d = scratch("x1", [N, D], F32)
    w1q_d = scratch("w1q", [8, 128, F1], BF16)
    w2q_d = scratch("w2q", [32, 128, D], BF16)
    xq2_d = scratch("xq2", [NWIN, 128, 32 * 512], BF16)
    kvdbg_d = (nc.dram_tensor("kvdbg", [64, NH * 65], BF16,
                              kind="ExternalOutput").ap()
               if "kvdbg" in debug_outputs else None)

    ctx = ExitStack()
    en = ctx.enter_context

    ident = en(nc.sbuf_tensor("ident", [128, 128], BF16))
    identf = en(nc.sbuf_tensor("identf", [128, 128], F32))
    ones1f = en(nc.sbuf_tensor("ones1f", [1, 128], F32))
    ones1b = en(nc.sbuf_tensor("ones1b", [1, 128], BF16))
    onesc = en(nc.sbuf_tensor("onesc", [128, 1], F32))
    eps_ln = en(nc.sbuf_tensor("eps_ln", [128, 1], F32))
    p.eps_ln = eps_ln
    m_bc = en(nc.sbuf_tensor("m_bc", [128, 4], F32))
    mval = en(nc.sbuf_tensor("mval", [1, 8], F32))
    kv_sb = en(nc.sbuf_tensor("kv_sb", [64, NH * 65], BF16))
    Msb = en(nc.sbuf_tensor("Msb", [128, D], BF16))
    Q1_sb = en(nc.sbuf_tensor("Q1_sb", [128, N], BF16))
    recs2_sb = en(nc.sbuf_tensor("recs2_sb", [128, NWIN * 4], F32))
    bn12 = en(nc.sbuf_tensor("bn12", [128, 12], F32))
    mv_t = en(nc.sbuf_tensor("mv_t", [128, 2], F32))
    rstd_t = en(nc.sbuf_tensor("rstd_t", [128, 1], F32))
    gam_bc = en(nc.sbuf_tensor("gam_bc", [128, D], F32))
    bet_bc = en(nc.sbuf_tensor("bet_bc", [128, D], F32))
    Mt = en(nc.sbuf_tensor("Mt", [128, 1], F32))
    negMt = en(nc.sbuf_tensor("negMt", [128, 1], F32))
    p.Mt = Mt
    p.negMt = negMt

    marks = {}

    with ExitStack() as px:
        pe = px.enter_context
        iota_i = pe(nc.sbuf_tensor("iota_i", [128, 128], I32))
        grow = pe(nc.sbuf_tensor("grow", [1, D], F32))
        brow = pe(nc.sbuf_tensor("brow", [1, D], F32))
        mtile = [pe(nc.sbuf_tensor(f"mtile{i}", [128, 2048], F32)) for i in range(2)]
        mred = pe(nc.sbuf_tensor("mred", [128, 2], F32))
        macc = pe(nc.sbuf_tensor("macc", [128, 2], F32))
        ps_bc = pe(nc.psum_tensor("ps_bc", [128, 512], F32))
        ps_m = pe(nc.psum_tensor("ps_m", [1, 1], F32))
        ps_mb = pe(nc.psum_tensor("ps_mb", [128, 1], F32))

        p.emit("gpsimd", lambda e: e.iota(iota_i[:, :], [[1, 128]],
                                          channel_multiplier=-1), inc="p")
        m_iota = p.mark("p")
        p.emit("vector", lambda e: e.tensor_single_scalar(
            ident[:], iota_i[:], 0, ALU.is_equal), waits=[m_iota], inc="v")
        p.emit("vector", lambda e: e.tensor_single_scalar(
            identf[:], iota_i[:], 0, ALU.is_equal), waits=[m_iota], inc="v")
        p.emit("vector", lambda e: e.memset(ones1f[:], 1.0), inc="v")
        p.emit("vector", lambda e: e.memset(ones1b[:], 1.0), inc="v")
        p.emit("vector", lambda e: e.memset(onesc[:], 1.0), inc="v")
        p.emit("vector", lambda e: e.memset(eps_ln[:], 1e-5), inc="v")
        p.emit("vector", lambda e: e.memset(Mt[:], MAGIC), inc="v")
        p.emit("vector", lambda e: e.memset(negMt[:], -MAGIC), inc="v")
        p.emit("vector", lambda e: e.memset(macc[:], 0.0), inc="v")
        m_consts = p.mark("v")

        for wi, (w_d, n_k, n_n, fw, fdim) in enumerate(
                ((w1T_d, 8, 2, 2048, F1), (w2T_d, 32, 1, 1024, F1))):
            for c in range(n_k):
                for nn2 in range(n_n):
                    i2 = c * n_n + nn2
                    buf = mtile[i2 % 2]
                    mk_free = marks.get(("mt", i2 % 2))
                    mk_ld = p.emit(
                        "sync", lambda e, b=buf, w=w_d, cc=c, n3=nn2, f2=fw:
                        e.dma_start(b[:, 0:f2], w[128 * cc:128 * (cc + 1),
                                                  f2 * n3:f2 * (n3 + 1)]),
                        waits=[mk_free] if mk_free else [],
                        inc=f"da{i2 % 2}", k=16)
                    p.emit("vector",
                           lambda e, b=buf, wi2=wi, f2=fw:
                           e.reduce_sum(mred[:, wi2:wi2 + 1], b[:, 0:f2], AX.X,
                                        apply_absolute_value=True),
                           waits=[mk_ld] + p.after("v"), inc="v", ks=True)
                    marks[("mt", i2 % 2)] = p.mark("v")
                    p.emit("vector",
                           lambda e, wi2=wi:
                           e.tensor_tensor(macc[:, wi2:wi2 + 1],
                                           macc[:, wi2:wi2 + 1],
                                           mred[:, wi2:wi2 + 1], ALU.add),
                           waits=p.after("v"), inc="v", ks=True)
            m_acc = p.mark("v")
            p.emit("tensor", lambda e, wi2=wi:
                   e.matmul(ps_m[:, :], macc[:, wi2:wi2 + 1], onesc[:, :],
                            start=True, stop=True),
                   waits=[m_acc, m_consts], inc="t")
            p.emit("vector", lambda e, wi2=wi, sc=1.0 / (fdim * D):
                   e.tensor_single_scalar(mval[0:1, 4 * wi2:4 * wi2 + 1],
                                          ps_m[:, :], sc, ALU.mult),
                   waits=p.after("t"), inc="v")
            p.emit("vector", lambda e, wi2=wi:
                   e.tensor_single_scalar(mval[0:1, 4 * wi2:4 * wi2 + 1],
                                          mval[0:1, 4 * wi2:4 * wi2 + 1],
                                          1e-5, ALU.max),
                   waits=p.after("v"), inc="v", ks=True)
            p.emit("vector", lambda e, wi2=wi:
                   e.reciprocal(mval[0:1, 4 * wi2 + 1:4 * wi2 + 2],
                                mval[0:1, 4 * wi2:4 * wi2 + 1]),
                   waits=p.after("v"), inc="v", ks=True)
            m_mv = p.mark("v")
            for part in range(2):
                p.emit("tensor", lambda e, wi2=wi, pp=part:
                       e.matmul(ps_mb[:, :], ones1f[:, :],
                                mval[0:1, 4 * wi2 + pp:4 * wi2 + pp + 1],
                                start=True, stop=True),
                       waits=[m_mv] + p.after("v"), inc="t")
                p.emit("vector", lambda e, wi2=wi, pp=part:
                       e.tensor_copy(m_bc[:, 2 * wi2 + pp:2 * wi2 + pp + 1],
                                     ps_mb[:, :]),
                       waits=p.after("t"), inc="v")
        marks["m_bc"] = p.mark("v")

        p.emit("sync", lambda e: e.dma_start(grow[:], gam_d), inc="d", k=16)
        p.emit("sync", lambda e: e.dma_start(brow[:], bet_d), inc="d", k=16)
        m_rows = p.mark("d")
        for half in range(2):
            for (row, dst) in ((grow, gam_bc), (brow, bet_bc)):
                p.emit("tensor", lambda e, r=row, h2=half:
                       e.matmul(ps_bc[:, :], ones1f[:, :],
                                r[0:1, 512 * h2:512 * (h2 + 1)],
                                start=True, stop=True),
                       waits=[m_rows, m_consts] + p.after("v"), inc="t")
                p.emit("vector", lambda e, d2=dst, h2=half:
                       e.tensor_copy(d2[:, 512 * h2:512 * (h2 + 1)],
                                     ps_bc[:, :]),
                       waits=p.after("t"), inc="v")
        marks["gb_bc"] = p.mark("v")
        p.flush()

    if phases >= 1:
        with ExitStack() as px:
            pe = px.enter_context
            x0t = [pe(nc.sbuf_tensor(f"x0t{i}", [128, 4 * D], F32)) for i in range(2)]
            xnt = [pe(nc.sbuf_tensor(f"xnt{i}", [128, 4 * D], BF16)) for i in range(2)]
            xc1 = pe(nc.sbuf_tensor("xc1p1", [128, D], F32))
            marks["xn0"] = [None] * 8
            last_xn = {}
            for j in range(8):
                par = j % 2
                mk_ld = p.emit(
                    "sync", lambda e, b=x0t[par], jj=j:
                    e.dma_start(b[:].rearrange("p (a d) -> p a d", a=4),
                                x0_d[512 * jj:512 * (jj + 1), :]
                                .rearrange("(a p) d -> p a d", p=128)),
                    waits=[last_xn.get(par)], inc=f"da{par}", k=16)
                for tc in range(4):
                    xap = x0t[par][:, D * tc:D * (tc + 1)]
                    m_r, m_xc = ln_stats(p, xap, bn12, mv_t, rstd_t, xc1[:, :],
                                         pre=[mk_ld] + p.after("s"),
                                         xc_pre=p.after("s"))
                    p.emit("scalar",
                           lambda e, o=xnt[par][:, D * tc:D * (tc + 1)],
                           i=xc1[:, :], r=rstd_t[:, 0:1]:
                           e.activation(o, i, AF.Identity, scale=r),
                           waits=[m_r, m_xc,
                                  marks.get(("xnst", par))], inc="s")
                last_xn[par] = p.mark("s")
                mk_st = p.emit(
                    "sync", lambda e, b=xnt[par], jj=j:
                    e.dma_start(xn0_d[512 * jj:512 * (jj + 1), :]
                                .rearrange("(a p) d -> p a d", p=128),
                                b[:].rearrange("p (a d) -> p a d", a=4)),
                    waits=[last_xn[par]], inc=f"db{par}", k=16)
                marks[("xnst", par)] = mk_st
                marks["xn0"][j] = mk_st
            p.flush()

    if phases >= 2:
        _p2(nc, p, en, marks, locals())
    if phases >= 3:
        _p3(nc, p, en, marks, locals())
    if phases >= 4:
        _p4(nc, p, en, marks, locals())
    if phases >= 5:
        _p5a(nc, p, en, marks, locals())
    if phases >= 6:
        _p5b(nc, p, en, marks, locals())

    p.emit("sync", lambda e: e.nop(), waits=p.after(*list(p.h.keys())))
    p.flush()
    ctx.close()
    return nc


def _p2(nc, p, en, marks, L):
    xn0_d, wkT_d, wvT_d, kv_sb, ident = (
        L["xn0_d"], L["wkT_d"], L["wvT_d"], L["kv_sb"], L["ident"])
    with ExitStack() as px:
        pe = px.enter_context
        wkv = pe(nc.sbuf_tensor("wkv", [128, 8 * 2048], BF16))
        xnf = [pe(nc.sbuf_tensor(f"xnf{i}", [128, 8 * 512], BF16)) for i in range(2)]
        ktmp = pe(nc.sbuf_tensor("ktmp", [128, 512], F32))
        ksb = pe(nc.sbuf_tensor("ksb", [128, 1024], BF16))
        v_aug = pe(nc.sbuf_tensor("v_aug", [128, NH * 65], BF16))
        kvw = [pe(nc.psum_tensor(f"kvw{i}", [128, 512], F32)) for i in range(4)]
        kvacc = [pe(nc.psum_tensor(f"kvacc{i}", [64, 260], F32)) for i in range(4)]

        for c in range(8):
            p.emit("gpsimd", lambda e, cc=c:
                   e.dma_start(wkv[:, 2048 * cc:2048 * cc + 1024],
                               wkT_d[128 * cc:128 * (cc + 1), :]), inc="g", k=16)
            p.emit("gpsimd", lambda e, cc=c:
                   e.dma_start(wkv[:, 2048 * cc + 1024:2048 * (cc + 1)],
                               wvT_d[128 * cc:128 * (cc + 1), :]), inc="g", k=16)
        m_wkv = p.mark("g")
        for h in range(NH):
            p.emit("vector", lambda e, hh=h:
                   e.memset(v_aug[:, 65 * hh + 64:65 * hh + 65], 1.0), inc="v")
        m_vones = p.mark("v")

        m_evac_s = None
        m_evac_v = None
        for j in range(8):
            par = j % 2
            mk_tp = None
            for c in range(8):
                mk_tp = p.emit(
                    "sync", lambda e, cc=c, jj=j, b=xnf[par]:
                    e.dma_start_transpose(
                        b[:, 512 * cc:512 * (cc + 1)],
                        xn0_d[512 * jj:512 * (jj + 1), 128 * cc:128 * (cc + 1)]),
                    waits=[marks.get(("xnf_used", par))],
                    inc=f"da{par}", k=16)
            for tc in range(4):
                first = True
                mm = None
                for c in range(8):
                    for nch in range(4):
                        w0 = []
                        if first:
                            w0 = [mk_tp, m_wkv, m_evac_s, m_evac_v]
                            first = False
                        mm = p.emit(
                            "tensor", lambda e, cc=c, nn=nch, t2=tc, b=xnf[par]:
                            e.matmul(kvw[nn][:, :],
                                     b[:, 512 * cc + 128 * t2:512 * cc + 128 * (t2 + 1)],
                                     wkv[:, 2048 * cc + 512 * nn:2048 * cc + 512 * (nn + 1)],
                                     start=(cc == 0), stop=(cc == 7)),
                            waits=w0, inc="t")
                m_qkv = p.mark("t")
                if tc == 3:
                    marks[("xnf_used", par)] = m_qkv
                for half in range(2):
                    p.emit("scalar", lambda e, h2=half:
                           e.activation(ktmp[:, :], kvw[h2][:, :], AF.Square),
                           waits=[m_qkv] + p.after("s"), inc="s")
                    p.emit("scalar", lambda e, h2=half:
                           e.activation(ksb[:, 512 * h2:512 * (h2 + 1)],
                                        ktmp[:, :], AF.Exp, scale=-0.5),
                           waits=p.after("s"), inc="s")
                m_evac_s = p.mark("s")
                for h in range(NH):
                    src = kvw[2 + h // 8]
                    p.emit("vector", lambda e, hh=h, sp=src:
                           e.tensor_copy(v_aug[:, 65 * hh:65 * hh + 64],
                                         sp[:, 64 * (hh % 8):64 * (hh % 8) + 64]),
                           waits=[m_qkv, m_vones], inc="v")
                m_evac_v = p.mark("v")
                for h in range(NH):
                    g4, slot = h // 4, h % 4
                    stf = (j == 0 and tc == 0 and slot == 0)
                    spf = (j == 7 and tc == 3 and slot == 3)
                    p.emit("tensor", lambda e, hh=h, g2=g4, s2=slot, st=stf, sp=spf:
                           e.matmul(kvacc[g2][0:64, 65 * s2:65 * (s2 + 1)],
                                    ksb[:, 64 * hh:64 * (hh + 1)],
                                    v_aug[:, 65 * hh:65 * (hh + 1)],
                                    start=st, stop=sp),
                           waits=[m_evac_s, m_evac_v], inc="t")
        m_kvmm = p.mark("t")
        for g4 in range(4):
            p.emit("vector", lambda e, g2=g4:
                   e.tensor_copy(kv_sb[0:64, 260 * g2:260 * (g2 + 1)],
                                 kvacc[g2][0:64, :]),
                   waits=[m_kvmm], inc="v")
        marks["kv_sb"] = p.mark("v")
        if "kvdbg" in L["debug_outputs"]:
            kvdbg_d = L["kvdbg_d"]
            p.emit("sync", lambda e: e.dma_start(kvdbg_d, kv_sb[0:64, :]),
                   waits=[marks["kv_sb"]], inc="d", k=16)
        p.flush()


def _p3(nc, p, en, marks, L):
    xn0_d, g0_d = L["xn0_d"], L["g0_d"]
    wgT_d, wuT_d, wdT_d = L["wgT_d"], L["wuT_d"], L["wdT_d"]
    w1T_d, w2T_d, w1q_d, w2q_d = L["w1T_d"], L["w2T_d"], L["w1q_d"], L["w2q_d"]
    m_bc = L["m_bc"]
    with ExitStack() as px:
        pe = px.enter_context
        wg = pe(nc.sbuf_tensor("wg", [128, 8 * F0], BF16))
        wu = pe(nc.sbuf_tensor("wu", [128, 8 * F0], BF16))
        wd = pe(nc.sbuf_tensor("wd", [128, 24 * 1024], BF16))
        xnf3 = [pe(nc.sbuf_tensor(f"xnf3{i}", [128, 8 * 256], BF16)) for i in range(2)]
        hsb = pe(nc.sbuf_tensor("hsb", [128, 24 * 256], BF16))
        sgt = [pe(nc.sbuf_tensor(f"sgt{i}", [128, 256], F32)) for i in range(2)]
        g0t = [pe(nc.sbuf_tensor(f"g0t{i}", [128, D], F32)) for i in range(2)]
        qtile = [pe(nc.sbuf_tensor("qtile0", [128, 512], F32))] * 2
        qb = pe(nc.sbuf_tensor("qb", [128, 512], F32))
        qout = [pe(nc.sbuf_tensor(f"qout{i}", [128, 512], BF16)) for i in range(2)]
        qtr = pe(nc.sbuf_tensor("qtr", [128, 512], F32))
        qgate = pe(nc.sbuf_tensor("qgate", [128, 512], I32))
        pg = [pe(nc.psum_tensor(f"pg{i}", [128, 256], F32)) for i in range(2)]
        pu = [pe(nc.psum_tensor(f"pu{i}", [128, 256], F32)) for i in range(2)]
        pd = [pe(nc.psum_tensor(f"pd{i}", [128, 512], F32)) for i in range(4)]

        for c in range(8):
            p.emit("gpsimd", lambda e, cc=c:
                   e.dma_start(wg[:, F0 * cc:F0 * (cc + 1)],
                               wgT_d[128 * cc:128 * (cc + 1), :]), inc="g", k=16)
            p.emit("gpsimd", lambda e, cc=c:
                   e.dma_start(wu[:, F0 * cc:F0 * (cc + 1)],
                               wuT_d[128 * cc:128 * (cc + 1), :]), inc="g", k=16)
        m_wgu = p.mark("g")
        for m in range(24):
            p.emit("gpsimd", lambda e, mm=m:
                   e.dma_start(wd[:, 1024 * mm:1024 * (mm + 1)],
                               wdT_d[128 * mm:128 * (mm + 1), :]), inc="dh0",
                   k=16)
        m_w = p.mark("dh0")

        qjobs = []
        for c in range(8):
            for nn2 in range(8):
                qjobs.append((0, w1T_d, w1q_d, c, nn2))
        for c in range(32):
            for nn2 in range(2):
                qjobs.append((1, w2T_d, w2q_d, c, nn2))
        qidx = [0]

        def quant_step():
            if qidx[0] >= len(qjobs):
                return
            wi, src_d, dst_d, c, nn2 = qjobs[qidx[0]]
            par = qidx[0] % 2
            qidx[0] += 1
            mk_ld = p.emit("gpsimd", lambda e, b=qtile[par], w=src_d, cc=c, n3=nn2:
                           e.dma_start(b[:], w[128 * cc:128 * (cc + 1),
                                              512 * n3:512 * (n3 + 1)]),
                           waits=[marks.get(("qt_used", par))],
                           inc=f"df{par}", k=16)
            p.emit("vector", lambda e, b=qtile[par], sc=m_bc[:, 2 * wi + 1:2 * wi + 2]:
                   e.tensor_single_scalar(qb[:], b[:], sc, ALU.mult),
                   waits=[mk_ld, marks["m_bc"]] + p.after("v"), inc="v")
            marks[("qt_used", par)] = p.mark("v")
            p.emit("vector", lambda e:
                   e.tensor_scalar(qb[:], qb[:], 1.0, -1.0, ALU.min, ALU.max),
                   waits=p.after("v"), inc="v")
            mk_r = round_emit(p, qout[par][:, :], qb[:], qb[:], qtr[:],
                              qgate[:].bitcast(F32),
                              pre=p.after("v") + [marks.get(("qo_used", par))])
            marks[("qo_used", par)] = p.emit(
                "gpsimd", lambda e, b=qout[par], w=dst_d, cc=c, n3=nn2:
                e.dma_start(w[cc, :, 512 * n3:512 * (n3 + 1)], b[:]),
                waits=[mk_r], inc=f"dg{par}", k=16)

        m_sg = {}
        m_h = {}
        m_g0s = {}
        marks["g0"] = [None] * 32
        for j in range(16):
            par = j % 2
            mk_tp = None
            for c in range(8):
                mk_tp = p.emit(
                    "sync", lambda e, cc=c, jj=j, b=xnf3[par]:
                    e.dma_start_transpose(
                        b[:, 256 * cc:256 * (cc + 1)],
                        xn0_d[256 * jj:256 * (jj + 1), 128 * cc:128 * (cc + 1)]),
                    waits=[marks.get(("xnf3_used", par))],
                    inc=f"da{par}", k=16)
            for _ in range(8):
                quant_step()
            m_dn_prev = marks.get(("pd_used",))
            for m in range(24):
                mp = m % 2
                first_w = [mk_tp, m_wgu, m_sg.get(mp), m_h.get(mp)]
                mm = None
                for c in range(8):
                    mm = p.emit("tensor", lambda e, cc=c, mm2=m, b=xnf3[par]:
                                e.matmul(pg[mm2 % 2][:, :],
                                         wg[:, F0 * cc + 128 * mm2:F0 * cc + 128 * (mm2 + 1)],
                                         b[:, 256 * cc:256 * (cc + 1)],
                                         start=(cc == 0), stop=(cc == 7)),
                                waits=first_w if c == 0 else [], inc="t")
                m_pg = p.mark("t")
                for c in range(8):
                    mm = p.emit("tensor", lambda e, cc=c, mm2=m, b=xnf3[par]:
                                e.matmul(pu[mm2 % 2][:, :],
                                         wu[:, F0 * cc + 128 * mm2:F0 * cc + 128 * (mm2 + 1)],
                                         b[:, 256 * cc:256 * (cc + 1)],
                                         start=(cc == 0), stop=(cc == 7)),
                                waits=[], inc="t")
                m_pu = p.mark("t")
                p.emit("scalar", lambda e, mp2=mp:
                       e.activation(sgt[mp2][:], pg[mp2][:, :], AF.Silu,
                                    scale=INV127),
                       waits=[m_pg, m_h.get(mp)] + p.after("s"), inc="s")
                m_sg[mp] = p.mark("s")
                w_h = [m_sg[mp], m_pu]
                if m == 0:
                    w_h.append(marks.get(("hsb_used",)))
                p.emit("vector", lambda e, mp2=mp, mm2=m:
                       e.tensor_tensor(hsb[:, 256 * mm2:256 * (mm2 + 1)],
                                       sgt[mp2][:], pu[mp2][:, :], ALU.mult),
                       waits=w_h, inc="v")
                m_h[mp] = p.mark("v")
            marks[("xnf3_used", par)] = p.mark("t")
            m_hall = p.mark("v")
            for tc2 in range(2):
                for half in range(2):
                    bank = 2 * tc2 + half
                    for m in range(24):
                        w0 = []
                        if m == 0:
                            w0 = [m_hall, m_dn_prev, m_w]
                        p.emit("tensor",
                               lambda e, mm2=m, t3=tc2, h3=half, b2=bank:
                               e.matmul(pd[b2][:, :],
                                        hsb[:, 256 * mm2 + 128 * t3:256 * mm2 + 128 * (t3 + 1)],
                                        wd[:, 1024 * mm2 + 512 * h3:1024 * mm2 + 512 * (h3 + 1)],
                                        start=(mm2 == 0), stop=(mm2 == 23)),
                               waits=w0, inc="t")
                    m_pd = p.mark("t")
                    p.emit("scalar", lambda e, t3=tc2, h3=half, b2=bank:
                           e.activation(g0t[t3][:, 512 * h3:512 * (h3 + 1)],
                                        pd[b2][:, :], AF.Silu, scale=INV127SQ),
                           waits=[m_pd, m_g0s.get(tc2)] + p.after("s"), inc="s")
            marks[("pd_used",)] = p.mark("s")
            marks[("hsb_used",)] = p.mark("t")
            for tc2 in range(2):
                rc = 2 * j + tc2
                m_g0s[tc2] = p.emit(
                    "sync", lambda e, t3=tc2, r2=rc:
                    e.dma_start(g0_d[128 * r2:128 * (r2 + 1), :], g0t[t3][:]),
                    waits=[marks[("pd_used",)]], inc=f"de{tc2}", k=16)
                marks["g0"][rc] = m_g0s[tc2]
        while qidx[0] < len(qjobs):
            quant_step()
        marks["wq_done"] = None
        p.flush()


def _p4(nc, p, en, marks, L):
    xn0_d, x0_d, g0_d, x1_d = L["xn0_d"], L["x0_d"], L["g0_d"], L["x1_d"]
    wqT_d, wpT_d, wQT_d, wKT_d, bproj_d = (
        L["wqT_d"], L["wpT_d"], L["wQT_d"], L["wKT_d"], L["bproj_d"])
    kv_sb, Msb, Q1_sb, ident, ones1b = (
        L["kv_sb"], L["Msb"], L["Q1_sb"], L["ident"], L["ones1b"])
    bn12, mv_t, rstd_t = L["bn12"], L["mv_t"], L["rstd_t"]
    with ExitStack() as px:
        pe = px.enter_context
        wq2 = pe(nc.sbuf_tensor("wq2", [128, 8 * 1024], BF16))
        wp2 = pe(nc.sbuf_tensor("wp2", [128, 8 * 1024], BF16))
        wQs = pe(nc.sbuf_tensor("wQs", [128, 8 * 128], BF16))
        wKs = pe(nc.sbuf_tensor("wKs", [128, 8 * 128], BF16))
        bproj_b = pe(nc.sbuf_tensor("bproj_b", [1, D], BF16))
        xnf4 = [pe(nc.sbuf_tensor(f"xnf4{i}", [128, 8 * 512], BF16)) for i in range(2)]
        qtmp = pe(nc.sbuf_tensor("qtmp", [128, 512], F32))
        q_sb2 = pe(nc.sbuf_tensor("q_sb2", [64, 16 * 512], BF16))
        num_sb = pe(nc.sbuf_tensor("num_sb", [128, 16 * 64], BF16))
        den16 = pe(nc.sbuf_tensor("den16", [128, 16], F32))
        rec16 = pe(nc.sbuf_tensor("rec16", [128, 16], F32))
        numz_tm = pe(nc.sbuf_tensor("numz_tm", [128, 1024], BF16))
        numz_fm = pe(nc.sbuf_tensor("numz_fm", [128, 8 * 128], BF16))
        x0c = [pe(nc.sbuf_tensor(f"x0c{i}", [128, D], F32)) for i in range(2)]
        g0c = [pe(nc.sbuf_tensor(f"g0c{i}", [128, D], F32)) for i in range(2)]
        x1t = [pe(nc.sbuf_tensor(f"x1t{i}", [128, D], F32)) for i in range(2)]
        cmb = pe(nc.sbuf_tensor("cmb", [128, D], F32))
        xc4 = pe(nc.sbuf_tensor("xc4", [128, D], F32))
        xn1t = pe(nc.sbuf_tensor("xn1t", [128, D], BF16))
        xn1f = pe(nc.sbuf_tensor("xn1f", [128, 8 * 128], BF16))
        k1sb = pe(nc.sbuf_tensor("k1sb", [128, 128], BF16))
        M_acc = [pe(nc.psum_tensor(f"Macc{i}", [128, 512], F32)) for i in range(2)]
        pA = pe(nc.psum_tensor("pA", [128, 512], F32))
        pB = pe(nc.psum_tensor("pB", [128, 512], F32))
        ptp = pe(nc.psum_tensor("ptp", [128, 128], BF16))
        psA = pe(nc.psum_tensor("psA", [128, 512], F32))
        psB = pe(nc.psum_tensor("psB", [128, 512], F32))
        pk = pe(nc.psum_tensor("pk", [128, 128], F32))

        for c in range(8):
            p.emit("gpsimd", lambda e, cc=c:
                   e.dma_start(wq2[:, 1024 * cc:1024 * (cc + 1)],
                               wqT_d[128 * cc:128 * (cc + 1), :]), inc="g", k=16)
            p.emit("gpsimd", lambda e, cc=c:
                   e.dma_start(wp2[:, 1024 * cc:1024 * (cc + 1)],
                               wpT_d[128 * cc:128 * (cc + 1), :]), inc="g", k=16)
            p.emit("gpsimd", lambda e, cc=c:
                   e.dma_start(wQs[:, 128 * cc:128 * (cc + 1)],
                               wQT_d[128 * cc:128 * (cc + 1), :]), inc="g", k=16)
            p.emit("gpsimd", lambda e, cc=c:
                   e.dma_start(wKs[:, 128 * cc:128 * (cc + 1)],
                               wKT_d[128 * cc:128 * (cc + 1), :]), inc="g", k=16)
        p.emit("gpsimd", lambda e: e.dma_start(bproj_b[:], bproj_d), inc="g", k=16)
        m_w4 = p.mark("g")

        marks["x1"] = [None] * 32
        qb_banks = [pA, pB]
        m_qevac = {}
        m_nd_used = {}
        m_proj_used = {}
        m_cmb = {}
        ld_marks = {}

        def emit_xg_load(j, tc):
            rc = 4 * j + tc
            if rc >= 32 or rc in ld_marks:
                return
            parc = tc % 2
            mk1 = p.emit("sync", lambda e, b=x0c[parc], r2=rc:
                         e.dma_start(b[:], x0_d[128 * r2:128 * (r2 + 1), :]),
                         waits=[m_cmb.get(parc)], inc=f"dc{parc}", k=16)
            mk2 = p.emit("sync", lambda e, b=g0c[parc], r2=rc:
                         e.dma_start(b[:], g0_d[128 * r2:128 * (r2 + 1), :]),
                         waits=[m_cmb.get(parc)], inc=f"dd{parc}", k=16)
            ld_marks[rc] = (mk1, mk2)

        for j in range(8):
            par = j % 2
            mk_tp = None
            for c in range(8):
                mk_tp = p.emit(
                    "sync", lambda e, cc=c, jj=j, b=xnf4[par]:
                    e.dma_start_transpose(
                        b[:, 512 * cc:512 * (cc + 1)],
                        xn0_d[512 * jj:512 * (jj + 1), 128 * cc:128 * (cc + 1)]),
                    waits=[marks.get(("xnf4_used", par))],
                    inc=f"da{par}", k=16)
            emit_xg_load(j, 0)
            emit_xg_load(j, 1)
            for m in range(8):
                bank = qb_banks[m % 2]
                for c in range(8):
                    w0 = []
                    if c == 0:
                        w0 = [mk_tp, m_w4, m_qevac.get(m % 2),
                              marks.get(("q_used",))]
                    p.emit("tensor", lambda e, cc=c, mm2=m, b=xnf4[par], bk=bank:
                           e.matmul(bk[:, :],
                                    wq2[:, 1024 * cc + 128 * mm2:1024 * cc + 128 * (mm2 + 1)],
                                    b[:, 512 * cc:512 * (cc + 1)],
                                    start=(cc == 0), stop=(cc == 7)),
                           waits=w0, inc="t")
                m_q = p.mark("t")
                p.emit("scalar", lambda e, bk=bank:
                       e.activation(qtmp[:], bk[:, :], AF.Square),
                       waits=[m_q] + p.after("s"), inc="s")
                for hh2 in range(2):
                    p.emit("scalar", lambda e, mm2=m, h3=hh2:
                           e.activation(q_sb2[0:64, 512 * (2 * mm2 + h3):512 * (2 * mm2 + h3 + 1)],
                                        qtmp[64 * h3:64 * (h3 + 1), :],
                                        AF.Exp, scale=-0.5),
                           waits=p.after("s"), inc="s")
                m_qevac[m % 2] = p.mark("s")
            m_qstage = p.mark("s")
            for tc in range(4):
                rc = 4 * j + tc
                for h in range(NH):
                    bank = qb_banks[h % 2]
                    w0 = [m_nd_used.get((h % 2, "s")), m_nd_used.get((h % 2, "v"))]
                    if h == 0:
                        w0 += [m_qstage, marks["kv_sb"]]
                    p.emit("tensor", lambda e, hh=h, t2=tc, bk=bank:
                           e.matmul(bk[:, 0:65],
                                    q_sb2[0:64, 512 * hh + 128 * t2:512 * hh + 128 * (t2 + 1)],
                                    kv_sb[0:64, 65 * hh:65 * (hh + 1)],
                                    start=True, stop=True),
                           waits=w0, inc="t")
                    m_nd = p.mark("t")
                    p.emit("scalar", lambda e, hh=h, bk=bank:
                           e.copy(num_sb[:, 64 * hh:64 * (hh + 1)], bk[:, 0:64]),
                           waits=[m_nd] + p.after("s"), inc="s")
                    m_nd_used[(h % 2, "s")] = p.mark("s")
                    p.emit("vector", lambda e, hh=h, bk=bank:
                           e.tensor_copy(den16[:, hh:hh + 1], bk[:, 64:65]),
                           waits=[m_nd], inc="v")
                    m_nd_used[(h % 2, "v")] = p.mark("v")
                p.emit("vector", lambda e:
                       e.tensor_single_scalar(rec16[:], den16[:], 1e-6, ALU.add),
                       waits=p.after("v", "s"), inc="v", ks=True)
                p.emit("vector", lambda e: e.reciprocal(rec16[:], rec16[:]),
                       waits=p.after("v"), inc="v", ks=True)
                m_rec = p.mark("v")
                for h in range(NH):
                    p.emit("vector", lambda e, hh=h:
                           e.tensor_single_scalar(numz_tm[:, 64 * hh:64 * (hh + 1)],
                                                  num_sb[:, 64 * hh:64 * (hh + 1)],
                                                  rec16[:, hh:hh + 1], ALU.mult),
                           waits=[m_rec], inc="v", ks=True)
                m_numz = p.mark("v")
                m_tpc = marks.get(("ptp_free",))
                for c in range(8):
                    p.emit("tensor", lambda e, cc=c:
                           e.transpose(ptp[:, :], numz_tm[:, 128 * cc:128 * (cc + 1)],
                                       ident[:]),
                           waits=[m_numz, m_tpc] if c == 0 else [m_tpc], inc="t")
                    m_t1 = p.mark("t")
                    m_tpc = p.emit("vector", lambda e, cc=c:
                                   e.tensor_copy(numz_fm[:, 128 * cc:128 * (cc + 1)],
                                                 ptp[:, :]),
                                   waits=[m_t1], inc="v")
                marks[("ptp_free",)] = m_tpc
                for half in range(2):
                    bank = psA if half == 0 else psB
                    for c in range(8):
                        w0 = [m_tpc, m_cmb.get(("proj", half))] if c == 0 else []
                        p.emit("tensor", lambda e, cc=c, h3=half, bk=bank:
                               e.matmul(bk[:, :],
                                        numz_fm[:, 128 * cc:128 * (cc + 1)],
                                        wp2[:, 1024 * cc + 512 * h3:1024 * cc + 512 * (h3 + 1)],
                                        start=(cc == 0), stop=False),
                               waits=w0, inc="t")
                    p.emit("tensor", lambda e, h3=half, bk=bank:
                           e.matmul(bk[:, :], ones1b[:, :],
                                    bproj_b[0:1, 512 * h3:512 * (h3 + 1)],
                                    start=False, stop=True),
                           waits=[], inc="t")
                m_proj = p.mark("t")
                parc = tc % 2
                parx = tc % 2
                mk1, mk2 = ld_marks[rc]
                for half in range(2):
                    bank = psA if half == 0 else psB
                    p.emit("vector", lambda e, h3=half, bk=bank, gc=g0c[parc]:
                           e.tensor_tensor(cmb[:, 512 * h3:512 * (h3 + 1)],
                                           gc[:, 512 * h3:512 * (h3 + 1)],
                                           bk[:, :], ALU.mult),
                           waits=[m_proj, mk2] + p.after("v"), inc="v")
                    p.emit("vector", lambda e, h3=half, xc=x0c[parc], xt=x1t[parx]:
                           e.tensor_tensor(xt[:, 512 * h3:512 * (h3 + 1)],
                                           cmb[:, 512 * h3:512 * (h3 + 1)],
                                           xc[:, 512 * h3:512 * (h3 + 1)], ALU.add),
                           waits=[mk1, marks.get(("x1st", parx))] + p.after("v"),
                           inc="v")
                m_x1 = p.mark("v")
                m_cmb[parc] = m_x1
                m_cmb[("proj", 0)] = m_x1
                m_cmb[("proj", 1)] = m_x1
                emit_xg_load(j, tc + 2)
                marks["x1"][rc] = p.emit(
                    "sync", lambda e, xt=x1t[parx], r2=rc:
                    e.dma_start(x1_d[128 * r2:128 * (r2 + 1), :], xt[:]),
                    waits=[m_x1], inc=f"db{parx}", k=16)
                marks[("x1st", parx)] = marks["x1"][rc]
                m_r, m_xc = ln_stats(p, x1t[parx][:, :], bn12, mv_t, rstd_t,
                                     xc4[:, :], pre=[m_x1] + p.after("s"),
                                     xc_pre=p.after("s"))
                p.emit("scalar", lambda e:
                       e.activation(xn1t[:], xc4[:], AF.Identity,
                                    scale=rstd_t[:, 0:1]),
                       waits=[m_r, m_xc, marks.get(("xn1t_used",))], inc="s")
                m_xn1 = p.mark("s")
                m_tpc2 = marks.get(("ptp_free",))
                for c in range(8):
                    p.emit("tensor", lambda e, cc=c:
                           e.transpose(ptp[:, :], xn1t[:, 128 * cc:128 * (cc + 1)],
                                       ident[:]),
                           waits=[m_xn1, m_tpc2] if c == 0 else [m_tpc2], inc="t")
                    m_t1 = p.mark("t")
                    m_tpc2 = p.emit("vector", lambda e, cc=c:
                                    e.tensor_copy(xn1f[:, 128 * cc:128 * (cc + 1)],
                                                  ptp[:, :]),
                                    waits=[m_t1], inc="v")
                marks[("ptp_free",)] = m_tpc2
                for c in range(8):
                    w0 = [m_tpc2, marks.get(("pk_used",))] if c == 0 else []
                    p.emit("tensor", lambda e, cc=c:
                           e.matmul(pk[:, :], xn1f[:, 128 * cc:128 * (cc + 1)],
                                    wKs[:, 128 * cc:128 * (cc + 1)],
                                    start=(cc == 0), stop=(cc == 7)),
                           waits=w0, inc="t")
                m_k1 = p.mark("t")
                p.emit("vector", lambda e: e.tensor_copy(k1sb[:], pk[:, :]),
                       waits=[m_k1], inc="v")
                m_k1v = p.mark("v")
                for half in range(2):
                    stf = (j == 0 and tc == 0)
                    spf = (j == 7 and tc == 3)
                    p.emit("tensor", lambda e, h3=half, st=stf, sp=spf:
                           e.matmul(M_acc[h3][:, :], k1sb[:],
                                    xn1t[:, 512 * h3:512 * (h3 + 1)],
                                    start=st, stop=sp),
                           waits=[m_k1v] if half == 0 else [], inc="t")
                for c in range(8):
                    w0 = [m_k1v] if c == 0 else []
                    p.emit("tensor", lambda e, cc=c:
                           e.matmul(pk[:, :], wQs[:, 128 * cc:128 * (cc + 1)],
                                    xn1f[:, 128 * cc:128 * (cc + 1)],
                                    start=(cc == 0), stop=(cc == 7)),
                           waits=w0, inc="t")
                m_q1 = p.mark("t")
                marks[("xn1t_used",)] = m_q1
                mkq = p.emit("vector", lambda e, r2=rc:
                             e.tensor_copy(Q1_sb[:, 128 * r2:128 * (r2 + 1)],
                                           pk[:, :]),
                             waits=[m_q1], inc="v")
                marks[("pk_used",)] = mkq
                marks[("q_used",)] = m_q1 if tc == 3 else marks.get(("q_used",))
            marks[("xnf4_used",)] = p.mark("t")
            marks[("xnf4_used", par)] = marks[("xnf4_used",)]
        m_tfin = p.mark("t")
        for half in range(2):
            p.emit("vector", lambda e, h3=half:
                   e.tensor_copy(Msb[:, 512 * h3:512 * (h3 + 1)],
                                 M_acc[h3][:, :]),
                   waits=[m_tfin], inc="v")
        marks["Msb"] = p.mark("v")
        p.flush()


def _p5a(nc, p, en, marks, L):
    x1_d, w1q_d, xq2_d, cw_d, cb_d = (
        L["x1_d"], L["w1q_d"], L["xq2_d"], L["cw_d"], L["cb_d"])
    ident, identf, ones1f, m_bc, recs2_sb = (
        L["ident"], L["identf"], L["ones1f"], L["m_bc"], L["recs2_sb"])
    bn12, mv_t, rstd_t = L["bn12"], L["mv_t"], L["rstd_t"]
    with ExitStack() as px:
        pe = px.enter_context
        w1q = pe(nc.sbuf_tensor("w1q_sb", [128, 8 * F1], BF16))
        cwsb = pe(nc.sbuf_tensor("cwsb", [128, 96], F32))
        cbsb = pe(nc.sbuf_tensor("cbsb", [128, 32], F32))
        x1c5 = [pe(nc.sbuf_tensor(f"x1c5{i}", [128, D], F32)) for i in range(2)]
        xc5 = pe(nc.sbuf_tensor("xc5", [128, D], F32))
        xn5 = pe(nc.sbuf_tensor("xn5", [128, D], F32))
        xqt = pe(nc.sbuf_tensor("xqt", [128, D], BF16))
        rtt = pe(nc.sbuf_tensor("rtt", [128, 1024], F32))
        rtr = pe(nc.sbuf_tensor("rtr", [128, 512], F32))
        rgate = pe(nc.sbuf_tensor("rgate", [128, 1024], I32))
        xqf = pe(nc.sbuf_tensor("xqf", [128, 8 * 512], BF16))
        h5 = pe(nc.sbuf_tensor("h5", [128, 32 * 513], F32))
        hs5 = pe(nc.sbuf_tensor("hs5", [128, 32], F32))
        recs5 = pe(nc.sbuf_tensor("recs5", [128, 4], F32))
        amx_c = pe(nc.sbuf_tensor("amx_c", [128, 1], F32))
        s127 = pe(nc.sbuf_tensor("s127", [128, 1], I32))
        a2t = pe(nc.sbuf_tensor("a2t", [128, 1], F32))
        s2c = pe(nc.sbuf_tensor("s2c", [128, 1], F32))
        recs_row = pe(nc.sbuf_tensor("recs_row", [1, 512], F32))
        s2row = pe(nc.sbuf_tensor("s2row", [1, 512], F32))
        rbc = pe(nc.sbuf_tensor("rbc", [128, 512], I32))
        cv1 = pe(nc.sbuf_tensor("cv1", [128, 1024], F32))
        cv2 = pe(nc.sbuf_tensor("cv2", [128, 512], F32))
        mx5 = pe(nc.sbuf_tensor("mx5", [128, 512], F32))
        mn5 = pe(nc.sbuf_tensor("mn5", [128, 512], F32))
        xq2t = [pe(nc.sbuf_tensor(f"xq2t{i}", [128, 1024], BF16)) for i in range(2)]
        ph5 = [pe(nc.psum_tensor(f"ph5{i}", [128, 512], F32)) for i in range(2)]
        ptp5 = pe(nc.psum_tensor("ptp5", [128, 128], BF16))
        ptp5f = pe(nc.psum_tensor("ptp5f", [128, 128], F32))
        pbc5 = pe(nc.psum_tensor("pbc5", [128, 512], F32))

        for c in range(8):
            p.emit("sync", lambda e, cc=c:
                   e.dma_start(w1q[:, F1 * cc:F1 * (cc + 1)], w1q_d[cc]),
                   inc="dc0", k=16)
        m_w1q = p.mark("dc0")
        p.emit("sync", lambda e: e.dma_start(cwsb[:], cw_d), inc="d", k=16)
        p.emit("sync", lambda e: e.dma_start(cbsb[:], cb_d), inc="d", k=16)
        m_cw = p.mark("d")
        p.emit("vector", lambda e: e.memset(recs_row[:], 0.0), inc="v")
        p.emit("vector", lambda e: e.memset(s2row[:], 0.0), inc="v")
        p.emit("vector", lambda e: e.memset(xq2t[0][:], 0.0), inc="v")
        p.emit("vector", lambda e: e.memset(cv1[:], 0.0), inc="v")
        p.emit("vector", lambda e: e.memset(xq2t[1][:], 0.0), inc="v")
        marks["xq2st"] = [None] * NWIN

        for w in range(NWIN):
            o0 = WS * w
            ow = min(WS * (w + 1), N) - o0
            hn = min(512, N - o0)
            nck = (hn + 127) // 128
            p.emit("vector", lambda e: e.memset(mx5[:], -3.0e38),
                   waits=[marks.get(("mx_used",))], inc="v")
            p.emit("vector", lambda e: e.memset(mn5[:], 3.0e38),
                   waits=[marks.get(("mn_used",))], inc="v")
            m_mninit = p.mark("v")
            m_xqf = None
            for k2 in range(nck):
                rows = min(128, hn - 128 * k2)
                par = k2 % 2
                mk_ld = p.emit(
                    "sync", lambda e, b=x1c5[par], r0=o0 + 128 * k2, rr=rows:
                    e.dma_start(b[0:rr, :], x1_d[r0:r0 + rr, :]),
                    waits=[marks.get(("x1c5_used", par))], inc=f"da{par}", k=16)
                m_r, m_xc = ln_stats(p, x1c5[par][0:rows, :], bn12, mv_t,
                                     rstd_t, xc5[0:rows, :], rows=rows,
                                     pre=[mk_ld] + p.after("s", "v"),
                                     xc_pre=p.after("s"))
                p.emit("vector", lambda e, rr=rows:
                       e.reduce_max(amx_c[0:rr, :], xc5[0:rr, :], AX.X,
                                    apply_absolute_value=True),
                       waits=[m_xc] + p.after("v"), inc="v", ks=True)
                p.emit("vector", lambda e, rr=rows:
                       e.tensor_tensor(amx_c[0:rr, :], amx_c[0:rr, :],
                                       rstd_t[0:rr, :], ALU.mult),
                       waits=[m_r] + p.after("v"), inc="v", ks=True)
                p.emit("vector", lambda e, rr=rows:
                       e.tensor_single_scalar(amx_c[0:rr, :], amx_c[0:rr, :],
                                              1e-5, ALU.max),
                       waits=p.after("v"), inc="v", ks=True)
                p.emit("vector", lambda e, rr=rows:
                       e.reciprocal(s127[0:rr, :].bitcast(F32),
                                    amx_c[0:rr, :]),
                       waits=p.after("v"), inc="v", ks=True)
                p.emit("vector", lambda e, rr=rows:
                       e.tensor_single_scalar(s127[0:rr, :].bitcast(F32),
                                              s127[0:rr, :].bitcast(F32),
                                              127.0, ALU.mult),
                       waits=p.after("v"), inc="v", ks=True)
                p.emit("vector", lambda e, rr=rows, kk=k2:
                       e.tensor_single_scalar(recs5[0:rr, kk:kk + 1],
                                              amx_c[0:rr, :], INV127, ALU.mult),
                       waits=p.after("v"), inc="v", ks=True)
                m_recs = p.mark("v")
                marks[("x1c5_used", par)] = m_recs
                p.emit("scalar", lambda e, rr=rows:
                       e.activation(xn5[0:rr, :], xc5[0:rr, :], AF.Identity,
                                    scale=rstd_t[0:rr, 0:1]),
                       waits=[m_r, m_xc, marks.get(("xn5_used",))] + p.after("s"),
                       inc="s")
                m_xn5 = p.mark("s")
                p.emit("vector", lambda e, rr=rows:
                       e.tensor_scalar(cv1[0:rr, 0:D],
                                       xn5[0:rr, 0:D],
                                       s127[0:rr, 0:1].bitcast(F32),
                                       None, ALU.mult),
                       waits=[m_xn5, marks.get(("xqt_used",)),
                              marks.get(("cv1_used",))] + p.after("v"),
                       inc="v", ks=True)
                mk_rnd = round_wide(
                    p, xqt[0:rows, 0:D], cv1[0:rows, 0:D],
                    rtt[0:rows, 0:D], xn5[0:rows, 0:D],
                    rgate[0:rows, 0:D].bitcast(F32), K=1,
                    pre=p.after("v"), use_s="neg")
                marks[("cv1_used",)] = mk_rnd
                marks[("xn5_used",)] = mk_rnd
                for c in range(8):
                    p.emit("tensor", lambda e, cc=c, rr=rows:
                           e.transpose(ptp5[:, 0:rr],
                                       xqt[0:rr, 128 * cc:128 * (cc + 1)],
                                       ident[0:rr, 0:rr]),
                           waits=[mk_rnd, marks.get(("ptp5_free",)),
                                  marks.get(("xqf_used",))], inc="t")
                    m_t1 = p.mark("t")
                    mk_cp = p.emit(
                        "vector", lambda e, cc=c, rr=rows, kk=k2:
                        e.tensor_copy(xqf[:, 512 * cc + 128 * kk:512 * cc + 128 * kk + rr],
                                      ptp5[:, 0:rr]),
                        waits=[m_t1], inc="v")
                    marks[("ptp5_free",)] = mk_cp
                marks[("xqt_used",)] = p.mark("t")
                m_xqf = marks[("ptp5_free",)]
                p.emit("tensor", lambda e, rr=rows, kk=k2:
                       e.transpose(ptp5f[0:1, 0:rr], recs5[0:rr, kk:kk + 1],
                                   identf[0:rr, 0:rr]),
                       waits=[m_recs, marks.get(("ptp5f_free",))], inc="t")
                m_t2 = p.mark("t")
                marks[("ptp5f_free",)] = p.emit(
                    "vector", lambda e, rr=rows, kk=k2:
                    e.tensor_copy(recs_row[0:1, 128 * kk:128 * kk + rr],
                                  ptp5f[0:1, 0:rr]),
                    waits=[m_t2], inc="v")
            p.emit("tensor", lambda e:
                   e.matmul(pbc5[:, :], ones1f[:, :], recs_row[0:1, :],
                            start=True, stop=True),
                   waits=[marks[("ptp5f_free",)], marks.get(("pbc5_free",))] +
                         p.after("v"), inc="t")
            m_t3 = p.mark("t")
            m_rbc = p.emit("vector", lambda e:
                           e.tensor_copy(rbc[:].bitcast(F32), pbc5[:, :]),
                           waits=[m_t3] + p.after("p"), inc="v")
            marks[("pbc5_free",)] = m_rbc
            mhv5 = {}
            mcv1_5 = {}
            msilu5 = {}

            def conv_A(m):
                mp = m % 2
                for c in range(8):
                    w0 = []
                    if c == 0:
                        w0 = [m_xqf, m_w1q, marks.get(("ph_used", mp))]
                    p.emit("tensor", lambda e, cc=c, mm2=m, hn2=hn, bk=ph5[mp]:
                           e.matmul(bk[:, 0:hn2],
                                    w1q[:, F1 * cc + 128 * mm2:F1 * cc + 128 * (mm2 + 1)],
                                    xqf[:, 512 * cc:512 * cc + hn2],
                                    start=(cc == 0), stop=(cc == 7)),
                           waits=w0, inc="t")
                m_mm = p.mark("t")
                p.emit("vector", lambda e, mm2=m, hn2=hn, bk=ph5[mp]:
                       e.scalar_tensor_tensor(
                           h5[:, 513 * mm2 + 1:513 * mm2 + 1 + hn2],
                           bk[:, 0:hn2], m_bc[:, 0:1],
                           rbc[:, 0:hn2].bitcast(F32),
                           ALU.mult, ALU.mult),
                       waits=[m_mm, m_rbc, marks.get(("h5q", m))] + p.after("v"),
                       inc="v")
                marks[("ph_used", mp)] = p.mark("v")
                if w == 0:
                    p.emit("vector", lambda e, mm2=m:
                           e.memset(h5[:, 513 * mm2:513 * mm2 + 1], 0.0),
                           waits=p.after("v"), inc="v")
                else:
                    p.emit("vector", lambda e, mm2=m:
                           e.tensor_copy(h5[:, 513 * mm2:513 * mm2 + 1],
                                         hs5[:, mm2:mm2 + 1]),
                           waits=p.after("v"), inc="v", ks=True)
                if w == NWIN - 1:
                    p.emit("vector", lambda e, mm2=m:
                           e.memset(h5[:, 513 * mm2 + 17:513 * mm2 + 18], 0.0),
                           waits=p.after("v"), inc="v")
                else:
                    p.emit("vector", lambda e, mm2=m:
                           e.tensor_copy(hs5[:, mm2:mm2 + 1],
                                         h5[:, 513 * mm2 + 510:513 * mm2 + 511]),
                           waits=p.after("v"), inc="v")
                mhv5[m] = p.mark("v")
                p.emit("scalar", lambda e, mm2=m, ow2=ow:
                       e.activation(cv1[:, 0:ow2],
                                    h5[:, 513 * mm2 + 1:513 * mm2 + 1 + ow2],
                                    AF.Identity,
                                    scale=cwsb[:, 3 * mm2 + 1:3 * mm2 + 2],
                                    bias=cbsb[:, mm2:mm2 + 1]),
                       waits=[mhv5[m], m_cw, marks.get(("cv1_used",))] +
                             p.after("s"), inc="s")
                mcv1_5[m] = p.mark("s")

            def conv_B(m):
                p.emit("vector", lambda e, mm2=m, ow2=ow:
                       e.scalar_tensor_tensor(cv2[:, 0:ow2],
                                              h5[:, 513 * mm2:513 * mm2 + ow2],
                                              cwsb[:, 3 * mm2:3 * mm2 + 1],
                                              cv1[:, 0:ow2], ALU.mult, ALU.add),
                       waits=[mcv1_5[m], mhv5[m], marks.get(("cv2_used",))] +
                             p.after("v"), inc="v")
                m_cv2 = p.mark("v")
                marks[("cv1_used",)] = m_cv2
                p.emit("vector", lambda e, mm2=m, ow2=ow:
                       e.scalar_tensor_tensor(h5[:, 513 * mm2:513 * mm2 + ow2],
                                              h5[:, 513 * mm2 + 2:513 * mm2 + 2 + ow2],
                                              cwsb[:, 3 * mm2 + 2:3 * mm2 + 3],
                                              cv2[:, 0:ow2], ALU.mult, ALU.add),
                       waits=[m_cv2] + p.after("v"), inc="v")
                m_cnv = p.mark("v")
                marks[("cv2_used",)] = m_cnv
                p.emit("scalar", lambda e, mm2=m, ow2=ow:
                       e.activation(h5[:, 513 * mm2:513 * mm2 + ow2],
                                    h5[:, 513 * mm2:513 * mm2 + ow2], AF.Silu),
                       waits=[m_cnv] + p.after("s"), inc="s")
                msilu5[m] = p.mark("s")

            def conv_C(m):
                p.emit("vector", lambda e, mm2=m, ow2=ow:
                       e.tensor_tensor(mx5[:, 0:ow2], mx5[:, 0:ow2],
                                       h5[:, 513 * mm2:513 * mm2 + ow2],
                                       ALU.max),
                       waits=[msilu5[m]] + p.after("v"), inc="v")
                p.emit("vector", lambda e, mm2=m, ow2=ow:
                       e.tensor_tensor(mn5[:, 0:ow2], mn5[:, 0:ow2],
                                       h5[:, 513 * mm2:513 * mm2 + ow2],
                                       ALU.min),
                       waits=[msilu5[m]] + p.after("v"), inc="v")

            for m in range(34):
                if 1 <= m < 33:
                    conv_B(m - 1)
                if m < 32:
                    conv_A(m)
                if m >= 2:
                    conv_C(m - 2)
            marks[("xqf_used",)] = p.mark("t")
            m_mins = p.mark("p")
            p.emit("vector", lambda e:
                   e.tensor_single_scalar(mn5[:], mn5[:], -1.0, ALU.mult),
                   waits=[m_mins] + p.after("v"), inc="v")
            p.emit("vector", lambda e:
                   e.tensor_tensor(mx5[:], mx5[:], mn5[:], ALU.max),
                   waits=p.after("v"), inc="v")
            m_amax = p.mark("v")
            marks[("mn_used",)] = m_amax
            for k2 in range((ow + 127) // 128):
                koff = 128 * k2
                rk = min(128, ow - koff)
                p.emit("tensor", lambda e, ko=koff, rr=rk:
                       e.transpose(ptp5f[0:rr, :], mx5[:, ko:ko + rr], identf[:]),
                       waits=[m_amax, marks.get(("ptp5f_free",))], inc="t")
                m_t4 = p.mark("t")
                p.emit("vector", lambda e, rr=rk:
                       e.reduce_max(a2t[0:rr, :], ptp5f[0:rr, :], AX.X),
                       waits=[m_t4] + p.after("v"), inc="v", ks=True)
                marks[("ptp5f_free",)] = p.mark("v")
                p.emit("vector", lambda e, rr=rk:
                       e.tensor_single_scalar(a2t[0:rr, :], a2t[0:rr, :],
                                              1e-5, ALU.max),
                       waits=p.after("v"), inc="v", ks=True)
                p.emit("vector", lambda e, rr=rk:
                       e.reciprocal(s2c[0:rr, :], a2t[0:rr, :]),
                       waits=p.after("v"), inc="v", ks=True)
                p.emit("vector", lambda e, rr=rk:
                       e.tensor_single_scalar(s2c[0:rr, :], s2c[0:rr, :],
                                              127.0, ALU.mult),
                       waits=p.after("v"), inc="v", ks=True)
                p.emit("vector", lambda e, rr=rk, w2=w, kk=k2:
                       e.tensor_single_scalar(recs2_sb[0:rr, 4 * w2 + kk:4 * w2 + kk + 1],
                                              a2t[0:rr, :], INV127, ALU.mult),
                       waits=p.after("v"), inc="v", ks=True)
                m_s2 = p.mark("v")
                p.emit("tensor", lambda e, rr=rk:
                       e.transpose(ptp5f[0:1, 0:rr], s2c[0:rr, 0:1],
                                   identf[0:rr, 0:rr]),
                       waits=[m_s2, marks.get(("ptp5f_free",))], inc="t")
                m_t5 = p.mark("t")
                marks[("ptp5f_free",)] = p.emit(
                    "vector", lambda e, ko=koff, rr=rk:
                    e.tensor_copy(s2row[0:1, ko:ko + rr], ptp5f[0:1, 0:rr]),
                    waits=[m_t5], inc="v")
            p.emit("tensor", lambda e:
                   e.matmul(pbc5[:, :], ones1f[:, :], s2row[0:1, :],
                            start=True, stop=True),
                   waits=[marks[("ptp5f_free",)], marks.get(("pbc5_free",))] +
                         p.after("v"), inc="t")
            m_t6 = p.mark("t")
            m_s2bc = p.emit("vector", lambda e:
                            e.tensor_copy(rbc[:].bitcast(F32), pbc5[:, :]),
                            waits=[m_t6] + p.after("p"), inc="v")
            marks[("pbc5_free",)] = m_s2bc
            marks[("mx_used",)] = m_t6
            for mp in range(16):
                par = mp % 2
                m0, m1 = 2 * mp, 2 * mp + 1
                for sub, mm in ((0, m0), (1, m1)):
                    p.emit("vector", lambda e, mm2=mm, ow2=ow, s4=sub:
                           e.tensor_tensor(cv1[:, 512 * s4:512 * s4 + ow2],
                                           h5[:, 513 * mm2:513 * mm2 + ow2],
                                           rbc[:, 0:ow2].bitcast(F32),
                                           ALU.mult),
                           waits=[m_s2bc, marks.get(("xq2st", par))] +
                                 p.after("v"), inc="v")
                    marks[("h5q", mm)] = p.mark("v")
                wid = 512 + ow
                mk_r4 = round_wide(p, xq2t[par][:, 0:wid], cv1[:, 0:wid],
                                   rtt[:, 0:wid], xn5[:, 0:wid],
                                   rgate[:, 0:wid].bitcast(F32), K=2,
                                   pre=p.after("v"), eng="vector", use_s=True)
                marks[("xq2st", par)] = p.emit(
                    "sync", lambda e, mm2=m0, w2=w, b=xq2t[par]:
                    e.dma_start(xq2_d[w2, :, 512 * mm2:512 * (mm2 + 2)], b[:]),
                    waits=[mk_r4], inc=f"db{par}", k=16)
                marks[("xn5_used",)] = mk_r4
            marks["xq2st"][w] = (marks[("xq2st", 0)], marks[("xq2st", 1)])
        p.flush()


def _p5b(nc, p, en, marks, L):
    xq2_d, w2q_d, x1_d, out_d = L["xq2_d"], L["w2q_d"], L["x1_d"], L["out_d"]
    Q1_sb, Msb, gam_bc, bet_bc, m_bc, recs2_sb = (
        L["Q1_sb"], L["Msb"], L["gam_bc"], L["bet_bc"], L["m_bc"],
        L["recs2_sb"])
    bn12, mv_t, rstd_t = L["bn12"], L["mv_t"], L["rstd_t"]
    with ExitStack() as px:
        pe = px.enter_context
        w2q = pe(nc.sbuf_tensor("w2q_sb", [128, 32 * 1024], BF16))
        xq2s = [pe(nc.sbuf_tensor(f"xq2s{i}", [128, 32 * 512], BF16)) for i in range(2)]
        x1c6 = [pe(nc.sbuf_tensor(f"x1c6{i}", [128, D], F32)) for i in range(2)]
        g1t = [pe(nc.sbuf_tensor(f"g1t{i}", [128, D], F32)) for i in range(2)]
        xc6 = pe(nc.sbuf_tensor("xc6", [128, D], F32))
        xn6 = pe(nc.sbuf_tensor("xn6", [128, D], F32))
        Jn6 = pe(nc.sbuf_tensor("Jn6", [128, D], F32))
        tn6 = pe(nc.sbuf_tensor("tn6", [128, D], F32))
        s1t = pe(nc.sbuf_tensor("s1t", [128, D], F32))
        ot6 = [pe(nc.sbuf_tensor(f"ot6{i}", [128, D], F32)) for i in range(2)]
        sc2 = [pe(nc.sbuf_tensor(f"sc2{i}", [128, 1], F32)) for i in range(2)]
        po6 = [pe(nc.psum_tensor(f"po6{i}", [128, 512], F32)) for i in range(2)]
        pj6 = [pe(nc.psum_tensor(f"pj6{i}", [128, 512], F32)) for i in range(2)]

        for c in range(32):
            p.emit("sync", lambda e, cc=c:
                   e.dma_start(w2q[:, 1024 * cc:1024 * (cc + 1)], w2q_d[cc]),
                   inc="dc0", k=16)
        m_w2q = p.mark("dc0")

        for w in range(NWIN):
            o0 = WS * w
            ow = min(WS * (w + 1), N) - o0
            wpar = w % 2
            mk_x = p.emit("sync", lambda e, w2=w, b=xq2s[wpar]:
                          e.dma_start(b[:], xq2_d[w2]),
                          waits=[marks.get(("xq2s_used", wpar))],
                          inc=f"da{wpar}", k=16)
            for k2 in range((ow + 127) // 128):
                t0 = o0 + 128 * k2
                rk = min(128, ow - 128 * k2)
                par = k2 % 2
                mk_l = p.emit("sync", lambda e, b=x1c6[par], t1=t0, rr=rk:
                              e.dma_start(b[0:rr, :], x1_d[t1:t1 + rr, :]),
                              waits=[marks.get(("x1c6_used", par))],
                              inc=f"dd{par}", k=16)
                p.emit("vector", lambda e, rr=rk, w2=w, kk=k2, s2=sc2[par]:
                       e.tensor_tensor(s2[0:rr, :],
                                       recs2_sb[0:rr, 4 * w2 + kk:4 * w2 + kk + 1],
                                       m_bc[0:rr, 2:3], ALU.mult),
                       waits=[marks.get(("sc2_used", par))] + p.after("v"),
                       inc="v", ks=True)
                m_sc2 = p.mark("v")
                for half in range(2):
                    for c in range(32):
                        w0 = []
                        if c == 0:
                            w0 = [mk_x, m_w2q, marks.get(("po6_used", half))]
                        p.emit("tensor",
                               lambda e, cc=c, kk=k2, h3=half, rr=rk,
                               bk=po6[half], b=xq2s[wpar]:
                               e.matmul(bk[0:rr, :],
                                        b[:, 512 * cc + 128 * kk:512 * cc + 128 * kk + rr],
                                        w2q[:, 1024 * cc + 512 * h3:1024 * cc + 512 * (h3 + 1)],
                                        start=(cc == 0), stop=(cc == 31)),
                               waits=w0, inc="t")
                    m_o = p.mark("t")
                    p.emit("scalar", lambda e, h3=half, rr=rk, bk=po6[half],
                           g1=g1t[par], s2=sc2[par]:
                           e.activation(g1[0:rr, 512 * h3:512 * (h3 + 1)],
                                        bk[0:rr, :], AF.Silu,
                                        scale=s2[0:rr, 0:1]),
                           waits=[m_o, m_sc2, marks.get(("g1_used", par))] +
                                 p.after("s"), inc="s")
                    marks[("po6_used", half)] = p.mark("s")
                m_g1 = p.mark("s")
                marks[("sc2_used", par)] = m_g1
                marks[("xq2s_used", wpar)] = p.mark("t")
                m_r, m_xc = ln_stats(p, x1c6[par][0:rk, :], bn12, mv_t, rstd_t,
                                     xc6[0:rk, :], rows=rk,
                                     pre=[mk_l] + p.after("s", "v"),
                                     xc_pre=p.after("s"))
                p.emit("scalar", lambda e, rr=rk:
                       e.activation(xn6[0:rr, :], xc6[0:rr, :], AF.Identity,
                                    scale=rstd_t[0:rr, 0:1]),
                       waits=[m_r, m_xc, marks.get(("xn6_used",))] + p.after("s"),
                       inc="s")
                m_xn6 = p.mark("s")
                for half in range(2):
                    p.emit("tensor", lambda e, h3=half, t1=t0, rr=rk, bk=pj6[half]:
                           e.matmul(bk[0:rr, :], Q1_sb[:, t1:t1 + rr],
                                    Msb[:, 512 * h3:512 * (h3 + 1)],
                                    start=True, stop=True),
                           waits=[marks.get(("pj6_used", half))], inc="t")
                    m_j = p.mark("t")
                    p.emit("vector", lambda e, h3=half, rr=rk, bk=pj6[half]:
                           e.scalar_tensor_tensor(
                               Jn6[0:rr, 512 * h3:512 * (h3 + 1)],
                               bk[0:rr, :], ISQ_DK,
                               xn6[0:rr, 512 * h3:512 * (h3 + 1)],
                               ALU.mult, ALU.add),
                           waits=[m_j, m_xn6] + p.after("v"), inc="v")
                    marks[("pj6_used", half)] = p.mark("v")
                marks[("xn6_used",)] = p.mark("v")
                m_jn = p.mark("v")
                m_r2, m_xc2 = ln_stats(p, Jn6[0:rk, :], bn12, mv_t, rstd_t,
                                       xc6[0:rk, :], rows=rk,
                                       pre=[m_jn] + p.after("s"),
                                       xc_pre=p.after("s"))
                p.emit("scalar", lambda e, rr=rk:
                       e.activation(tn6[0:rr, :], xc6[0:rr, :], AF.Identity,
                                    scale=rstd_t[0:rr, 0:1]),
                       waits=[m_r2, m_xc2, marks.get(("tn6_used",))] +
                             p.after("s"), inc="s")
                m_tn = p.mark("s")
                p.emit("vector", lambda e, rr=rk:
                       e.tensor_tensor(s1t[0:rr, :], tn6[0:rr, :],
                                       gam_bc[0:rr, :], ALU.mult),
                       waits=[m_tn, marks["gb_bc"]] + p.after("v"), inc="v")
                marks[("tn6_used",)] = p.mark("v")
                p.emit("vector", lambda e, rr=rk:
                       e.tensor_tensor(s1t[0:rr, :], s1t[0:rr, :],
                                       bet_bc[0:rr, :], ALU.add),
                       waits=p.after("v"), inc="v")
                p.emit("vector", lambda e, rr=rk, g1=g1t[par]:
                       e.tensor_tensor(s1t[0:rr, :], s1t[0:rr, :],
                                       g1[0:rr, :], ALU.mult),
                       waits=[m_g1] + p.after("v"), inc="v")
                marks[("g1_used", par)] = p.mark("v")
                p.emit("vector", lambda e, rr=rk, b=x1c6[par], o=ot6[par]:
                       e.tensor_tensor(o[0:rr, :], s1t[0:rr, :], b[0:rr, :],
                                       ALU.add),
                       waits=[marks.get(("ot_used", par))] + p.after("v"),
                       inc="v")
                m_out = p.mark("v")
                marks[("x1c6_used", par)] = m_out
                mk_st = p.emit("sync", lambda e, t1=t0, rr=rk, o=ot6[par]:
                               e.dma_start(out_d[t1:t1 + rr, :], o[0:rr, :]),
                               waits=[m_out], inc=f"de{par}", k=16)
                marks[("ot_used", par)] = mk_st
        p.flush()


def _prep_inputs(inputs):
    f32 = np.float32
    W_qkv = np.asarray(inputs["L0_attn_W_qkv"], f32)
    base = {
        "wqT": np.ascontiguousarray(W_qkv[0:D, :].T),
        "wkT": np.ascontiguousarray(W_qkv[D:2 * D, :].T),
        "wvT": np.ascontiguousarray(W_qkv[2 * D:3 * D, :].T),
        "wpT": np.ascontiguousarray(np.asarray(inputs["L0_attn_W_proj"], f32).T),
        "bproj": np.ascontiguousarray(
            np.asarray(inputs["L0_attn_b_proj"], f32).reshape(1, D)),
        "wgT": np.ascontiguousarray(np.asarray(inputs["L0_ffn_w_gate"]).T),
        "wuT": np.ascontiguousarray(np.asarray(inputs["L0_ffn_w_up"]).T),
        "wdT": np.ascontiguousarray(np.asarray(inputs["L0_ffn_w_down"]).T),
        "wQT": np.ascontiguousarray(np.asarray(inputs["L1_attn_W_Q"], f32).T),
        "wKT": np.ascontiguousarray(np.asarray(inputs["L1_attn_W_K"], f32).T),
        "gam": np.ascontiguousarray(
            np.asarray(inputs["L1_attn_gamma"], f32).reshape(1, D)),
        "bet": np.ascontiguousarray(
            np.asarray(inputs["L1_attn_beta"], f32).reshape(1, D)),
        "w1T": np.ascontiguousarray(np.asarray(inputs["L1_ffn_w1"], f32).T),
        "cw": np.ascontiguousarray(
            np.asarray(inputs["L1_ffn_conv_w"], f32).reshape(32, 128, 3)
            .transpose(1, 0, 2).reshape(128, 96)),
        "cb": np.ascontiguousarray(
            np.asarray(inputs["L1_ffn_conv_b"], f32).reshape(32, 128).T),
        "w2T": np.ascontiguousarray(np.asarray(inputs["L1_ffn_w2"], f32).T),
    }
    x_all = np.asarray(inputs["L0_attn_x"], f32)
    return base, x_all


def kernel(**inputs):
    base, x_all = _prep_inputs(inputs)
    nc = build_program()
    in_maps = []
    for b in range(8):
        m = dict(base)
        m["x"] = np.ascontiguousarray(x_all[b])
        in_maps.append(m)
    res = run_bass_kernel_spmd(nc, in_maps, core_ids=list(range(8)))
    out = np.stack([np.asarray(r["out"], dtype=np.float32)
                    for r in res.results], axis=0)
    return out
